# version 07
# baseline (speedup 1.0000x reference)
import numpy as np
import ml_dtypes
from contextlib import ExitStack
from collections import defaultdict

import concourse.bass as bass
import concourse.mybir as mybir

F32 = mybir.dt.float32
BF16 = mybir.dt.bfloat16
AF = mybir.ActivationFunctionType
ALU = mybir.AluOpType
AX = mybir.AxisListType
BF = ml_dtypes.bfloat16

D = 1024
H = 16
HD = 64
NF = 8
EXP = 8
MH = 1408
NHT = 11
DH = 4096
NHU = 32
LN_EPS = 1e-5
ATT_EPS = 1e-6


class Prog:

    ENGS = ("sync", "tensor", "vector", "scalar", "gpsimd")

    def __init__(self, semc=None):
        self.ops = []
        self.semc = semc if semc is not None else defaultdict(int)

    def n(self, sem):
        return self.semc[sem]

    DMA_ENG = {"qs": "sync", "qp": "gpsimd", "qa": "scalar", "qv": "vector"}

    def dma(self, stream, fn, war=()):
        self.op(self.DMA_ENG[stream], fn, wait=[stream] + list(war), inc=[stream],
                is_dma=True)
        return self.semc[stream]

    AUTO = {"vector": "dv", "scalar": "ac"}

    def op(self, eng, fn, wait=(), inc=(), is_dma=False):
        ws = []
        for w in wait:
            if isinstance(w, str):
                ws.append((w, self.semc[w]))
            else:
                ws.append((w[0], w[1]))
        incs = []
        for i in inc:
            if isinstance(i, str):
                amt = 16 if i.startswith("q") else 1
                incs.append((i, amt))
            else:
                incs.append((i[0], i[1]))
        own = self.AUTO.get(eng) if not is_dma else None
        if own is not None:
            if not any(s == own for s, _ in ws):
                ws.append((own, self.semc[own]))
            if not any(s == own for s, _ in incs):
                incs.append((own, 1))
        for s, amt in incs:
            self.semc[s] += amt
        self.ops.append((eng, fn, ws, incs))

    def emit_into_block(self, blk, engines, sems):
        for eng_name in self.ENGS:
            my_ops = [o for o in self.ops if o[0] == eng_name]
            if not my_ops:
                continue
            def body(e, my_ops=my_ops, eng_name=eng_name):
                known = defaultdict(int)
                for _, fn, ws, incs in my_ops:
                    for s, v in ws:
                        if known[s] < v:
                            e.wait_ge(sems[s], v)
                            known[s] = v
                    inst = fn(e)
                    for s, amt in incs:
                        assert inst is not None, f"inc {s} on None inst"
                        inst.then_inc(sems[s], amt)

            getattr(blk, eng_name)(body)


def _dense_stage_mats(w, st, transpose):
    s = 1 << st
    out = np.zeros((32, 128, 128), dtype=np.float32)
    ws = w[st].reshape(DH // (2 * s), s, 2, 2)
    for blk in range(32):
        base = blk * 128
        for iloc in range(128):
            i = base + iloc
            if i & s:
                continue
            n = i >> (st + 1)
            j = i & (s - 1)
            w2 = ws[n, j]
            if transpose:
                w2 = w2.T
            out[blk, iloc, iloc] = w2[0, 0]
            out[blk, iloc, iloc + s] = w2[0, 1]
            out[blk, iloc + s, iloc] = w2[1, 0]
            out[blk, iloc + s, iloc + s] = w2[1, 1]
    return out


def stage_up_blocks(up_w):
    arr = np.empty((32, 7, 128, 128), dtype=np.float32)
    for k in range(6):
        arr[:, k] = _dense_stage_mats(up_w, k, transpose=False)
    arr[:, 6] = _dense_stage_mats(up_w, 6, transpose=True)
    return arr


def stage_dn_blocks(dn_w):
    arr = np.empty((32, 7, 128, 128), dtype=np.float32)
    arr[:, 0] = _dense_stage_mats(dn_w, 0, transpose=False)
    for k in range(1, 7):
        arr[:, k] = _dense_stage_mats(dn_w, k, transpose=True)
    return arr


def stage_tile_coefs(w, transpose):
    out = np.empty((128, 5, 16, 4), dtype=np.float32)
    for q in range(5):
        st = 7 + q
        s = 1 << st
        ws = w[st].reshape(DH // (2 * s), s, 2, 2)
        bit = 1 << (st - 7)
        atiles = [t for t in range(32) if not (t & bit)]
        assert len(atiles) == 16
        for pi, T in enumerate(atiles):
            for p in range(128):
                i = T * 128 + p
                n = i >> (st + 1)
                j = i & (s - 1)
                w2 = ws[n, j]
                if transpose:
                    w2 = w2.T
                out[p, q, pi, 0] = w2[0, 0]
                out[p, q, pi, 1] = w2[0, 1]
                out[p, q, pi, 2] = w2[1, 0]
                out[p, q, pi, 3] = w2[1, 1]
    return out.reshape(128, 5 * 16 * 4)


def stage_dn_tile_coefs(dn_w):
    base = stage_tile_coefs(dn_w, transpose=True).reshape(128, 5, 16, 4)
    rev = base[:, ::-1]
    return np.ascontiguousarray(rev.reshape(128, 5 * 16 * 4))


def stage_host(inputs, n_cores=8):
    f32 = np.float32
    x = np.ascontiguousarray(inputs["L0_attn_x"], dtype=f32)
    shared = {
        "wqT": np.ascontiguousarray(inputs["L0_attn_Wq"].T, f32),
        "wkT": np.ascontiguousarray(inputs["L0_attn_Wk"].T, f32),
        "wvT": np.ascontiguousarray(inputs["L0_attn_Wv"].T, f32),
        "woT": np.ascontiguousarray(inputs["L0_attn_Wo"].T, f32),
        "bq_f": np.ascontiguousarray(np.asarray(inputs["L0_attn_bq"], f32).reshape(NF, 128).T),
        "bk_f": np.ascontiguousarray(np.asarray(inputs["L0_attn_bk"], f32).reshape(NF, 128).T),
        "bv_r": np.asarray(inputs["L0_attn_bv"], f32).reshape(1, D),
        "bo_r": np.asarray(inputs["L0_attn_bo"], f32).reshape(1, D),
        "wrT": np.ascontiguousarray(np.asarray(inputs["L0_ffn_w_router"], f32).T),
        "w1": np.ascontiguousarray(inputs["L0_ffn_w1"], f32),
        "w3": np.ascontiguousarray(inputs["L0_ffn_w3"], f32),
        "w2": np.ascontiguousarray(inputs["L0_ffn_w2"], f32),
        "wQ": np.ascontiguousarray(inputs["L1_attn_W_Q"], f32),
        "wK": np.ascontiguousarray(inputs["L1_attn_W_K"], f32),
        "wV": np.ascontiguousarray(inputs["L1_attn_W_V"], f32),
        "wO": np.ascontiguousarray(inputs["L1_attn_W_o"], f32),
        "wexpT": np.ascontiguousarray(np.asarray(inputs["L1_ffn_W_exp"], f32).T),
        "bexp_f": np.ascontiguousarray(np.asarray(inputs["L1_ffn_b_exp"], f32).reshape(NHU, 128).T),
        "upbias_f": np.ascontiguousarray(np.asarray(inputs["L1_ffn_up_bias"], f32).reshape(NHU, 128).T),
        "upA": stage_up_blocks(np.asarray(inputs["L1_ffn_up_weights"], f32)),
        "upC": stage_tile_coefs(np.asarray(inputs["L1_ffn_up_weights"], f32), transpose=False),
        "dnC": stage_dn_blocks(np.asarray(inputs["L1_ffn_down_weights"], f32)),
        "dnCoef": stage_dn_tile_coefs(np.asarray(inputs["L1_ffn_down_weights"], f32)),
        "wcon": np.ascontiguousarray(inputs["L1_ffn_W_con"], f32),
        "bcon_r": np.asarray(inputs["L1_ffn_b_con"], f32).reshape(1, D),
        "dbias_r": np.asarray(inputs["L1_ffn_down_bias"], f32).reshape(1, D),
        "ident": np.eye(128, dtype=BF),
        "ident32": np.eye(128, dtype=f32),
        "tri16": np.ascontiguousarray(np.tile(np.tril(np.ones((128, 128), f32)).T, (1, 16)).astype(BF)),
        "ones_c": np.ones((128, 1), dtype=BF),
        "ones_r": np.ones((1, 128), dtype=f32),
    }
    in_maps = []
    bpc = 16 // n_cores
    for c in range(n_cores):
        m = dict(shared)
        m["x"] = np.ascontiguousarray(x[c * bpc:(c + 1) * bpc])
        in_maps.append(m)
    return in_maps


class Builder:
    SEMS = ("qs", "qp", "qa", "pe", "dv", "ac")

    def __init__(self, S=2048, n_seq=2, phases=("W", "AB", "C", "D", "E"), expose=False):
        self.S = S
        self.NCH = S // 128
        self.n_seq = n_seq
        self.phases = phases
        self.expose = expose
        nc = self.nc = bass.Bass("TRN2", target_bir_lowering=False)
        self.stack = ExitStack()
        self._declare_dram()
        self.sems = {n: self.stack.enter_context(nc.semaphore(n)) for n in self.SEMS}
        self.semc = defaultdict(int)
        self.engines = {}

    def _declare_dram(self):
        nc, S = self.nc, self.S
        t = lambda n, sh, dt, kind: nc.dram_tensor(n, sh, dt, kind=kind).ap()
        I, O, N = "ExternalInput", "ExternalOutput", "Internal"
        if self.expose:
            N = O
        d = self.d = {}
        d["x"] = t("x", [self.n_seq, S, D], F32, I)
        for n in ("wqT", "wkT", "wvT", "woT", "wQ", "wK", "wV", "wO"):
            d[n] = t(n, [D, D], F32, I)
        d["bq_f"] = t("bq_f", [128, NF], F32, I)
        d["bk_f"] = t("bk_f", [128, NF], F32, I)
        for n in ("bv_r", "bo_r", "bcon_r", "dbias_r"):
            d[n] = t(n, [1, D], F32, I)
        d["wrT"] = t("wrT", [D, EXP], F32, I)
        d["w1"] = t("w1", [EXP, D, MH], F32, I)
        d["w3"] = t("w3", [EXP, D, MH], F32, I)
        d["w2"] = t("w2", [EXP, MH, D], F32, I)
        d["wexpT"] = t("wexpT", [DH, D], F32, I)
        d["wcon"] = t("wcon", [DH, D], F32, I)
        d["bexp_f"] = t("bexp_f", [128, NHU], F32, I)
        d["upbias_f"] = t("upbias_f", [128, NHU], F32, I)
        d["upA"] = t("upA", [32, 7, 128, 128], F32, I)
        d["dnC"] = t("dnC", [32, 7, 128, 128], F32, I)
        d["upC"] = t("upC", [128, 320], F32, I)
        d["dnCoef"] = t("dnCoef", [128, 320], F32, I)
        d["ident"] = t("ident", [128, 128], BF16, I)
        d["ident32"] = t("ident32", [128, 128], F32, I)
        d["tri16"] = t("tri16", [128, 128 * 16], BF16, I)
        d["ones_c"] = t("ones_c", [128, 1], BF16, I)
        d["ones_r"] = t("ones_r", [1, 128], F32, I)
        d["acc_d"] = t("acc_d", [self.n_seq, S, D], F32, N)
        d["w1b_d"] = t("w1b_d", [EXP, D, MH], BF16, N)
        d["w3b_d"] = t("w3b_d", [EXP, D, MH], BF16, N)
        d["w2b_d"] = t("w2b_d", [EXP, MH, D], BF16, N)
        d["wup_d"] = t("wup_d", [D, DH], BF16, N)
        d["wdown_d"] = t("wdown_d", [DH, D], BF16, N)
        d["bup_d"] = t("bup_d", [128, NHU], F32, N)
        d["bias_bc_d"] = t("bias_bc_d", [3, 128, D], F32, N)
        d["xnT_d"] = t("xnT_d", [self.n_seq, D, S], BF16, N)
        d["gates_d"] = t("gates_d", [self.n_seq, 128, self.NCH * EXP], F32, N)
        d["partial_d"] = t("partial_d", [self.n_seq, S, D], F32, N)
        d["x1_d"] = t("x1_d", [self.n_seq, S, D], F32, N)
        d["xn1T_d"] = t("xn1T_d", [self.n_seq, D, S], BF16, N)
        d["partial1_d"] = t("partial1_d", [self.n_seq, S, D], F32, N)
        d["y"] = t("y", [self.n_seq, S, D], F32, "ExternalOutput")

    _uid = 0

    def sbuf(self, stk, name, shape, dtype):
        Builder._uid += 1
        return stk.enter_context(self.nc.sbuf_tensor(f"{name}_{Builder._uid}", shape, dtype))

    def psum(self, stk, name, shape, dtype=F32):
        Builder._uid += 1
        return stk.enter_context(self.nc.psum_tensor(f"{name}_{Builder._uid}", shape, dtype))

    def phase_W(self):
        nc, d = self.nc, self.d
        P = Prog(self.semc)
        stk = ExitStack()
        g = self.g = {}
        g["ident"] = self.sbuf(self.stack, "c_ident", [128, 128], BF16)
        g["ident32"] = self.sbuf(self.stack, "c_ident32", [128, 128], F32)
        g["tri16"] = self.sbuf(self.stack, "c_tri16", [128, 2048], BF16)
        g["ones_c"] = self.sbuf(self.stack, "c_ones_c", [128, 1], BF16)
        g["ones_r"] = self.sbuf(self.stack, "c_ones_r", [1, 128], F32)
        g["bq_f"] = self.sbuf(self.stack, "c_bq_f", [128, NF], F32)
        g["bk_f"] = self.sbuf(self.stack, "c_bk_f", [128, NF], F32)
        g["bv_bc"] = self.sbuf(self.stack, "c_bv_bc", [128, D], F32)
        g["bo_bc"] = self.sbuf(self.stack, "c_bo_bc", [128, D], F32)
        g["bcd_bc"] = self.sbuf(self.stack, "c_bcd_bc", [128, D], F32)
        g["eps_ln"] = self.sbuf(self.stack, "c_eps_ln", [128, 1], F32)

        P.op("vector", lambda e: e.memset(g["eps_ln"][:], LN_EPS), inc=["dv"])
        for n in ("ident32", "ident", "tri16", "ones_c", "ones_r", "bq_f", "bk_f"):
            src = d[n][:, 0:2048] if n == "tri16" else d[n][:]
            P.dma("qs", lambda e, a=g[n], b=src: e.dma_start(a[:], b))
        cvstk = ExitStack()
        rows = self.sbuf(cvstk, "w_rows", [1, 4 * D], F32)
        for i, n in enumerate(("bv_r", "bo_r", "bcon_r", "dbias_r")):
            P.dma("qs", lambda e, i=i, n=n: e.dma_start(rows[0:1, i * D:(i + 1) * D], d[n][:]))
        pbc = self.psum(stk, "w_pbc", [128, 512], F32)
        qs_done = P.n("qs")
        for dst, ri in ((g["bv_bc"], 0), (g["bo_bc"], 1)):
            for h in range(2):
                P.op("tensor", lambda e, ri=ri, h=h: e.matmul(
                    pbc[:], g["ones_r"][:], rows[0:1, ri * D + h * 512: ri * D + (h + 1) * 512],
                    start=True, stop=True), wait=[("qs", qs_done)], inc=["pe"])
                P.op("vector", lambda e, dst=dst, h=h: e.tensor_copy(dst[:, h * 512:(h + 1) * 512], pbc[:]),
                     wait=["pe"], inc=["dv"])
                P.op("tensor", lambda e: e.nop(), wait=["dv"])
        for h in range(2):
            P.op("tensor", lambda e, h=h: e.matmul(
                pbc[:], g["ones_r"][:], rows[0:1, 2 * D + h * 512: 2 * D + (h + 1) * 512],
                start=True, stop=True), wait=[("qs", qs_done)], inc=["pe"])
            P.op("vector", lambda e, h=h: e.tensor_copy(g["bcd_bc"][:, h * 512:(h + 1) * 512], pbc[:]),
                 wait=["pe"], inc=["dv"])
            P.op("tensor", lambda e: e.nop(), wait=["dv"])
            P.op("tensor", lambda e, h=h: e.matmul(
                pbc[:], g["ones_r"][:], rows[0:1, 3 * D + h * 512: 3 * D + (h + 1) * 512],
                start=True, stop=True), wait=[("qs", qs_done)], inc=["pe"])
            P.op("vector", lambda e, h=h: e.tensor_tensor(
                g["bcd_bc"][:, h * 512:(h + 1) * 512], g["bcd_bc"][:, h * 512:(h + 1) * 512],
                pbc[:], ALU.add), wait=["pe", "dv"], inc=["dv"])
            P.op("tensor", lambda e: e.nop(), wait=["dv"])

        NW = NF * MH
        cstg = self.sbuf(cvstk, "wc_stg", [128, 2 * NW], F32)
        cbuf = self.sbuf(cvstk, "wc_buf", [128, 2 * NW], BF16)
        conv_jobs = []
        for e_ in range(EXP):
            conv_jobs += [(d["w1"], d["w1b_d"], e_, NF, MH), (d["w3"], d["w3b_d"], e_, NF, MH),
                          (d["w2"], d["w2b_d"], e_, NHT, D)]
        for ji, (srcw, dstw, e_, nch, w) in enumerate(conv_jobs):
            pp = ji % 2
            n = nch * w
            strm = "qs" if ji % 2 == 0 else "qa"
            P.dma(strm, lambda e, srcw=srcw, e_=e_, nch=nch, n=n, pp=pp: e.dma_start(
                cstg[:, pp * NW: pp * NW + n].rearrange("p (c x) -> p c x", c=nch),
                srcw[e_].rearrange("(c p) x -> p c x", p=128)),
                war=(["dv"] if ji % 2 == 0 else ["ac"]))
            if ji % 2 == 0:
                P.op("vector", lambda e, n=n, pp=pp: e.tensor_copy(
                    cbuf[:, pp * NW: pp * NW + n], cstg[:, pp * NW: pp * NW + n]),
                    wait=["qs", ("qp", P.n("qp"))], inc=["dv"])
            else:
                P.op("scalar", lambda e, n=n, pp=pp: e.activation(
                    cbuf[:, pp * NW: pp * NW + n], cstg[:, pp * NW: pp * NW + n],
                    AF.Copy), wait=["qa", ("qp", P.n("qp"))], inc=["ac"])
            P.dma("qp", lambda e, dstw=dstw, e_=e_, nch=nch, n=n, pp=pp: e.dma_start(
                dstw[e_].rearrange("(c p) x -> p c x", p=128),
                cbuf[:, pp * NW: pp * NW + n].rearrange("p (c x) -> p c x", c=nch)),
                war=["dv", "ac"])
        with self.nc.Block() as blk:
            P.emit_into_block(blk, None, self.sems)
        cvstk.close()
        P = Prog(self.semc)
        wt = self.sbuf(stk, "w_wt", [128, NHU * D], F32)
        coef = self.sbuf(stk, "w_coef", [128, 320], F32)
        blkbuf = self.sbuf(stk, "w_blk", [128, 2 * 7 * 128], F32)
        comp = self.sbuf(stk, "w_comp", [128, 32 * 128], F32)
        scr0 = self.sbuf(stk, "w_scr0", [128, D], F32)
        scr1 = self.sbuf(stk, "w_scr1", [128, D], F32)
        scr0b = self.sbuf(stk, "w_scr0b", [128, D], F32)
        scr1b = self.sbuf(stk, "w_scr1b", [128, D], F32)
        bvec = self.sbuf(stk, "w_bvec", [128, NHU], F32)
        bscr = self.sbuf(stk, "w_bscr", [128, 2], F32)
        rowb = self.sbuf(stk, "w_rowb", [128, DH], BF16)
        tb16 = self.sbuf(stk, "w_tb16", [128, 128], BF16)
        pcmp = self.psum(stk, "w_pcmp", [128, 128], F32)
        pcmp2 = self.psum(stk, "w_pcmp2", [128, 128], F32)
        papp = self.psum(stk, "w_papp", [128, 512], F32)
        papp2 = self.psum(stk, "w_papp2", [128, 512], F32)
        pbv = self.psum(stk, "w_pbv", [128, 1], F32)
        ptr = self.psum(stk, "w_ptr", [128, 128], BF16)
        ptr2 = self.psum(stk, "w_ptr2", [128, 128], BF16)

        def compose_blocks(src_ap, order_rev):
            seed_k = 6 if order_rev else 0
            ks = list(range(5, -1, -1) if order_rev else range(1, 7))
            dmad = {}
            for pair in range(16):
                for half in range(2):
                    blk = pair * 2 + half
                    pp = blk % 2
                    bb = blkbuf[:, pp * 7 * 128:(pp + 1) * 7 * 128]
                    P.dma("qs", lambda e, blk=blk, bb=bb: e.dma_start(
                        bb[:].rearrange("p (k n) -> p k n", k=7), src_ap[blk]),
                        war=["pe", "dv"])
                    dmad[blk] = P.n("qs")
                st = {}
                for half in range(2):
                    blk = pair * 2 + half
                    bb = blkbuf[:, (blk % 2) * 7 * 128:((blk % 2) + 1) * 7 * 128]
                    P.op("vector", lambda e, bb=bb, blk=blk: e.tensor_copy(
                        comp[:, blk * 128:(blk + 1) * 128],
                        bb[:, seed_k * 128:(seed_k + 1) * 128]),
                        wait=[("qs", dmad[blk])], inc=["dv"])
                    st[half] = {"dv": P.n("dv"), "pe": 0}
                for k in ks:
                    for half in range(2):
                        blk = pair * 2 + half
                        bb = blkbuf[:, (blk % 2) * 7 * 128:((blk % 2) + 1) * 7 * 128]
                        pc = pcmp if half == 0 else pcmp2
                        P.op("tensor", lambda e, bb=bb, blk=blk, k=k, pc=pc: e.matmul(
                            pc[:], bb[:, k * 128:(k + 1) * 128],
                            comp[:, blk * 128:(blk + 1) * 128],
                            start=True, stop=True),
                            wait=[("dv", st[half]["dv"])], inc=["pe"])
                        st[half]["pe"] = P.n("pe")
                    for half in range(2):
                        blk = pair * 2 + half
                        pc = pcmp if half == 0 else pcmp2
                        P.op("vector", lambda e, blk=blk, pc=pc: e.tensor_copy(
                            comp[:, blk * 128:(blk + 1) * 128], pc[:]),
                            wait=[("pe", st[half]["pe"])], inc=["dv"])
                        st[half]["dv"] = P.n("dv")

        def apply_blocks():
            for blk in range(32):
                base = blk * D
                for h, pp in ((0, papp), (1, papp2)):
                    P.op("tensor", lambda e, blk=blk, base=base, h=h, pp=pp: e.matmul(
                        pp[:], comp[:, blk * 128:(blk + 1) * 128],
                        wt[:, base + h * 512: base + (h + 1) * 512], start=True, stop=True),
                        wait=["dv"], inc=["pe"])
                for h, pp in ((0, papp), (1, papp2)):
                    P.op("vector", lambda e, base=base, h=h, pp=pp: e.tensor_copy(
                        wt[:, base + h * 512: base + (h + 1) * 512], pp[:]),
                        wait=["pe"], inc=["dv"])
                P.op("tensor", lambda e: e.nop(), wait=["dv"])

        def apply_blocks_bvec():
            for blk in range(32):
                P.op("tensor", lambda e, blk=blk: e.matmul(
                    pbv[:], comp[:, blk * 128:(blk + 1) * 128], bvec[:, blk:blk + 1],
                    start=True, stop=True), wait=["dv"], inc=["pe"])
                P.op("vector", lambda e, blk=blk: e.tensor_copy(bvec[:, blk:blk + 1], pbv[:]),
                     wait=["pe"], inc=["dv"])
                P.op("tensor", lambda e: e.nop(), wait=["dv"])

        def vector_stages(with_bvec):
            scrs = [(scr0, scr1), (scr0b, scr1b)]
            dv_after = {}
            gi = 0
            for q in range(5):
                bit = 1 << q
                atiles = [t for t in range(32) if not (t & bit)]
                for pi, Ta in enumerate(atiles):
                    Tb = Ta + bit
                    cbase = (q * 16 + pi) * 4
                    ca0 = coef[:, cbase + 0:cbase + 1]
                    cb0 = coef[:, cbase + 1:cbase + 2]
                    ca1 = coef[:, cbase + 2:cbase + 3]
                    cb1 = coef[:, cbase + 3:cbase + 4]
                    a = wt[:, Ta * D:(Ta + 1) * D]
                    b = wt[:, Tb * D:(Tb + 1) * D]
                    s0, s1_ = scrs[gi % 2]
                    if pi == 0:
                        wv = ["dv"]
                    else:
                        wv = [("dv", dv_after[gi - 2])] if gi >= 2 else []
                    P.op("scalar", lambda e, b=b, cb0=cb0, s0=s0: e.activation(
                        s0[:], b, AF.Copy, scale=cb0), wait=wv, inc=["ac"])
                    P.op("scalar", lambda e, a=a, ca1=ca1, s1_=s1_: e.activation(
                        s1_[:], a, AF.Copy, scale=ca1), inc=["ac"])
                    ac_now = P.n("ac")
                    P.op("vector", lambda e, a=a, ca0=ca0, s0=s0: e.scalar_tensor_tensor(
                        a, a, ca0, s0[:], ALU.mult, ALU.add), wait=[("ac", ac_now)], inc=["dv"])
                    P.op("vector", lambda e, b=b, cb1=cb1, s1_=s1_: e.scalar_tensor_tensor(
                        b, b, cb1, s1_[:], ALU.mult, ALU.add), inc=["dv"])
                    dv_after[gi] = P.n("dv")
                    gi += 1
            if with_bvec:
                for q in range(5):
                    bit = 1 << q
                    atiles = [t for t in range(32) if not (t & bit)]
                    for pi, Ta in enumerate(atiles):
                        Tb = Ta + bit
                        cbase = (q * 16 + pi) * 4
                        ca0 = coef[:, cbase + 0:cbase + 1]
                        cb0 = coef[:, cbase + 1:cbase + 2]
                        ca1 = coef[:, cbase + 2:cbase + 3]
                        cb1 = coef[:, cbase + 3:cbase + 4]
                        va = bvec[:, Ta:Ta + 1]
                        vb = bvec[:, Tb:Tb + 1]
                        P.op("scalar", lambda e, vb=vb, cb0=cb0: e.activation(
                            bscr[:, 0:1], vb, AF.Copy, scale=cb0), wait=["dv"], inc=["ac"])
                        P.op("scalar", lambda e, va=va, ca1=ca1: e.activation(
                            bscr[:, 1:2], va, AF.Copy, scale=ca1), inc=["ac"])
                        P.op("vector", lambda e, va=va, ca0=ca0: e.scalar_tensor_tensor(
                            va, va, ca0, bscr[:, 0:1], ALU.mult, ALU.add), wait=["ac"], inc=["dv"])
                        P.op("vector", lambda e, vb=vb, cb1=cb1: e.scalar_tensor_tensor(
                            vb, vb, cb1, bscr[:, 1:2], ALU.mult, ALU.add), inc=["dv"])
                        P.op("scalar", lambda e: e.nop(), wait=["dv"])

        P.dma("qs", lambda e: e.dma_start(
            wt[:].rearrange("p (t d) -> p t d", t=NHU), d["wexpT"].rearrange("(t p) d -> p t d", p=128)),
            war=["dv", "qp"])
        P.dma("qs", lambda e: e.dma_start(coef[:], d["upC"][:]))
        P.dma("qs", lambda e: e.dma_start(bvec[:], d["bexp_f"][:]))
        wt_loaded = P.n("qs")
        P.op("vector", lambda e: e.tensor_copy(scr0[:, 0:1], wt[:, 0:1]),
             wait=[("qs", wt_loaded)], inc=["dv"])
        compose_blocks(d["upA"], order_rev=True)
        apply_blocks()
        apply_blocks_bvec()
        vector_stages(with_bvec=True)
        ub = self.sbuf(stk, "w_ub", [128, NHU], F32)
        P.dma("qs", lambda e: e.dma_start(ub[:], d["upbias_f"][:]))
        P.op("vector", lambda e: e.tensor_tensor(bvec[:], bvec[:], ub[:], ALU.add),
             wait=["qs", "dv"], inc=["dv"])
        P.dma("qp", lambda e: e.dma_start(d["bup_d"][:], bvec[:]), war=["dv"])
        for dc in range(NF):
            for t32 in range(NHU):
                pp = t32 % 2
                ptr_ = ptr if pp == 0 else ptr2
                P.op("vector", lambda e, dc=dc, t32=t32: e.tensor_copy(
                    tb16[:], wt[:, t32 * D + dc * 128: t32 * D + (dc + 1) * 128]),
                    wait=["dv", "pe"], inc=["dv"])
                P.op("tensor", lambda e, ptr_=ptr_: e.transpose(ptr_[:], tb16[:], g["ident"][:]),
                     wait=["dv"], inc=["pe"])
                P.op("vector", lambda e, t32=t32, ptr_=ptr_: e.tensor_copy(
                    rowb[:, t32 * 128:(t32 + 1) * 128], ptr_[:]), wait=["pe"], inc=["dv"])
            P.dma("qp", lambda e, dc=dc: e.dma_start(
                d["wup_d"][dc * 128:(dc + 1) * 128, :], rowb[:]), war=["dv"])
            P.op("vector", lambda e: e.tensor_copy(scr0[:, 0:1], scr0[:, 0:1]),
                 wait=[("qp", P.n("qp"))], inc=["dv"])

        P.dma("qs", lambda e: e.dma_start(
            wt[:].rearrange("p (t d) -> p t d", t=NHU), d["wcon"].rearrange("(t p) d -> p t d", p=128)),
            war=["dv"])
        P.dma("qs", lambda e: e.dma_start(coef[:], d["dnCoef"][:]))
        wt2_loaded = P.n("qs")
        P.op("vector", lambda e: e.tensor_copy(scr0[:, 0:1], wt[:, 0:1]),
             wait=[("qs", wt2_loaded)], inc=["dv"])
        vector_stages(with_bvec=False)
        compose_blocks(d["dnC"], order_rev=False)
        apply_blocks()
        for t32 in range(NHU):
            P.op("vector", lambda e, t32=t32: e.tensor_copy(
                rowb[:, 0:D], wt[:, t32 * D:(t32 + 1) * D]),
                wait=["dv", ("qp", P.n("qp"))], inc=["dv"])
            P.dma("qp", lambda e, t32=t32: e.dma_start(
                d["wdown_d"][t32 * 128:(t32 + 1) * 128, :], rowb[:, 0:D]), war=["dv"])
        for i, n in enumerate(("bv_bc", "bo_bc", "bcd_bc")):
            P.dma("qp", lambda e, i=i, n=n: e.dma_start(d["bias_bc_d"][i], g[n][:]), war=["dv"])
        P.op("gpsimd", lambda e: e.nop(), wait=[("qp", P.n("qp"))])

        with self.nc.Block() as blk:
            P.emit_into_block(blk, None, self.sems)
        stk.close()

    def phase_AB(self):
        for b in range(self.n_seq):
            self._ab_seq(b)

    AB_PARTS = ("A", "B1", "B2")

    def _ab_seq(self, b):
        nc, d, g = self.nc, self.d, self.g
        S, NCH = self.S, self.NCH
        NG = S // 512
        sstk = ExitStack()
        qT = self.sbuf(sstk, "ab_qT", [128, NF * S], BF16)
        kT = self.sbuf(sstk, "ab_kT", [128, NF * S], BF16)
        vx = self.sbuf(sstk, "ab_vx", [128, NCH * H * 65], BF16)
        gates = self.sbuf(sstk, "ab_gates", [128, NCH * EXP], F32)

        astk = ExitStack()
        wq = self.sbuf(astk, "ab_wq", [128, NF * D], BF16)
        wk = self.sbuf(astk, "ab_wk", [128, NF * D], BF16)
        wv = self.sbuf(astk, "ab_wv", [128, NF * D], BF16)
        wr = self.sbuf(astk, "ab_wr", [128, NF * EXP], F32)
        stg = self.sbuf(astk, "ab_stg", [128, 2 * D], F32)
        xt = self.sbuf(astk, "a_xt", [128, 2 * D], F32)
        xnT = self.sbuf(astk, "a_xnT", [128, NF * 512], BF16)
        xnT32 = self.sbuf(astk, "a_xnT32", [128, NF * 128], F32)
        xnb = self.sbuf(astk, "a_xnb2", [128, 2 * D], F32)
        scr2 = self.sbuf(astk, "a_scr2", [128, 512], F32)
        scr3 = self.sbuf(astk, "a_scr3", [128, 512], F32)
        st8 = self.sbuf(astk, "a_st8", [128, 8], F32)
        pr8 = self.sbuf(astk, "a_pr8", [128, 8], F32)
        ptA = self.psum(astk, "a_ptA", [128, 128], F32)
        ptB = self.psum(astk, "a_ptB", [128, 128], F32)
        pq = self.psum(astk, "a_pq", [128, 512], F32)
        pq2 = self.psum(astk, "a_pq2", [128, 512], F32)
        pr = self.psum(astk, "a_pr", [128, 8], F32)

        P = Prog(self.semc)
        for wsb, srcw in ((wq, d["wqT"]), (wk, d["wkT"]), (wv, d["wvT"])):
            for fi in range(NF):
                pp = fi % 2
                P.dma("qs", lambda e, fi=fi, pp=pp, srcw=srcw: e.dma_start(
                    stg[:, pp * D:(pp + 1) * D], srcw[fi * 128:(fi + 1) * 128, :]),
                    war=["dv"])
                P.op("vector", lambda e, fi=fi, pp=pp, wsb=wsb: e.tensor_copy(
                    wsb[:, fi * D:(fi + 1) * D], stg[:, pp * D:(pp + 1) * D]),
                    wait=["qs"], inc=["dv"])
        P.dma("qs", lambda e: e.dma_start(wr[:].rearrange(
            "p (f x) -> p f x", f=NF), d["wrT"].rearrange("(f p) x -> p f x", p=128)),
            war=["dv"])
        wr_loaded = P.n("qs")
        pq_free = {}
        SUM, SQ, MEAN, VAR, RSTD, NMR = 0, 1, 2, 3, 4, 5
        RSUM, RM1, RM2 = 12, 13, 14
        st8x = self.sbuf(astk, "a_st8x", [128, 16], F32)

        def stats(ch):
            par = (ch % 2) * 16
            pp = ch % 2
            xt_c = xt[:, pp * D:(pp + 1) * D]
            xnb_c = xnb[:, pp * D:(pp + 1) * D]
            C = lambda i: st8x[:, (ch % 2) * 6 + i:(ch % 2) * 6 + i + 1] if False else None
            base = (ch % 2) * 6
            sl = lambda i: st8x[:, base + i: base + i + 1]
            P.dma("qs", lambda e, b=b, ch=ch, xt_c=xt_c: e.dma_start(
                xt_c, d["x"][b, ch * 128:(ch + 1) * 128, :]),
                war=[("ac", ln_free.get(ch - 2, (0, 0))[0]),
                     ("dv", ln_free.get(ch - 2, (0, 0))[1])])
            qs_x = P.n("qs")
            P.op("scalar", lambda e, xt_c=xt_c, xnb_c=xnb_c: e.activation(
                xnb_c, xt_c, AF.Square, accum_out=sl(SQ)),
                wait=[("qs", qs_x), ("pe", trans_free.get(ch - 2, 0))], inc=["ac"])
            P.op("vector", lambda e, xt_c=xt_c: e.reduce_sum(
                sl(SUM), xt_c, axis=AX.X), wait=[("qs", qs_x)], inc=["dv"])
            P.op("vector", lambda e: e.tensor_scalar(
                sl(MEAN), sl(SUM), 1.0 / D, None, ALU.mult), inc=["dv"])
            P.op("vector", lambda e: e.tensor_scalar(
                sl(NMR), sl(MEAN), -1.0 / D, None, ALU.mult), inc=["dv"])
            P.op("vector", lambda e: e.tensor_scalar(
                sl(VAR), sl(SQ), 1.0 / D, None, ALU.mult), wait=["ac"], inc=["dv"])
            P.op("vector", lambda e: e.scalar_tensor_tensor(
                sl(VAR), sl(SUM), sl(NMR), sl(VAR), ALU.mult, ALU.add), inc=["dv"])
            dv_v = P.n("dv")
            P.op("scalar", lambda e: e.activation(
                sl(RSTD), sl(VAR), AF.Ln, bias=g["eps_ln"][:]),
                wait=[("dv", dv_v)], inc=["ac"])
            P.op("scalar", lambda e: e.activation(
                sl(RSTD), sl(RSTD), AF.Exp, scale=-0.5), inc=["ac"])
            P.op("vector", lambda e: e.tensor_scalar(
                sl(NMR), sl(MEAN), -1.0, None, ALU.mult), inc=["dv"])
            return (P.n("ac"), P.n("dv"))

        ln_free = {}
        trans_free = {}
        stat_done = {}

        def transform(ch, gg, ct):
            pp = ch % 2
            base = (ch % 2) * 6
            sl = lambda i: st8x[:, base + i: base + i + 1]
            xt_c = xt[:, pp * D:(pp + 1) * D]
            xnb_c = xnb[:, pp * D:(pp + 1) * D]
            ac_v, dv_v = stat_done[ch]
            P.op("vector", lambda e, xt_c=xt_c, xnb_c=xnb_c: e.tensor_scalar(
                xnb_c, xt_c, sl(NMR), sl(RSTD), ALU.add, ALU.mult),
                wait=[("ac", ac_v), ("dv", dv_v), "pe"], inc=["dv"])
            xn_dv = P.n("dv")
            cp_dv = {0: xn_dv, 1: xn_dv}
            for f in range(NF):
                pt = ptA if f % 2 == 0 else ptB
                P.op("tensor", lambda e, f=f, pt=pt, xnb_c=xnb_c: e.transpose(
                    pt[:], xnb_c[:, f * 128:(f + 1) * 128], g["ident32"][:]),
                    wait=[("dv", cp_dv[f % 2])], inc=["pe"])
                P.op("vector", lambda e, f=f, ct=ct, pt=pt: e.tensor_copy(
                    xnT[:, f * 512 + ct * 128: f * 512 + (ct + 1) * 128], pt[:]),
                    wait=["pe", "qp"], inc=["dv"])
                P.op("vector", lambda e, f=f, pt=pt: e.tensor_copy(
                    xnT32[:, f * 128:(f + 1) * 128], pt[:]), inc=["dv"])
                cp_dv[f % 2] = P.n("dv")
            trans_free[ch] = P.n("pe")
            xnT32_dv = P.n("dv")
            for fi in range(NF):
                P.op("tensor", lambda e, fi=fi: e.matmul(
                    pr[:], xnT32[:, fi * 128:(fi + 1) * 128],
                    wr[:, fi * EXP:(fi + 1) * EXP],
                    start=(fi == 0), stop=(fi == NF - 1)),
                    wait=([("dv", max(xnT32_dv, rt_free[0])), ("qs", wr_loaded),
                           ("ac", rt_ac[0])] if fi == 0 else []),
                    inc=(["pe"] if fi == NF - 1 else []))
            pr_pe = P.n("pe")
            P.op("vector", lambda e: e.reduce_max(
                st8x[:, RM1:RM1 + 1], pr[:], axis=AX.X, negate=True),
                wait=[("pe", pr_pe)], inc=["dv"])
            P.op("scalar", lambda e: e.activation(
                pr8[:], pr[:], AF.Exp, bias=st8x[:, RM1:RM1 + 1],
                accum_out=st8x[:, RSUM:RSUM + 1]), wait=["dv"], inc=["ac"])
            rt_ac[0] = P.n("ac")
            P.op("vector", lambda e: e.reciprocal(
                st8x[:, RM2:RM2 + 1], st8x[:, RSUM:RSUM + 1]), wait=["ac"], inc=["dv"])
            P.op("vector", lambda e: e.tensor_scalar(
                pr8[:], pr8[:], st8x[:, RM2:RM2 + 1], None, ALU.mult), inc=["dv"])
            P.op("vector", lambda e: e.reduce_max(
                st8x[:, RM1:RM1 + 1], pr8[:], axis=AX.X), inc=["dv"])
            P.op("vector", lambda e: e.tensor_scalar(
                scr2[:, 0:8], pr8[:], st8x[:, RM1:RM1 + 1], None, ALU.is_ge), inc=["dv"])
            P.op("vector", lambda e: e.scalar_tensor_tensor(
                scr2[:, 0:8], scr2[:, 0:8], -1e30, pr8[:], ALU.mult, ALU.add), inc=["dv"])
            P.op("vector", lambda e: e.reduce_max(
                st8x[:, RM2:RM2 + 1], scr2[:, 0:8], axis=AX.X), inc=["dv"])
            P.op("vector", lambda e, ch=ch: e.tensor_scalar(
                gates[:, ch * EXP:(ch + 1) * EXP], pr8[:], st8x[:, RM2:RM2 + 1], None,
                ALU.is_ge), inc=["dv"])
            P.op("vector", lambda e, ch=ch: e.tensor_mul(
                gates[:, ch * EXP:(ch + 1) * EXP], gates[:, ch * EXP:(ch + 1) * EXP],
                pr8[:]), inc=["dv"])
            rt_free[0] = P.n("dv")
            ln_free[ch] = (P.n("ac"), P.n("dv"))

        rt_free = {0: 0}
        rt_ac = {0: 0}
        stat_done[0] = stats(0)
        for gg in range(NG):
            for ct in range(4):
                ch = gg * 4 + ct
                if ch + 1 < NCH:
                    stat_done[ch + 1] = stats(ch + 1)
                transform(ch, gg, ct)
            t0 = gg * 512
            P.dma("qp", lambda e, b=b, t0=t0: e.dma_start(
                d["xnT_d"][b].rearrange("(f p) t -> p f t", p=128)[:, :, t0:t0 + 512],
                xnT[:].rearrange("p (f t) -> p f t", f=NF)), war=["dv"])
            for wsb, dstT, bias, isq in ((wq, qT, g["bq_f"], True), (wk, kT, g["bk_f"], True)):
                for fo in range(NF):
                    pqx = pq if fo % 2 == 0 else pq2
                    fa = pq_free.get(id(pqx), ["dv", "ac"])
                    for fi in range(NF):
                        P.op("tensor", lambda e, fo=fo, fi=fi, wsb=wsb, pqx=pqx: e.matmul(
                            pqx[:], wsb[:, fi * D + fo * 128: fi * D + (fo + 1) * 128],
                            xnT[:, fi * 512:(fi + 1) * 512],
                            start=(fi == 0), stop=(fi == NF - 1)),
                            wait=(fa if fi == 0 else []),
                            inc=(["pe"] if fi == NF - 1 else []))
                    P.op("vector", lambda e, fo=fo, bias=bias, pqx=pqx: e.tensor_scalar(
                        scr2[:], pqx[:], bias[:, fo:fo + 1], 0.0, ALU.add, ALU.min),
                        wait=["pe"], inc=["dv"])
                    P.op("vector", lambda e, fo=fo, bias=bias, pqx=pqx: e.tensor_scalar(
                        scr3[:], pqx[:], bias[:, fo:fo + 1], 0.0, ALU.add, ALU.max),
                        inc=["dv"])
                    pq_free[id(pqx)] = [("dv", P.n("dv"))]
                    P.op("scalar", lambda e: e.activation(
                        scr2[:], scr2[:], AF.Exp), wait=["dv"], inc=["ac"])
                    P.op("vector", lambda e, fo=fo, t0=t0, dstT=dstT: e.tensor_add(
                        dstT[:, fo * S + t0: fo * S + t0 + 512], scr2[:], scr3[:]),
                        wait=["ac"], inc=["dv"])
            for ct in range(4):
                ch = gg * 4 + ct
                for oc in range(2):
                    pvx = pq if oc == 0 else pq2
                    fa = pq_free.get(id(pvx), ["dv", "ac"])
                    for fi in range(NF):
                        P.op("tensor", lambda e, ct=ct, oc=oc, fi=fi, pvx=pvx: e.matmul(
                            pvx[:], xnT[:, fi * 512 + ct * 128: fi * 512 + (ct + 1) * 128],
                            wv[:, fi * D + oc * 512: fi * D + (oc + 1) * 512],
                            start=(fi == 0), stop=(fi == NF - 1)),
                            wait=(fa if fi == 0 else []),
                            inc=(["pe"] if fi == NF - 1 else []))
                    vdst = vx[:, ch * H * 65 + oc * 8 * 65: ch * H * 65 + (oc + 1) * 8 * 65]
                    vdst = vdst.rearrange("p (a j) -> p a j", j=65)[:, :, 0:64]
                    P.op("vector", lambda e, oc=oc, vdst=vdst, pvx=pvx: e.scalar_tensor_tensor(
                        vdst, pvx[:].rearrange("p (a j) -> p a j", j=64), 1.0,
                        g["bv_bc"][:, oc * 512:(oc + 1) * 512].rearrange("p (a j) -> p a j", j=64),
                        ALU.mult, ALU.add), wait=["pe"], inc=["dv"])
                    pq_free[id(pvx)] = [("dv", P.n("dv"))]
        P.op("vector", lambda e: e.memset(
            vx[:].rearrange("p (c j) -> p c j", j=65)[:, :, 64:65], 1.0), inc=["dv"])
        P.dma("qp", lambda e, b=b: e.dma_start(d["gates_d"][b], gates[:]), war=["dv"])
        with nc.Block() as blk:
            P.emit_into_block(blk, None, self.sems)
        astk.close()
        o_s = self.sbuf(sstk, "ab_o", [128, NCH * D], BF16)
        if "B1" not in self.AB_PARTS:
            sstk.close()
            return

        bstk = ExitStack()
        atm = self.sbuf(bstk, "b_atm", [128, H * 128], BF16)
        ktok = self.sbuf(bstk, "b_ktok", [128, D], BF16)
        stS = self.sbuf(bstk, "b_stS", [128, 8 * 128], BF16)
        stZ = self.sbuf(bstk, "b_stZ", [128, 16], BF16)
        dsc = self.sbuf(bstk, "b_dsc", [128, 2 * H], F32)
        p_at = self.psum(bstk, "b_pat", [128, 2048], F32)
        p_stS = self.psum(bstk, "b_pstS", [128, 512], F32)
        p_stZ = self.psum(bstk, "b_pstZ", [128, 8], F32)
        p_tk = self.psum(bstk, "b_ptk", [128, 128], BF16)
        p_tk2 = self.psum(bstk, "b_ptk2", [128, 128], BF16)

        LV = getattr(self, "B1_LEVEL", 5)
        SLOT = [(h % 2) * 8 + (h // 2) for h in range(H)]
        P = Prog(self.semc)
        P.op("vector", lambda e: e.memset(p_stS[:], 0.0), inc=["dv"])
        P.op("vector", lambda e: e.memset(p_stZ[:], 0.0), inc=["dv"])
        P.op("vector", lambda e: e.memset(stS[:], 0.0), inc=["dv"])
        P.op("vector", lambda e: e.memset(stZ[:], 0.0), inc=["dv"])
        for ch in range(NCH):
            ts = ch * 128
            tk_free = {0: ("dv", P.n("dv")), 1: ("dv", P.n("dv"))}
            for f in range(NF):
                ptk = p_tk if f % 2 == 0 else p_tk2
                P.op("tensor", lambda e, f=f, ts=ts, ptk=ptk: e.transpose(
                    ptk[:], kT[:, f * S + ts: f * S + ts + 128], g["ident"][:]),
                    wait=([("ac", P.n("ac")), tk_free[f % 2]] if f <= 1 else [tk_free[f % 2]]),
                    inc=["pe"])
                P.op("vector", lambda e, f=f, ptk=ptk: e.tensor_copy(
                    ktok[:, f * 128:(f + 1) * 128], ptk[:]), wait=["pe"], inc=["dv"])
                tk_free[f % 2] = ("dv", P.n("dv"))
            if LV < 2:
                continue
            for sl in range(H):
                h = (sl // 8) + 2 * (sl % 8)
                f, po = h // 2, 64 * (h % 2)
                P.op("tensor", lambda e, sl=sl, f=f, po=po, ts=ts: e.matmul(
                    p_at[:, sl * 128:(sl + 1) * 128],
                    kT[po:po + 64, f * S + ts: f * S + ts + 128],
                    qT[po:po + 64, f * S + ts: f * S + ts + 128],
                    start=(sl % 4 == 0), stop=(sl % 4 == 3)),
                    wait=(["dv", "ac"] if sl == 0 else []), inc=(["pe"] if sl == H - 1 else []))
            for bk in range(4):
                P.op("vector", lambda e, bk=bk: e.tensor_tensor(
                    atm[:, bk * 512:(bk + 1) * 512], p_at[:, bk * 512:(bk + 1) * 512],
                    g["tri16"][:, bk * 512:(bk + 1) * 512], ALU.mult),
                    wait=(["pe"] if bk == 0 else []), inc=["dv"])
            if LV < 3:
                continue
            if LV >= 5 and ch > 0:
                sv = stS[:].rearrange("p (j c) -> p j c", c=128)
                P.op("vector", lambda e, sv=sv: e.tensor_copy(
                    sv[0:64, :, 0:64], p_stS[0:64, :].rearrange("p (j c) -> p j c", c=64)),
                    wait=["pe"], inc=["dv"])
                P.op("scalar", lambda e, sv=sv: e.activation(
                    sv[64:128, :, 64:128], p_stS[64:128, :].rearrange("p (j c) -> p j c", c=64),
                    AF.Copy), wait=["pe"], inc=["ac"])
                zv = stZ[:].rearrange("p (j c) -> p j c", c=2)
                P.op("vector", lambda e, zv=zv: e.tensor_copy(
                    zv[0:64, :, 0:1], p_stZ[0:64, :].rearrange("p (j c) -> p j c", c=1)),
                    inc=["dv"])
                P.op("scalar", lambda e, zv=zv: e.activation(
                    zv[64:128, :, 1:2], p_stZ[64:128, :].rearrange("p (j c) -> p j c", c=1),
                    AF.Copy), inc=["ac"])
            intra_stop = (ch == 0 or LV < 5)
            for h in range(H):
                asl = SLOT[h] * 128
                vsl = vx[:, ch * H * 65 + h * 65: ch * H * 65 + h * 65 + 64]
                P.op("tensor", lambda e, h=h, vsl=vsl, asl=asl, st_=intra_stop: e.matmul(
                    p_at[:, h * 64:(h + 1) * 64], atm[:, asl: asl + 128], vsl,
                    start=(h % 8 == 0), stop=(st_ and h % 8 == 7)),
                    wait=(["dv"] if h == 0 else []))
                P.op("tensor", lambda e, h=h, asl=asl, st_=intra_stop: e.matmul(
                    p_at[:, 2032 + h: 2033 + h], atm[:, asl: asl + 128],
                    g["ones_c"][:], start=(h == 0), stop=(st_ and h == H - 1)),
                    inc=(["pe"] if (intra_stop and h == H - 1) else []))
            if LV >= 5 and ch > 0:
                for j in range(8):
                    P.op("tensor", lambda e, j=j, ts=ts: e.matmul(
                        p_at[:, j * 128:(j + 1) * 128],
                        qT[:, j * S + ts: j * S + ts + 128],
                        stS[:, j * 128:(j + 1) * 128],
                        start=False, stop=(j % 4 == 3)),
                        wait=(["ac"] if j == 0 else []))
                    P.op("tensor", lambda e, j=j, ts=ts: e.matmul(
                        p_at[:, 2032 + 2 * j: 2034 + 2 * j],
                        qT[:, j * S + ts: j * S + ts + 128],
                        stZ[:, 2 * j: 2 * j + 2], start=False, stop=(j == 7)),
                        inc=(["pe"] if j == 7 else []))
            if LV < 4:
                pass
            for h in range(H) if LV >= 4 else []:
                vsl = vx[:, ch * H * 65 + h * 65: ch * H * 65 + h * 65 + 64]
                po, sc = 64 * (h % 2), 64 * (h // 2)
                P.op("tensor", lambda e, h=h, vsl=vsl, po=po, sc=sc: e.matmul(
                    p_stS[po:po + 64, sc:sc + 64], ktok[:, h * 64:(h + 1) * 64], vsl,
                    start=False, stop=False, skip_group_check=True),
                    wait=(["ac"] if h == 0 else []))
                P.op("tensor", lambda e, h=h, po=po: e.matmul(
                    p_stZ[po:po + 64, h // 2: h // 2 + 1], ktok[:, h * 64:(h + 1) * 64],
                    g["ones_c"][:], start=False, stop=False,
                    skip_group_check=True), inc=(["pe"] if h == H - 1 else []))
            dens = p_at[:, 2032:2048]
            P.op("vector", lambda e, dens=dens: e.tensor_scalar(
                dsc[:, 0:H], dens, ATT_EPS, None, ALU.add), wait=["pe"], inc=["dv"])
            P.op("vector", lambda e: e.reciprocal(dsc[:, H:2 * H], dsc[:, 0:H]), inc=["dv"])
            for h in range(H):
                P.op("scalar", lambda e, h=h, ch=ch: e.activation(
                    o_s[:, ch * D + h * 64: ch * D + (h + 1) * 64],
                    p_at[:, h * 64:(h + 1) * 64], AF.Copy,
                    scale=dsc[:, H + h:H + h + 1]), wait=["dv"], inc=["ac"])
        with nc.Block() as blk:
            P.emit_into_block(blk, None, self.sems)
        bstk.close()
        if "B2" not in self.AB_PARTS:
            sstk.close()
            return

        cstk = ExitStack()
        wo = self.sbuf(cstk, "b2_wo", [128, NF * D], BF16)
        stg2 = self.sbuf(cstk, "b2_stg", [128, 2 * D], F32)
        oT = self.sbuf(cstk, "b2_oT", [128, D], BF16)
        xt2 = self.sbuf(cstk, "b2_xt", [128, 2 * D], F32)
        pout = self.sbuf(cstk, "b2_pout", [128, 2 * D], F32)
        p_tr1 = self.psum(cstk, "b2_ptr1", [128, 128], BF16)
        p_tr2 = self.psum(cstk, "b2_ptr2", [128, 128], BF16)
        p_o = self.psum(cstk, "b2_po", [128, D], F32)

        P = Prog(self.semc)
        for fi in range(NF):
            pp = fi % 2
            P.dma("qs", lambda e, fi=fi, pp=pp: e.dma_start(
                stg2[:, pp * D:(pp + 1) * D], d["woT"][fi * 128:(fi + 1) * 128, :]),
                war=["dv"])
            P.op("vector", lambda e, fi=fi, pp=pp: e.tensor_copy(
                wo[:, fi * D:(fi + 1) * D], stg2[:, pp * D:(pp + 1) * D]),
                wait=["qs"], inc=["dv"])
        for ch in range(NCH):
            pp = ch % 2
            P.dma("qs", lambda e, b=b, ch=ch, pp=pp: e.dma_start(
                xt2[:, pp * D:(pp + 1) * D], d["x"][b, ch * 128:(ch + 1) * 128, :]),
                war=["dv"])
            qs_x = P.n("qs")
            for f in range(NF):
                pt = p_tr1 if f % 2 == 0 else p_tr2
                P.op("tensor", lambda e, f=f, ch=ch, pt=pt: e.transpose(
                    pt[:], o_s[:, ch * D + f * 128: ch * D + (f + 1) * 128], g["ident"][:]),
                    wait=["dv"], inc=["pe"])
                P.op("vector", lambda e, f=f, pt=pt: e.tensor_copy(
                    oT[:, f * 128:(f + 1) * 128], pt[:]), wait=["pe"], inc=["dv"])
            for dc in range(2):
                for fe in range(NF):
                    P.op("tensor", lambda e, dc=dc, fe=fe: e.matmul(
                        p_o[:, dc * 512:(dc + 1) * 512], oT[:, fe * 128:(fe + 1) * 128],
                        wo[:, fe * D + dc * 512: fe * D + (dc + 1) * 512],
                        start=(fe == 0), stop=(fe == NF - 1)),
                        wait=(["dv"] if fe == 0 else []),
                        inc=(["pe"] if fe == NF - 1 else []))
            P.op("vector", lambda e, pp=pp: e.scalar_tensor_tensor(
                pout[:, pp * D:(pp + 1) * D], p_o[:], 1.0, xt2[:, pp * D:(pp + 1) * D],
                ALU.mult, ALU.add), wait=["pe", ("qs", qs_x), "qp"], inc=["dv"])
            P.op("vector", lambda e, pp=pp: e.tensor_tensor(
                pout[:, pp * D:(pp + 1) * D], pout[:, pp * D:(pp + 1) * D],
                g["bo_bc"][:], ALU.add), inc=["dv"])
            P.op("tensor", lambda e: e.nop(), wait=["dv"])
            P.dma("qp", lambda e, b=b, ch=ch, pp=pp: e.dma_start(
                d["partial_d"][b, ch * 128:(ch + 1) * 128, :], pout[:, pp * D:(pp + 1) * D]),
                war=["dv"])
        P.op("gpsimd", lambda e: e.nop(), wait=["qp"])
        with nc.Block() as blk:
            P.emit_into_block(blk, None, self.sems)
        cstk.close()
        sstk.close()

    def phase_C(self):
        for b in range(self.n_seq):
            self._c_seq(b)

    def _c_seq(self, b):
        nc, d, g = self.nc, self.d, self.g
        S, NCH = self.S, self.NCH
        NTG = S // 256
        stk = ExitStack()
        gts = self.sbuf(stk, "c_gts", [128, NCH * EXP], F32)
        xnT = self.sbuf(stk, "c_xnT", [128, 2 * NF * 256], BF16)
        accs = self.sbuf(stk, "c_accs", [128, 2 * 2 * D], F32)
        w1sb = [self.sbuf(stk, "c_w1", [128, NF * MH], BF16) for _ in range(2)]
        w3sb = [self.sbuf(stk, "c_w3", [128, NF * MH], BF16) for _ in range(2)]
        w2sb = [self.sbuf(stk, "c_w2", [128, NHT * D], BF16) for _ in range(2)]
        s1 = self.sbuf(stk, "c_s1", [128, 2 * 256], F32)
        gsb = self.sbuf(stk, "c_gsb", [128, 2 * 256], BF16)
        xpt = self.sbuf(stk, "c_xpt", [128, 2 * D], F32)
        x1t = self.sbuf(stk, "c_x1t", [128, 2 * D], F32)
        p_h1 = [self.psum(stk, "c_ph1", [128, 256], F32) for _ in range(2)]
        p_h3 = [self.psum(stk, "c_ph3", [128, 256], F32) for _ in range(2)]
        p_y0 = self.psum(stk, "c_py0", [128, D], F32)
        p_y1 = self.psum(stk, "c_py1", [128, D], F32)

        P = Prog(self.semc)
        P.dma("qs", lambda e: e.dma_start(gts[:], d["gates_d"][b]), war=["dv", "pe"])
        qs_base = P.n("qs")
        xnT_v = xnT[:].rearrange("p (u f t) -> p u f t", u=2, f=NF)
        accs_v = accs[:].rearrange("p (u c x) -> p u c x", u=2, c=2)

        def load_weights(e_, war):
            up = e_ % 2
            P.dma("qs", lambda e, e_=e_, up=up: e.dma_start(
                w1sb[up][:].rearrange("p (f h) -> p f h", f=NF),
                d["w1b_d"][e_].rearrange("(f p) h -> p f h", p=128)), war=war)
            P.dma("qs", lambda e, e_=e_, up=up: e.dma_start(
                w3sb[up][:].rearrange("p (f h) -> p f h", f=NF),
                d["w3b_d"][e_].rearrange("(f p) h -> p f h", p=128)))
            P.dma("qs", lambda e, e_=e_, up=up: e.dma_start(
                w2sb[up][:].rearrange("p (i x) -> p i x", i=NHT),
                d["w2b_d"][e_].rearrange("(i p) x -> p i x", p=128)))
            return P.n("qs")

        def emit_up(e_, tg, ht, ready_qs, free_at):
            pp = ht % 2
            up = e_ % 2
            for pi, (ph, wsb) in enumerate(((p_h1[pp], w1sb[up]), (p_h3[pp], w3sb[up]))):
                for fi in range(NF):
                    w0 = []
                    if pi == 0 and fi == 0:
                        w0 = list(free_at) + [("qs", ready_qs)]
                    P.op("tensor", lambda e, tg=tg, ht=ht, fi=fi, ph=ph, wsb=wsb: e.matmul(
                        ph[:], wsb[:, fi * MH + ht * 128: fi * MH + (ht + 1) * 128],
                        xnT_v[:, tg % 2, fi, :],
                        start=(fi == 0), stop=(fi == NF - 1)),
                        wait=w0, inc=(["pe"] if fi == NF - 1 else []))
            pe_now = P.n("pe")
            P.op("scalar", lambda e, pp=pp: e.activation(
                s1[:, pp * 256:(pp + 1) * 256], p_h1[pp][:], AF.Silu),
                wait=[("pe", pe_now)], inc=["ac"])
            ac_now = P.n("ac")
            P.op("vector", lambda e, pp=pp: e.tensor_tensor(
                gsb[:, pp * 256:(pp + 1) * 256], s1[:, pp * 256:(pp + 1) * 256],
                p_h3[pp][:], ALU.mult), wait=[("ac", ac_now)], inc=["dv"])
            return [("ac", ac_now), ("dv", P.n("dv"))]

        def emit_y(e_, tg, ht, dv_need):
            pp = ht % 2
            up = e_ % 2
            for sub, py in ((0, p_y0), (1, p_y1)):
                for dc in range(2):
                    P.op("tensor", lambda e, sub=sub, dc=dc, ht=ht, py=py, pp=pp, up=up: e.matmul(
                        py[:, dc * 512:(dc + 1) * 512],
                        gsb[:, pp * 256 + sub * 128: pp * 256 + sub * 128 + 128],
                        w2sb[up][:, ht * D + dc * 512: ht * D + (dc + 1) * 512],
                        start=(ht == 0), stop=(ht == NHT - 1)),
                        wait=([("dv", dv_need)] if (dc == 0 and sub == 0) else []),
                        inc=(["pe"] if (ht == NHT - 1 and sub == 1 and dc == 1) else []))

        wq_ready = {0: load_weights(0, ["pe", "dv"])}
        xnT_ready = {}
        xnT_free = {}
        pt_free = {0: 0}
        x1_stored = {0: 0}
        acc_stored = {}
        pe_after_expert = {}
        for e_ in range(EXP):
            for tg in range(NTG):
                war = [("pe", xnT_free.get((e_, tg - 2), xnT_free.get((e_ - 1, NTG - 2 + (tg % 2) if NTG - 2 + (tg % 2) >= 0 else 0), 0)))]
                P.dma("qs", lambda e, b=b, tg=tg: e.dma_start(
                    xnT_v[:, tg % 2], d["xnT_d"][b].rearrange(
                        "(f p) t -> p f t", p=128)[:, :, tg * 256:(tg + 1) * 256]),
                    war=war)
                if e_ > 0:
                    qp_need = max(acc_stored.get((e_ - 1, tg), 0),
                                  acc_stored.get((e_, tg - 2), 0))
                    P.dma("qs", lambda e, b=b, tg=tg: e.dma_start(
                        accs_v[:, tg % 2], d["acc_d"][b, tg * 256:(tg + 1) * 256, :]
                        .rearrange("(c p) x -> p c x", p=128)),
                        war=[("qp", qp_need)])
                xnT_ready[tg] = P.n("qs")
                done_at = {}
                for ht in range(NHT + 1):
                    if ht < NHT:
                        if ht <= 1:
                            free_at = [("dv", P.n("dv")), ("ac", P.n("ac"))]
                        else:
                            free_at = done_at[ht - 2]
                        done_at[ht] = emit_up(e_, tg, ht,
                                              max(wq_ready[e_], xnT_ready[tg]), free_at)
                    if ht > 0:
                        emit_y(e_, tg, ht - 1, dict(done_at[ht - 1])["dv"])
                xnT_free[(e_, tg)] = P.n("pe")
                qs_need = xnT_ready[tg]
                if e_ == EXP - 1:
                    P.dma("qs", lambda e, b=b, tg=tg: e.dma_start(
                        xpt[:].rearrange("p (c x) -> p c x", c=2),
                        d["partial_d"][b, tg * 256:(tg + 1) * 256, :]
                        .rearrange("(c p) x -> p c x", p=128)), war=[("dv", pt_free[0])])
                    qs_need = P.n("qs")
                for sub, py in ((0, p_y0), (1, p_y1)):
                    ch = tg * 2 + sub
                    gcol = gts[:, ch * EXP + e_: ch * EXP + e_ + 1]
                    asl = accs_v[:, tg % 2, sub, :]
                    if e_ == 0:
                        P.op("vector", lambda e, asl=asl, py=py, gcol=gcol: e.tensor_scalar(
                            asl, py[:], gcol, None, ALU.mult),
                            wait=["pe", ("qs", qs_base),
                                  ("qp", acc_stored.get((e_, tg - 2), 0))], inc=["dv"])
                    else:
                        P.op("vector", lambda e, asl=asl, py=py, gcol=gcol: e.scalar_tensor_tensor(
                            asl, py[:], gcol, asl, ALU.mult, ALU.add),
                            wait=["pe", ("qs", qs_need),
                                  ("qp", acc_stored.get((e_, tg - 2), 0))], inc=["dv"])
                    if e_ == EXP - 1:
                        P.op("vector", lambda e, asl=asl, sub=sub: e.tensor_tensor(
                            x1t[:, sub * D:(sub + 1) * D], asl,
                            xpt[:, sub * D:(sub + 1) * D], ALU.add),
                            wait=[("qp", x1_stored[0])], inc=["dv"])
                if e_ == EXP - 1:
                    pt_free[0] = P.n("dv")
                    P.dma("qp", lambda e, b=b, tg=tg: e.dma_start(
                        d["x1_d"][b, tg * 256:(tg + 1) * 256, :].rearrange(
                            "(c p) x -> p c x", p=128),
                        x1t[:].rearrange("p (c x) -> p c x", c=2)), war=["dv"])
                    x1_stored[0] = P.n("qp")
                else:
                    P.dma("qp", lambda e, b=b, tg=tg: e.dma_start(
                        d["acc_d"][b, tg * 256:(tg + 1) * 256, :].rearrange(
                            "(c p) x -> p c x", p=128), accs_v[:, tg % 2]), war=["dv"])
                acc_stored[(e_, tg)] = P.n("qp")
                if tg == 0 and e_ + 1 < EXP:
                    wq_ready[e_ + 1] = load_weights(
                        e_ + 1, [("pe", pe_after_expert.get(e_ - 1, 0))])
            pe_after_expert[e_] = P.n("pe")
        P.op("gpsimd", lambda e: e.nop(), wait=["qp"])
        with nc.Block() as blk:
            P.emit_into_block(blk, None, self.sems)
        stk.close()

    def phase_D(self):
        for b in range(self.n_seq):
            self._d_seq(b)

    def _d_seq(self, b):
        nc, d, g = self.nc, self.d, self.g
        S, NCH = self.S, self.NCH
        NG = S // 512
        sstk = ExitStack()
        qT = self.sbuf(sstk, "d_qT", [128, NF * S], BF16)
        kT = self.sbuf(sstk, "d_kT", [128, NF * S], BF16)
        vx = self.sbuf(sstk, "d_vx", [128, NCH * H * 65], BF16)

        wstk = ExitStack()
        wQ = self.sbuf(wstk, "d_wQ", [128, NF * D], BF16)
        wK = self.sbuf(wstk, "d_wK", [128, NF * D], BF16)
        wV = self.sbuf(wstk, "d_wV", [128, NF * D], BF16)
        stg = self.sbuf(wstk, "d_stg", [128, 2 * D], F32)
        xt = self.sbuf(wstk, "d_xt", [128, 2 * D], F32)
        xnb = self.sbuf(wstk, "d_xnb", [128, D], BF16)
        xnT = self.sbuf(wstk, "d_xnT", [128, NF * 512], BF16)
        scr = self.sbuf(wstk, "d_scr", [128, D], F32)
        st8 = self.sbuf(wstk, "d_st8", [128, 8], F32)
        ptA = self.psum(wstk, "d_ptA", [128, 128], BF16)
        ptB = self.psum(wstk, "d_ptB", [128, 128], BF16)
        pq = self.psum(wstk, "d_pq", [128, 512], F32)
        pq2 = self.psum(wstk, "d_pq2", [128, 512], F32)

        P = Prog(self.semc)
        pq_free = {}
        for wsb, src in ((wQ, d["wQ"]), (wK, d["wK"]), (wV, d["wV"])):
            for fi in range(NF):
                pp = fi % 2
                P.dma("qs", lambda e, fi=fi, pp=pp, src=src: e.dma_start(
                    stg[:, pp * D:(pp + 1) * D], src[fi * 128:(fi + 1) * 128, :]),
                    war=["dv"])
                P.op("vector", lambda e, fi=fi, pp=pp, wsb=wsb: e.tensor_copy(
                    wsb[:, fi * D:(fi + 1) * D], stg[:, pp * D:(pp + 1) * D]),
                    wait=["qs"], inc=["dv"])
        SUM, SQ, MEAN, VAR, RSTD, NMR = range(6)
        st8x = self.sbuf(wstk, "d_st8x", [128, 12], F32)
        xnb2 = self.sbuf(wstk, "d_xnb2", [128, 2 * D], F32)
        ln_free = {}
        trans_free = {}
        stat_done = {}

        def stats(ch):
            pp = ch % 2
            base = pp * 6
            sl = lambda i: st8x[:, base + i: base + i + 1]
            xt_c = xt[:, pp * D:(pp + 1) * D]
            xnb_c = xnb2[:, pp * D:(pp + 1) * D]
            P.dma("qs", lambda e, b=b, ch=ch, xt_c=xt_c: e.dma_start(
                xt_c, d["x1_d"][b, ch * 128:(ch + 1) * 128, :]),
                war=[("ac", ln_free.get(ch - 2, (0, 0))[0]),
                     ("dv", ln_free.get(ch - 2, (0, 0))[1])])
            qs_x = P.n("qs")
            P.op("scalar", lambda e, xt_c=xt_c, xnb_c=xnb_c: e.activation(
                xnb_c, xt_c, AF.Square, accum_out=sl(SQ)),
                wait=[("qs", qs_x), ("pe", trans_free.get(ch - 2, 0))], inc=["ac"])
            P.op("vector", lambda e, xt_c=xt_c: e.reduce_sum(
                sl(SUM), xt_c, axis=AX.X), wait=[("qs", qs_x)], inc=["dv"])
            P.op("vector", lambda e: e.tensor_scalar(
                sl(MEAN), sl(SUM), 1.0 / D, None, ALU.mult), inc=["dv"])
            P.op("vector", lambda e: e.tensor_scalar(
                sl(NMR), sl(MEAN), -1.0 / D, None, ALU.mult), inc=["dv"])
            P.op("vector", lambda e: e.tensor_scalar(
                sl(VAR), sl(SQ), 1.0 / D, None, ALU.mult), wait=["ac"], inc=["dv"])
            P.op("vector", lambda e: e.scalar_tensor_tensor(
                sl(VAR), sl(SUM), sl(NMR), sl(VAR), ALU.mult, ALU.add), inc=["dv"])
            dv_v = P.n("dv")
            P.op("scalar", lambda e: e.activation(
                sl(RSTD), sl(VAR), AF.Ln, bias=g["eps_ln"][:]),
                wait=[("dv", dv_v)], inc=["ac"])
            P.op("scalar", lambda e: e.activation(
                sl(RSTD), sl(RSTD), AF.Exp, scale=-0.5), inc=["ac"])
            P.op("vector", lambda e: e.tensor_scalar(
                sl(NMR), sl(MEAN), -1.0, None, ALU.mult), inc=["dv"])
            return (P.n("ac"), P.n("dv"))

        def transform(ch, ct):
            pp = ch % 2
            base = pp * 6
            sl = lambda i: st8x[:, base + i: base + i + 1]
            xt_c = xt[:, pp * D:(pp + 1) * D]
            xnb_c = xnb2[:, pp * D:(pp + 1) * D]
            ac_v, dv_v = stat_done[ch]
            P.op("vector", lambda e, xt_c=xt_c: e.tensor_scalar(
                xnb[:], xt_c, sl(NMR), sl(RSTD), ALU.add, ALU.mult),
                wait=[("ac", ac_v), ("dv", dv_v), "pe"], inc=["dv"])
            xn_dv = P.n("dv")
            cp_dv = {0: xn_dv, 1: xn_dv}
            for f in range(NF):
                pt = ptA if f % 2 == 0 else ptB
                P.op("tensor", lambda e, f=f, pt=pt: e.transpose(
                    pt[:], xnb[:, f * 128:(f + 1) * 128], g["ident"][:]),
                    wait=[("dv", cp_dv[f % 2])], inc=["pe"])
                P.op("vector", lambda e, f=f, ct=ct, pt=pt: e.tensor_copy(
                    xnT[:, f * 512 + ct * 128: f * 512 + (ct + 1) * 128], pt[:]),
                    wait=["pe", "qp"], inc=["dv"])
                cp_dv[f % 2] = P.n("dv")
            trans_free[ch] = P.n("pe")
            ln_free[ch] = (P.n("ac"), P.n("dv"))

        stat_done[0] = stats(0)
        for gg in range(NG):
            for ct in range(4):
                ch = gg * 4 + ct
                if ch + 1 < NCH:
                    stat_done[ch + 1] = stats(ch + 1)
                transform(ch, ct)
            t0 = gg * 512
            P.dma("qp", lambda e, b=b, t0=t0: e.dma_start(
                d["xn1T_d"][b].rearrange("(f p) t -> p f t", p=128)[:, :, t0:t0 + 512],
                xnT[:].rearrange("p (f t) -> p f t", f=NF)), war=["dv"])
            for wsb, dstT in ((wQ, qT), (wK, kT)):
                for fo in range(NF):
                    pqx = pq if fo % 2 == 0 else pq2
                    fa = pq_free.get(id(pqx), ["dv"])
                    for fi in range(NF):
                        P.op("tensor", lambda e, fo=fo, fi=fi, wsb=wsb, pqx=pqx: e.matmul(
                            pqx[:], wsb[:, fi * D + fo * 128: fi * D + (fo + 1) * 128],
                            xnT[:, fi * 512:(fi + 1) * 512],
                            start=(fi == 0), stop=(fi == NF - 1)),
                            wait=(fa if fi == 0 else []),
                            inc=(["pe"] if fi == NF - 1 else []))
                    P.op("vector", lambda e, fo=fo, t0=t0, dstT=dstT, pqx=pqx: e.tensor_copy(
                        dstT[:, fo * S + t0: fo * S + t0 + 512], pqx[:]),
                        wait=["pe"], inc=["dv"])
                    pq_free[id(pqx)] = [("dv", P.n("dv"))]
            for ct in range(4):
                ch = gg * 4 + ct
                for oc in range(2):
                    pvx = pq if oc == 0 else pq2
                    fa = pq_free.get(id(pvx), ["dv"])
                    for fi in range(NF):
                        P.op("tensor", lambda e, ct=ct, oc=oc, fi=fi, pvx=pvx: e.matmul(
                            pvx[:], xnT[:, fi * 512 + ct * 128: fi * 512 + (ct + 1) * 128],
                            wV[:, fi * D + oc * 512: fi * D + (oc + 1) * 512],
                            start=(fi == 0), stop=(fi == NF - 1)),
                            wait=(fa if fi == 0 else []),
                            inc=(["pe"] if fi == NF - 1 else []))
                    vdst = vx[:, ch * H * 65 + oc * 8 * 65: ch * H * 65 + (oc + 1) * 8 * 65]
                    vdst = vdst.rearrange("p (a j) -> p a j", j=65)[:, :, 0:64]
                    P.op("vector", lambda e, vdst=vdst, pvx=pvx: e.tensor_copy(
                        vdst, pvx[:].rearrange("p (a j) -> p a j", j=64)),
                        wait=["pe"], inc=["dv"])
                    pq_free[id(pvx)] = [("dv", P.n("dv"))]
        P.op("vector", lambda e: e.memset(
            vx[:].rearrange("p (c j) -> p c j", j=65)[:, :, 64:65], 1.0), inc=["dv"])
        with nc.Block() as blk:
            P.emit_into_block(blk, None, self.sems)
        wstk.close()

        fstk = ExitStack()
        o_s = self.sbuf(fstk, "d_o", [128, NCH * D], BF16)
        psb = self.sbuf(fstk, "d_psb", [128, 2 * 2048], BF16)
        dsc = self.sbuf(fstk, "d_dsc", [128, 2 * H], F32)
        wO = self.sbuf(fstk, "d_wO", [128, NF * D], BF16)
        stg2 = self.sbuf(fstk, "d_stg2", [128, 2 * D], F32)
        oT = self.sbuf(fstk, "d_oT", [128, D], BF16)
        xt2 = self.sbuf(fstk, "d_xt2", [128, 2 * D], F32)
        pout = self.sbuf(fstk, "d_pout", [128, 2 * D], F32)
        p_st = self.psum(fstk, "d_pst", [128, 2048], F32)
        p_num = self.psum(fstk, "d_pnum", [128, D], F32)
        p_den = self.psum(fstk, "d_pden", [128, H], F32)
        p_tr = self.psum(fstk, "d_ptr", [128, 128], BF16)

        P = Prog(self.semc)
        for fi in range(NF):
            pp = fi % 2
            P.dma("qs", lambda e, fi=fi, pp=pp: e.dma_start(
                stg2[:, pp * D:(pp + 1) * D], d["wO"][fi * 128:(fi + 1) * 128, :]),
                war=["dv"])
            P.op("vector", lambda e, fi=fi, pp=pp: e.tensor_copy(
                wO[:, fi * D:(fi + 1) * D], stg2[:, pp * D:(pp + 1) * D]),
                wait=["qs"], inc=["dv"])
        for qt in range(NCH):
            qs_ = qt * 128
            SLOT = [(h % 2) * 8 + (h // 2) for h in range(H)]
            exp_done = {}

            def emit_st(kt):
                ks = kt * 128
                pb = (kt % 2) * 2048
                for half in range(2):
                    for sl in range(half * 8, half * 8 + 8):
                        h = (sl // 8) + 2 * (sl % 8)
                        f, po = h // 2, 64 * (h % 2)
                        w0 = ([exp_done[(kt - 1, half)]] if (kt > 0 and sl % 8 == 0)
                              else (["ac", "dv"] if (kt == 0 and sl == 0) else []))
                        P.op("tensor", lambda e, sl=sl, f=f, po=po, ks=ks, qs_=qs_: e.matmul(
                            p_st[:, sl * 128:(sl + 1) * 128],
                            kT[po:po + 64, f * S + ks: f * S + ks + 128],
                            qT[po:po + 64, f * S + qs_: f * S + qs_ + 128],
                            start=(sl % 4 == 0), stop=(sl % 4 == 3)),
                            wait=w0, inc=(["pe"] if sl % 8 == 7 else []))
                    pe_now = P.n("pe")
                    P.op("scalar", lambda e, half=half, pb=pb: e.activation(
                        psb[:, pb + half * 1024: pb + (half + 1) * 1024],
                        p_st[:, half * 1024:(half + 1) * 1024], AF.Exp, scale=0.125),
                        wait=[("pe", pe_now)], inc=["ac"])
                    if kt == qt:
                        P.op("vector", lambda e, half=half, pb=pb: e.tensor_tensor(
                            psb[:, pb + half * 1024: pb + (half + 1) * 1024],
                            psb[:, pb + half * 1024: pb + (half + 1) * 1024],
                            g["tri16"][:, half * 1024:(half + 1) * 1024], ALU.mult),
                            wait=["ac"], inc=["dv"])
                        exp_done[(kt, half)] = ("dv", P.n("dv"))
                    else:
                        exp_done[(kt, half)] = ("ac", P.n("ac"))

            def emit_pv(kt):
                pb = (kt % 2) * 2048
                for half in range(2):
                    heads = [(sl // 8) + 2 * (sl % 8) for sl in range(half * 8, half * 8 + 8)]
                    for i, h in enumerate(heads):
                        asl = pb + SLOT[h] * 128
                        vsl = vx[:, kt * H * 65 + h * 65: kt * H * 65 + h * 65 + 64]
                        w1_ = [exp_done[(kt, half)]] if i == 0 else []
                        if kt == 0 and i == 0:
                            w1_ = w1_ + [("dv", dv_qt0), ("ac", ac_qt0)]
                        P.op("tensor", lambda e, h=h, vsl=vsl, asl=asl, kt=kt, qt=qt: e.matmul(
                            p_num[:, h * 64:(h + 1) * 64], psb[:, asl: asl + 128], vsl,
                            start=(kt == 0 and h % 8 == 0), stop=(kt == qt and h % 8 == 7)),
                            wait=w1_)
                        P.op("tensor", lambda e, h=h, asl=asl, kt=kt, qt=qt: e.matmul(
                            p_den[:, h:h + 1], psb[:, asl: asl + 128], g["ones_c"][:],
                            start=(kt == 0 and h == 0), stop=(kt == qt and h == H - 1)),
                            inc=(["pe"] if (half == 1 and i == 7) else []))

            dv_qt0 = P.n("dv")
            ac_qt0 = P.n("ac")
            for kt in range(qt + 2):
                if kt <= qt:
                    emit_st(kt)
                if kt > 0:
                    emit_pv(kt - 1)
            P.op("vector", lambda e: e.reciprocal(dsc[:, H:2 * H], p_den[:]),
                 wait=["pe"], inc=["dv"])
            for h in range(H):
                P.op("vector", lambda e, h=h, qt=qt: e.tensor_scalar(
                    o_s[:, qt * D + h * 64: qt * D + (h + 1) * 64],
                    p_num[:, h * 64:(h + 1) * 64], dsc[:, H + h:H + h + 1], None,
                    ALU.mult), inc=["dv"])
        for ch in range(NCH):
            pp = ch % 2
            P.dma("qs", lambda e, b=b, ch=ch, pp=pp: e.dma_start(
                xt2[:, pp * D:(pp + 1) * D], d["x1_d"][b, ch * 128:(ch + 1) * 128, :]),
                war=["dv"])
            qs_x = P.n("qs")
            for f in range(NF):
                P.op("tensor", lambda e, f=f, ch=ch: e.transpose(
                    p_tr[:], o_s[:, ch * D + f * 128: ch * D + (f + 1) * 128], g["ident"][:]),
                    wait=["dv", "ac"], inc=["pe"])
                P.op("vector", lambda e, f=f: e.tensor_copy(
                    oT[:, f * 128:(f + 1) * 128], p_tr[:]), wait=["pe"], inc=["dv"])
            for dc in range(2):
                for fe in range(NF):
                    P.op("tensor", lambda e, dc=dc, fe=fe: e.matmul(
                        p_num[:, dc * 512:(dc + 1) * 512], oT[:, fe * 128:(fe + 1) * 128],
                        wO[:, fe * D + dc * 512: fe * D + (dc + 1) * 512],
                        start=(fe == 0), stop=(fe == NF - 1)),
                        wait=(["dv", "ac"] if fe == 0 else []),
                        inc=(["pe"] if fe == NF - 1 else []))
            P.op("vector", lambda e, pp=pp: e.scalar_tensor_tensor(
                pout[:, pp * D:(pp + 1) * D], p_num[:], 1.0, xt2[:, pp * D:(pp + 1) * D],
                ALU.mult, ALU.add), wait=["pe", ("qs", qs_x), "qp"], inc=["dv"])
            P.op("tensor", lambda e: e.nop(), wait=["dv"])
            P.dma("qp", lambda e, b=b, ch=ch, pp=pp: e.dma_start(
                d["partial1_d"][b, ch * 128:(ch + 1) * 128, :], pout[:, pp * D:(pp + 1) * D]),
                war=["dv"])
        P.op("gpsimd", lambda e: e.nop(), wait=["qp"])
        with nc.Block() as blk:
            P.emit_into_block(blk, None, self.sems)
        fstk.close()
        sstk.close()

    def phase_E(self):
        nc, d, g = self.nc, self.d, self.g
        S, NCH = self.S, self.NCH
        stk = ExitStack()
        wup = self.sbuf(stk, "e_wup", [128, NF * DH], BF16)
        wdn = self.sbuf(stk, "e_wdn", [128, NHU * D], BF16)
        bup = self.sbuf(stk, "e_bup", [128, NHU], F32)
        P = Prog(self.semc)
        P.dma("qs", lambda e: e.dma_start(
            wup[:].rearrange("p (f h) -> p f h", f=NF),
            d["wup_d"].rearrange("(f p) h -> p f h", p=128)), war=["pe", "dv"])
        P.dma("qs", lambda e: e.dma_start(
            wdn[:].rearrange("p (u x) -> p u x", u=NHU),
            d["wdown_d"].rearrange("(u p) x -> p u x", p=128)), war=["dv"])
        P.dma("qs", lambda e: e.dma_start(bup[:], d["bup_d"][:]), war=["dv"])
        with nc.Block() as blk:
            P.emit_into_block(blk, None, self.sems)
        for b in range(self.n_seq):
            self._e_seq(b, wup, wdn, bup)
        stk.close()

    def _e_seq(self, b, wup, wdn, bup):
        nc, d, g = self.nc, self.d, self.g
        S, NCH = self.S, self.NCH
        NTG = S // 256
        stk = ExitStack()
        xnT = self.sbuf(stk, "e_xnT", [128, NF * S], BF16)
        gsb = self.sbuf(stk, "e_gsb", [128, NHU * 256], BF16)
        xpt = self.sbuf(stk, "e_xpt", [128, D], F32)
        yt = self.sbuf(stk, "e_yt", [128, D], F32)
        p_h = self.psum(stk, "e_ph", [128, 256], F32)
        p_h2 = self.psum(stk, "e_ph2", [128, 256], F32)
        p_y0 = self.psum(stk, "e_py0", [128, D], F32)
        p_y1 = self.psum(stk, "e_py1", [128, D], F32)

        P = Prog(self.semc)
        P.dma("qs", lambda e: e.dma_start(
            xnT[:].rearrange("p (f t) -> p f t", f=NF),
            d["xn1T_d"][b].rearrange("(f p) t -> p f t", p=128)), war=["dv", "pe"])
        qs_base = P.n("qs")
        for tg in range(NTG):
            for hu in range(NHU):
                ph = p_h if hu % 2 == 0 else p_h2
                for fd in range(NF):
                    P.op("tensor", lambda e, hu=hu, fd=fd, tg=tg, ph=ph: e.matmul(
                        ph[:], wup[:, fd * DH + hu * 128: fd * DH + (hu + 1) * 128],
                        xnT[:, fd * S + tg * 256: fd * S + (tg + 1) * 256],
                        start=(fd == 0), stop=(fd == NF - 1)),
                        wait=(((["ac", "dv", ("qs", qs_base)] if tg == 0 else ["ac"])
                               if hu <= 1 else ["ac"]) if fd == 0 else []),
                        inc=(["pe"] if fd == NF - 1 else []))
                P.op("scalar", lambda e, hu=hu, ph=ph: e.activation(
                    gsb[:, hu * 256:(hu + 1) * 256], ph[:], AF.Gelu,
                    bias=bup[:, hu:hu + 1]), wait=["pe"], inc=["ac"])
            for sub, py in ((0, p_y0), (1, p_y1)):
                for dc in range(2):
                    for hu in range(NHU):
                        P.op("tensor", lambda e, sub=sub, dc=dc, hu=hu, py=py: e.matmul(
                            py[:, dc * 512:(dc + 1) * 512],
                            gsb[:, hu * 256 + sub * 128: hu * 256 + sub * 128 + 128],
                            wdn[:, hu * D + dc * 512: hu * D + (dc + 1) * 512],
                            start=(hu == 0), stop=(hu == NHU - 1)),
                            wait=(["ac", "dv"] if hu == 0 else []),
                            inc=(["pe"] if hu == NHU - 1 else []))
            for sub, py in ((0, p_y0), (1, p_y1)):
                ch = tg * 2 + sub
                P.dma("qs", lambda e, b=b, ch=ch: e.dma_start(
                    xpt[:], d["partial1_d"][b, ch * 128:(ch + 1) * 128, :]), war=["dv"])
                qs_x = P.n("qs")
                P.op("vector", lambda e, py=py: e.scalar_tensor_tensor(
                    yt[:], py[:], 1.0, xpt[:], ALU.mult, ALU.add),
                    wait=["pe", ("qs", qs_x), "qp"], inc=["dv"])
                P.op("vector", lambda e: e.tensor_tensor(
                    yt[:], yt[:], g["bcd_bc"][:], ALU.add), inc=["dv"])
                P.dma("qp", lambda e, b=b, ch=ch: e.dma_start(
                    d["y"][b, ch * 128:(ch + 1) * 128, :], yt[:]), war=["dv"])
        P.op("gpsimd", lambda e: e.nop(), wait=["qp"])
        with nc.Block() as blk:
            P.emit_into_block(blk, None, self.sems)
        stk.close()


from concourse.bass_utils import run_bass_kernel_spmd


def kernel(**inputs):
    in_maps = stage_host(inputs, n_cores=8)
    b = Builder(S=2048, n_seq=2)
    b.phase_W()
    b.phase_AB()
    b.phase_C()
    b.phase_D()
    b.phase_E()
    res = run_bass_kernel_spmd(b.nc, in_maps, core_ids=list(range(8)))
    out = np.concatenate(
        [np.asarray(r["y"], dtype=np.float32).reshape(2, 2048, D) for r in res.results],
        axis=0)
    return out


# version 08
# speedup vs baseline: 1.0304x; 1.0304x over previous
import numpy as np
import ml_dtypes
from contextlib import ExitStack
from collections import defaultdict

import concourse.bass as bass
import concourse.mybir as mybir

F32 = mybir.dt.float32
BF16 = mybir.dt.bfloat16
AF = mybir.ActivationFunctionType
ALU = mybir.AluOpType
AX = mybir.AxisListType
BF = ml_dtypes.bfloat16

D = 1024
H = 16
HD = 64
NF = 8
EXP = 8
MH = 1408
NHT = 11
DH = 4096
NHU = 32
LN_EPS = 1e-5
ATT_EPS = 1e-6


class Prog:

    ENGS = ("sync", "tensor", "vector", "scalar", "gpsimd")

    def __init__(self, semc=None):
        self.ops = []
        self.semc = semc if semc is not None else defaultdict(int)

    def n(self, sem):
        return self.semc[sem]

    DMA_ENG = {"qs": "sync", "qp": "gpsimd", "qa": "scalar", "qv": "vector"}

    def dma(self, stream, fn, war=()):
        self.op(self.DMA_ENG[stream], fn, wait=[stream] + list(war), inc=[stream],
                is_dma=True)
        return self.semc[stream]

    AUTO = {"vector": "dv", "scalar": "ac"}

    def op(self, eng, fn, wait=(), inc=(), is_dma=False):
        ws = []
        for w in wait:
            if isinstance(w, str):
                ws.append((w, self.semc[w]))
            else:
                ws.append((w[0], w[1]))
        incs = []
        for i in inc:
            if isinstance(i, str):
                amt = 16 if i.startswith("q") else 1
                incs.append((i, amt))
            else:
                incs.append((i[0], i[1]))
        own = self.AUTO.get(eng) if not is_dma else None
        if own is not None:
            if not any(s == own for s, _ in ws):
                ws.append((own, self.semc[own]))
            if not any(s == own for s, _ in incs):
                incs.append((own, 1))
        for s, amt in incs:
            self.semc[s] += amt
        self.ops.append((eng, fn, ws, incs))

    def emit_into_block(self, blk, engines, sems):
        for eng_name in self.ENGS:
            my_ops = [o for o in self.ops if o[0] == eng_name]
            if not my_ops:
                continue
            def body(e, my_ops=my_ops, eng_name=eng_name):
                known = defaultdict(int)
                for _, fn, ws, incs in my_ops:
                    for s, v in ws:
                        if known[s] < v:
                            e.wait_ge(sems[s], v)
                            known[s] = v
                    inst = fn(e)
                    for s, amt in incs:
                        assert inst is not None, f"inc {s} on None inst"
                        inst.then_inc(sems[s], amt)

            getattr(blk, eng_name)(body)


def _dense_stage_mats(w, st, transpose):
    s = 1 << st
    out = np.zeros((32, 128, 128), dtype=np.float32)
    ws = w[st].reshape(DH // (2 * s), s, 2, 2)
    for blk in range(32):
        base = blk * 128
        for iloc in range(128):
            i = base + iloc
            if i & s:
                continue
            n = i >> (st + 1)
            j = i & (s - 1)
            w2 = ws[n, j]
            if transpose:
                w2 = w2.T
            out[blk, iloc, iloc] = w2[0, 0]
            out[blk, iloc, iloc + s] = w2[0, 1]
            out[blk, iloc + s, iloc] = w2[1, 0]
            out[blk, iloc + s, iloc + s] = w2[1, 1]
    return out


def stage_up_blocks(up_w):
    arr = np.empty((32, 7, 128, 128), dtype=np.float32)
    for k in range(6):
        arr[:, k] = _dense_stage_mats(up_w, k, transpose=False)
    arr[:, 6] = _dense_stage_mats(up_w, 6, transpose=True)
    return arr


def stage_dn_blocks(dn_w):
    arr = np.empty((32, 7, 128, 128), dtype=np.float32)
    arr[:, 0] = _dense_stage_mats(dn_w, 0, transpose=False)
    for k in range(1, 7):
        arr[:, k] = _dense_stage_mats(dn_w, k, transpose=True)
    return arr


def stage_tile_coefs(w, transpose):
    out = np.empty((128, 5, 16, 4), dtype=np.float32)
    for q in range(5):
        st = 7 + q
        s = 1 << st
        ws = w[st].reshape(DH // (2 * s), s, 2, 2)
        bit = 1 << (st - 7)
        atiles = [t for t in range(32) if not (t & bit)]
        assert len(atiles) == 16
        for pi, T in enumerate(atiles):
            for p in range(128):
                i = T * 128 + p
                n = i >> (st + 1)
                j = i & (s - 1)
                w2 = ws[n, j]
                if transpose:
                    w2 = w2.T
                out[p, q, pi, 0] = w2[0, 0]
                out[p, q, pi, 1] = w2[0, 1]
                out[p, q, pi, 2] = w2[1, 0]
                out[p, q, pi, 3] = w2[1, 1]
    return out.reshape(128, 5 * 16 * 4)


def stage_dn_tile_coefs(dn_w):
    base = stage_tile_coefs(dn_w, transpose=True).reshape(128, 5, 16, 4)
    rev = base[:, ::-1]
    return np.ascontiguousarray(rev.reshape(128, 5 * 16 * 4))


def stage_host(inputs, n_cores=8):
    f32 = np.float32
    x = np.ascontiguousarray(inputs["L0_attn_x"], dtype=f32)
    shared = {
        "wqT": np.ascontiguousarray(inputs["L0_attn_Wq"].T, f32),
        "wkT": np.ascontiguousarray(inputs["L0_attn_Wk"].T, f32),
        "wvT": np.ascontiguousarray(inputs["L0_attn_Wv"].T, f32),
        "woT": np.ascontiguousarray(inputs["L0_attn_Wo"].T, f32),
        "bq_f": np.ascontiguousarray(np.asarray(inputs["L0_attn_bq"], f32).reshape(NF, 128).T),
        "bk_f": np.ascontiguousarray(np.asarray(inputs["L0_attn_bk"], f32).reshape(NF, 128).T),
        "bv_r": np.asarray(inputs["L0_attn_bv"], f32).reshape(1, D),
        "bo_r": np.asarray(inputs["L0_attn_bo"], f32).reshape(1, D),
        "wrT": np.ascontiguousarray(np.asarray(inputs["L0_ffn_w_router"], f32).T),
        "w1": np.ascontiguousarray(inputs["L0_ffn_w1"], f32),
        "w3": np.ascontiguousarray(inputs["L0_ffn_w3"], f32),
        "w2": np.ascontiguousarray(inputs["L0_ffn_w2"], f32),
        "wQ": np.ascontiguousarray(inputs["L1_attn_W_Q"], f32),
        "wK": np.ascontiguousarray(inputs["L1_attn_W_K"], f32),
        "wV": np.ascontiguousarray(inputs["L1_attn_W_V"], f32),
        "wO": np.ascontiguousarray(inputs["L1_attn_W_o"], f32),
        "wexpT": np.ascontiguousarray(np.asarray(inputs["L1_ffn_W_exp"], f32).T),
        "bexp_f": np.ascontiguousarray(np.asarray(inputs["L1_ffn_b_exp"], f32).reshape(NHU, 128).T),
        "upbias_f": np.ascontiguousarray(np.asarray(inputs["L1_ffn_up_bias"], f32).reshape(NHU, 128).T),
        "upA": stage_up_blocks(np.asarray(inputs["L1_ffn_up_weights"], f32)),
        "upC": stage_tile_coefs(np.asarray(inputs["L1_ffn_up_weights"], f32), transpose=False),
        "dnC": stage_dn_blocks(np.asarray(inputs["L1_ffn_down_weights"], f32)),
        "dnCoef": stage_dn_tile_coefs(np.asarray(inputs["L1_ffn_down_weights"], f32)),
        "wcon": np.ascontiguousarray(inputs["L1_ffn_W_con"], f32),
        "bcon_r": np.asarray(inputs["L1_ffn_b_con"], f32).reshape(1, D),
        "dbias_r": np.asarray(inputs["L1_ffn_down_bias"], f32).reshape(1, D),
        "ident": np.eye(128, dtype=BF),
        "ident32": np.eye(128, dtype=f32),
        "tri16": np.ascontiguousarray(np.tile(np.tril(np.ones((128, 128), f32)).T, (1, 16)).astype(BF)),
        "ones_c": np.ones((128, 1), dtype=BF),
        "ones_r": np.ones((1, 128), dtype=f32),
    }
    in_maps = []
    bpc = 16 // n_cores
    for c in range(n_cores):
        m = dict(shared)
        m["x"] = np.ascontiguousarray(x[c * bpc:(c + 1) * bpc])
        in_maps.append(m)
    return in_maps


class Builder:
    SEMS = ("qs", "qp", "qa", "pe", "dv", "ac")

    def __init__(self, S=2048, n_seq=2, phases=("W", "AB", "C", "D", "E"), expose=False):
        self.S = S
        self.NCH = S // 128
        self.n_seq = n_seq
        self.phases = phases
        self.expose = expose
        nc = self.nc = bass.Bass("TRN2", target_bir_lowering=False)
        self.stack = ExitStack()
        self._declare_dram()
        self.sems = {n: self.stack.enter_context(nc.semaphore(n)) for n in self.SEMS}
        self.semc = defaultdict(int)
        self.engines = {}

    def _declare_dram(self):
        nc, S = self.nc, self.S
        t = lambda n, sh, dt, kind: nc.dram_tensor(n, sh, dt, kind=kind).ap()
        I, O, N = "ExternalInput", "ExternalOutput", "Internal"
        if self.expose:
            N = O
        d = self.d = {}
        d["x"] = t("x", [self.n_seq, S, D], F32, I)
        for n in ("wqT", "wkT", "wvT", "woT", "wQ", "wK", "wV", "wO"):
            d[n] = t(n, [D, D], F32, I)
        d["bq_f"] = t("bq_f", [128, NF], F32, I)
        d["bk_f"] = t("bk_f", [128, NF], F32, I)
        for n in ("bv_r", "bo_r", "bcon_r", "dbias_r"):
            d[n] = t(n, [1, D], F32, I)
        d["wrT"] = t("wrT", [D, EXP], F32, I)
        d["w1"] = t("w1", [EXP, D, MH], F32, I)
        d["w3"] = t("w3", [EXP, D, MH], F32, I)
        d["w2"] = t("w2", [EXP, MH, D], F32, I)
        d["wexpT"] = t("wexpT", [DH, D], F32, I)
        d["wcon"] = t("wcon", [DH, D], F32, I)
        d["bexp_f"] = t("bexp_f", [128, NHU], F32, I)
        d["upbias_f"] = t("upbias_f", [128, NHU], F32, I)
        d["upA"] = t("upA", [32, 7, 128, 128], F32, I)
        d["dnC"] = t("dnC", [32, 7, 128, 128], F32, I)
        d["upC"] = t("upC", [128, 320], F32, I)
        d["dnCoef"] = t("dnCoef", [128, 320], F32, I)
        d["ident"] = t("ident", [128, 128], BF16, I)
        d["ident32"] = t("ident32", [128, 128], F32, I)
        d["tri16"] = t("tri16", [128, 128 * 16], BF16, I)
        d["ones_c"] = t("ones_c", [128, 1], BF16, I)
        d["ones_r"] = t("ones_r", [1, 128], F32, I)
        d["acc_d"] = t("acc_d", [self.n_seq, S, D], F32, N)
        d["w1b_d"] = t("w1b_d", [EXP, D, MH], BF16, N)
        d["w3b_d"] = t("w3b_d", [EXP, D, MH], BF16, N)
        d["w2b_d"] = t("w2b_d", [EXP, MH, D], BF16, N)
        d["wup_d"] = t("wup_d", [D, DH], BF16, N)
        d["wdown_d"] = t("wdown_d", [DH, D], BF16, N)
        d["bup_d"] = t("bup_d", [128, NHU], F32, N)
        d["bias_bc_d"] = t("bias_bc_d", [3, 128, D], F32, N)
        d["xnT_d"] = t("xnT_d", [self.n_seq, D, S], BF16, N)
        d["gates_d"] = t("gates_d", [self.n_seq, 128, self.NCH * EXP], F32, N)
        d["partial_d"] = t("partial_d", [self.n_seq, S, D], F32, N)
        d["x1_d"] = t("x1_d", [self.n_seq, S, D], F32, N)
        d["xn1T_d"] = t("xn1T_d", [self.n_seq, D, S], BF16, N)
        d["partial1_d"] = t("partial1_d", [self.n_seq, S, D], F32, N)
        d["y"] = t("y", [self.n_seq, S, D], F32, "ExternalOutput")

    _uid = 0

    def sbuf(self, stk, name, shape, dtype):
        Builder._uid += 1
        return stk.enter_context(self.nc.sbuf_tensor(f"{name}_{Builder._uid}", shape, dtype))

    def psum(self, stk, name, shape, dtype=F32):
        Builder._uid += 1
        return stk.enter_context(self.nc.psum_tensor(f"{name}_{Builder._uid}", shape, dtype))

    def phase_W(self):
        nc, d = self.nc, self.d
        P = Prog(self.semc)
        stk = ExitStack()
        g = self.g = {}
        g["ident"] = self.sbuf(self.stack, "c_ident", [128, 128], BF16)
        g["ident32"] = self.sbuf(self.stack, "c_ident32", [128, 128], F32)
        g["tri16"] = self.sbuf(self.stack, "c_tri16", [128, 2048], BF16)
        g["ones_c"] = self.sbuf(self.stack, "c_ones_c", [128, 1], BF16)
        g["ones_r"] = self.sbuf(self.stack, "c_ones_r", [1, 128], F32)
        g["bq_f"] = self.sbuf(self.stack, "c_bq_f", [128, NF], F32)
        g["bk_f"] = self.sbuf(self.stack, "c_bk_f", [128, NF], F32)
        g["bv_bc"] = self.sbuf(self.stack, "c_bv_bc", [128, D], F32)
        g["bo_bc"] = self.sbuf(self.stack, "c_bo_bc", [128, D], F32)
        g["bcd_bc"] = self.sbuf(self.stack, "c_bcd_bc", [128, D], F32)
        g["eps_ln"] = self.sbuf(self.stack, "c_eps_ln", [128, 1], F32)

        P.op("vector", lambda e: e.memset(g["eps_ln"][:], LN_EPS), inc=["dv"])
        for n in ("ident32", "ident", "tri16", "ones_c", "ones_r", "bq_f", "bk_f"):
            src = d[n][:, 0:2048] if n == "tri16" else d[n][:]
            P.dma("qs", lambda e, a=g[n], b=src: e.dma_start(a[:], b))
        cvstk = ExitStack()
        rows = self.sbuf(cvstk, "w_rows", [1, 4 * D], F32)
        for i, n in enumerate(("bv_r", "bo_r", "bcon_r", "dbias_r")):
            P.dma("qs", lambda e, i=i, n=n: e.dma_start(rows[0:1, i * D:(i + 1) * D], d[n][:]))
        pbc = self.psum(stk, "w_pbc", [128, 512], F32)
        qs_done = P.n("qs")
        for dst, ri in ((g["bv_bc"], 0), (g["bo_bc"], 1)):
            for h in range(2):
                P.op("tensor", lambda e, ri=ri, h=h: e.matmul(
                    pbc[:], g["ones_r"][:], rows[0:1, ri * D + h * 512: ri * D + (h + 1) * 512],
                    start=True, stop=True), wait=[("qs", qs_done)], inc=["pe"])
                P.op("vector", lambda e, dst=dst, h=h: e.tensor_copy(dst[:, h * 512:(h + 1) * 512], pbc[:]),
                     wait=["pe"], inc=["dv"])
                P.op("tensor", lambda e: e.nop(), wait=["dv"])
        for h in range(2):
            P.op("tensor", lambda e, h=h: e.matmul(
                pbc[:], g["ones_r"][:], rows[0:1, 2 * D + h * 512: 2 * D + (h + 1) * 512],
                start=True, stop=True), wait=[("qs", qs_done)], inc=["pe"])
            P.op("vector", lambda e, h=h: e.tensor_copy(g["bcd_bc"][:, h * 512:(h + 1) * 512], pbc[:]),
                 wait=["pe"], inc=["dv"])
            P.op("tensor", lambda e: e.nop(), wait=["dv"])
            P.op("tensor", lambda e, h=h: e.matmul(
                pbc[:], g["ones_r"][:], rows[0:1, 3 * D + h * 512: 3 * D + (h + 1) * 512],
                start=True, stop=True), wait=[("qs", qs_done)], inc=["pe"])
            P.op("vector", lambda e, h=h: e.tensor_tensor(
                g["bcd_bc"][:, h * 512:(h + 1) * 512], g["bcd_bc"][:, h * 512:(h + 1) * 512],
                pbc[:], ALU.add), wait=["pe", "dv"], inc=["dv"])
            P.op("tensor", lambda e: e.nop(), wait=["dv"])

        NW = NF * MH
        cstg = self.sbuf(cvstk, "wc_stg", [128, 2 * NW], F32)
        cbuf = self.sbuf(cvstk, "wc_buf", [128, 2 * NW], BF16)
        conv_jobs = []
        for e_ in range(EXP):
            conv_jobs += [(d["w1"], d["w1b_d"], e_, NF, MH), (d["w3"], d["w3b_d"], e_, NF, MH),
                          (d["w2"], d["w2b_d"], e_, NHT, D)]
        for ji, (srcw, dstw, e_, nch, w) in enumerate(conv_jobs):
            pp = ji % 2
            n = nch * w
            strm = "qs" if ji % 2 == 0 else "qa"
            P.dma(strm, lambda e, srcw=srcw, e_=e_, nch=nch, n=n, pp=pp: e.dma_start(
                cstg[:, pp * NW: pp * NW + n].rearrange("p (c x) -> p c x", c=nch),
                srcw[e_].rearrange("(c p) x -> p c x", p=128)),
                war=(["dv"] if ji % 2 == 0 else ["ac"]))
            if ji % 2 == 0:
                P.op("vector", lambda e, n=n, pp=pp: e.tensor_copy(
                    cbuf[:, pp * NW: pp * NW + n], cstg[:, pp * NW: pp * NW + n]),
                    wait=["qs", ("qp", P.n("qp"))], inc=["dv"])
            else:
                P.op("scalar", lambda e, n=n, pp=pp: e.activation(
                    cbuf[:, pp * NW: pp * NW + n], cstg[:, pp * NW: pp * NW + n],
                    AF.Copy), wait=["qa", ("qp", P.n("qp"))], inc=["ac"])
            P.dma("qp", lambda e, dstw=dstw, e_=e_, nch=nch, n=n, pp=pp: e.dma_start(
                dstw[e_].rearrange("(c p) x -> p c x", p=128),
                cbuf[:, pp * NW: pp * NW + n].rearrange("p (c x) -> p c x", c=nch)),
                war=["dv", "ac"])
        with self.nc.Block() as blk:
            P.emit_into_block(blk, None, self.sems)
        cvstk.close()
        P = Prog(self.semc)
        wt = self.sbuf(stk, "w_wt", [128, NHU * D], F32)
        coef = self.sbuf(stk, "w_coef", [128, 320], F32)
        blkbuf = self.sbuf(stk, "w_blk", [128, 2 * 7 * 128], F32)
        comp = self.sbuf(stk, "w_comp", [128, 32 * 128], F32)
        scr0 = self.sbuf(stk, "w_scr0", [128, D], F32)
        scr1 = self.sbuf(stk, "w_scr1", [128, D], F32)
        scr0b = self.sbuf(stk, "w_scr0b", [128, D], F32)
        scr1b = self.sbuf(stk, "w_scr1b", [128, D], F32)
        bvec = self.sbuf(stk, "w_bvec", [128, NHU], F32)
        bscr = self.sbuf(stk, "w_bscr", [128, 2], F32)
        rowb = self.sbuf(stk, "w_rowb", [128, DH], BF16)
        tb16 = self.sbuf(stk, "w_tb16", [128, 128], BF16)
        pcmp = self.psum(stk, "w_pcmp", [128, 128], F32)
        pcmp2 = self.psum(stk, "w_pcmp2", [128, 128], F32)
        papp = self.psum(stk, "w_papp", [128, 512], F32)
        papp2 = self.psum(stk, "w_papp2", [128, 512], F32)
        pbv = self.psum(stk, "w_pbv", [128, 1], F32)
        ptr = self.psum(stk, "w_ptr", [128, 128], BF16)
        ptr2 = self.psum(stk, "w_ptr2", [128, 128], BF16)

        def compose_blocks(src_ap, order_rev):
            seed_k = 6 if order_rev else 0
            ks = list(range(5, -1, -1) if order_rev else range(1, 7))
            dmad = {}
            for pair in range(16):
                for half in range(2):
                    blk = pair * 2 + half
                    pp = blk % 2
                    bb = blkbuf[:, pp * 7 * 128:(pp + 1) * 7 * 128]
                    P.dma("qs", lambda e, blk=blk, bb=bb: e.dma_start(
                        bb[:].rearrange("p (k n) -> p k n", k=7), src_ap[blk]),
                        war=["pe", "dv"])
                    dmad[blk] = P.n("qs")
                st = {}
                for half in range(2):
                    blk = pair * 2 + half
                    bb = blkbuf[:, (blk % 2) * 7 * 128:((blk % 2) + 1) * 7 * 128]
                    P.op("vector", lambda e, bb=bb, blk=blk: e.tensor_copy(
                        comp[:, blk * 128:(blk + 1) * 128],
                        bb[:, seed_k * 128:(seed_k + 1) * 128]),
                        wait=[("qs", dmad[blk])], inc=["dv"])
                    st[half] = {"dv": P.n("dv"), "pe": 0}
                for k in ks:
                    for half in range(2):
                        blk = pair * 2 + half
                        bb = blkbuf[:, (blk % 2) * 7 * 128:((blk % 2) + 1) * 7 * 128]
                        pc = pcmp if half == 0 else pcmp2
                        P.op("tensor", lambda e, bb=bb, blk=blk, k=k, pc=pc: e.matmul(
                            pc[:], bb[:, k * 128:(k + 1) * 128],
                            comp[:, blk * 128:(blk + 1) * 128],
                            start=True, stop=True),
                            wait=[("dv", st[half]["dv"])], inc=["pe"])
                        st[half]["pe"] = P.n("pe")
                    for half in range(2):
                        blk = pair * 2 + half
                        pc = pcmp if half == 0 else pcmp2
                        P.op("vector", lambda e, blk=blk, pc=pc: e.tensor_copy(
                            comp[:, blk * 128:(blk + 1) * 128], pc[:]),
                            wait=[("pe", st[half]["pe"])], inc=["dv"])
                        st[half]["dv"] = P.n("dv")

        def apply_blocks():
            for blk in range(32):
                base = blk * D
                for h, pp in ((0, papp), (1, papp2)):
                    P.op("tensor", lambda e, blk=blk, base=base, h=h, pp=pp: e.matmul(
                        pp[:], comp[:, blk * 128:(blk + 1) * 128],
                        wt[:, base + h * 512: base + (h + 1) * 512], start=True, stop=True),
                        wait=["dv"], inc=["pe"])
                for h, pp in ((0, papp), (1, papp2)):
                    P.op("vector", lambda e, base=base, h=h, pp=pp: e.tensor_copy(
                        wt[:, base + h * 512: base + (h + 1) * 512], pp[:]),
                        wait=["pe"], inc=["dv"])
                P.op("tensor", lambda e: e.nop(), wait=["dv"])

        def apply_blocks_bvec():
            for blk in range(32):
                P.op("tensor", lambda e, blk=blk: e.matmul(
                    pbv[:], comp[:, blk * 128:(blk + 1) * 128], bvec[:, blk:blk + 1],
                    start=True, stop=True), wait=["dv"], inc=["pe"])
                P.op("vector", lambda e, blk=blk: e.tensor_copy(bvec[:, blk:blk + 1], pbv[:]),
                     wait=["pe"], inc=["dv"])
                P.op("tensor", lambda e: e.nop(), wait=["dv"])

        def vector_stages(with_bvec):
            scrs = [(scr0, scr1), (scr0b, scr1b)]
            dv_after = {}
            gi = 0
            for q in range(5):
                bit = 1 << q
                atiles = [t for t in range(32) if not (t & bit)]
                for pi, Ta in enumerate(atiles):
                    Tb = Ta + bit
                    cbase = (q * 16 + pi) * 4
                    ca0 = coef[:, cbase + 0:cbase + 1]
                    cb0 = coef[:, cbase + 1:cbase + 2]
                    ca1 = coef[:, cbase + 2:cbase + 3]
                    cb1 = coef[:, cbase + 3:cbase + 4]
                    a = wt[:, Ta * D:(Ta + 1) * D]
                    b = wt[:, Tb * D:(Tb + 1) * D]
                    s0, s1_ = scrs[gi % 2]
                    if pi == 0:
                        wv = ["dv"]
                    else:
                        wv = [("dv", dv_after[gi - 2])] if gi >= 2 else []
                    P.op("scalar", lambda e, b=b, cb0=cb0, s0=s0: e.activation(
                        s0[:], b, AF.Copy, scale=cb0), wait=wv, inc=["ac"])
                    P.op("scalar", lambda e, a=a, ca1=ca1, s1_=s1_: e.activation(
                        s1_[:], a, AF.Copy, scale=ca1), inc=["ac"])
                    ac_now = P.n("ac")
                    P.op("vector", lambda e, a=a, ca0=ca0, s0=s0: e.scalar_tensor_tensor(
                        a, a, ca0, s0[:], ALU.mult, ALU.add), wait=[("ac", ac_now)], inc=["dv"])
                    P.op("vector", lambda e, b=b, cb1=cb1, s1_=s1_: e.scalar_tensor_tensor(
                        b, b, cb1, s1_[:], ALU.mult, ALU.add), inc=["dv"])
                    dv_after[gi] = P.n("dv")
                    gi += 1
            if with_bvec:
                for q in range(5):
                    bit = 1 << q
                    atiles = [t for t in range(32) if not (t & bit)]
                    for pi, Ta in enumerate(atiles):
                        Tb = Ta + bit
                        cbase = (q * 16 + pi) * 4
                        ca0 = coef[:, cbase + 0:cbase + 1]
                        cb0 = coef[:, cbase + 1:cbase + 2]
                        ca1 = coef[:, cbase + 2:cbase + 3]
                        cb1 = coef[:, cbase + 3:cbase + 4]
                        va = bvec[:, Ta:Ta + 1]
                        vb = bvec[:, Tb:Tb + 1]
                        P.op("scalar", lambda e, vb=vb, cb0=cb0: e.activation(
                            bscr[:, 0:1], vb, AF.Copy, scale=cb0), wait=["dv"], inc=["ac"])
                        P.op("scalar", lambda e, va=va, ca1=ca1: e.activation(
                            bscr[:, 1:2], va, AF.Copy, scale=ca1), inc=["ac"])
                        P.op("vector", lambda e, va=va, ca0=ca0: e.scalar_tensor_tensor(
                            va, va, ca0, bscr[:, 0:1], ALU.mult, ALU.add), wait=["ac"], inc=["dv"])
                        P.op("vector", lambda e, vb=vb, cb1=cb1: e.scalar_tensor_tensor(
                            vb, vb, cb1, bscr[:, 1:2], ALU.mult, ALU.add), inc=["dv"])
                        P.op("scalar", lambda e: e.nop(), wait=["dv"])

        P.dma("qs", lambda e: e.dma_start(
            wt[:].rearrange("p (t d) -> p t d", t=NHU), d["wexpT"].rearrange("(t p) d -> p t d", p=128)),
            war=["dv", "qp"])
        P.dma("qs", lambda e: e.dma_start(coef[:], d["upC"][:]))
        P.dma("qs", lambda e: e.dma_start(bvec[:], d["bexp_f"][:]))
        wt_loaded = P.n("qs")
        P.op("vector", lambda e: e.tensor_copy(scr0[:, 0:1], wt[:, 0:1]),
             wait=[("qs", wt_loaded)], inc=["dv"])
        compose_blocks(d["upA"], order_rev=True)
        apply_blocks()
        apply_blocks_bvec()
        vector_stages(with_bvec=True)
        ub = self.sbuf(stk, "w_ub", [128, NHU], F32)
        P.dma("qs", lambda e: e.dma_start(ub[:], d["upbias_f"][:]))
        P.op("vector", lambda e: e.tensor_tensor(bvec[:], bvec[:], ub[:], ALU.add),
             wait=["qs", "dv"], inc=["dv"])
        P.dma("qp", lambda e: e.dma_start(d["bup_d"][:], bvec[:]), war=["dv"])
        for dc in range(NF):
            for t32 in range(NHU):
                pp = t32 % 2
                ptr_ = ptr if pp == 0 else ptr2
                P.op("vector", lambda e, dc=dc, t32=t32: e.tensor_copy(
                    tb16[:], wt[:, t32 * D + dc * 128: t32 * D + (dc + 1) * 128]),
                    wait=["dv", "pe"], inc=["dv"])
                P.op("tensor", lambda e, ptr_=ptr_: e.transpose(ptr_[:], tb16[:], g["ident"][:]),
                     wait=["dv"], inc=["pe"])
                P.op("vector", lambda e, t32=t32, ptr_=ptr_: e.tensor_copy(
                    rowb[:, t32 * 128:(t32 + 1) * 128], ptr_[:]), wait=["pe"], inc=["dv"])
            P.dma("qp", lambda e, dc=dc: e.dma_start(
                d["wup_d"][dc * 128:(dc + 1) * 128, :], rowb[:]), war=["dv"])
            P.op("vector", lambda e: e.tensor_copy(scr0[:, 0:1], scr0[:, 0:1]),
                 wait=[("qp", P.n("qp"))], inc=["dv"])

        P.dma("qs", lambda e: e.dma_start(
            wt[:].rearrange("p (t d) -> p t d", t=NHU), d["wcon"].rearrange("(t p) d -> p t d", p=128)),
            war=["dv"])
        P.dma("qs", lambda e: e.dma_start(coef[:], d["dnCoef"][:]))
        wt2_loaded = P.n("qs")
        P.op("vector", lambda e: e.tensor_copy(scr0[:, 0:1], wt[:, 0:1]),
             wait=[("qs", wt2_loaded)], inc=["dv"])
        vector_stages(with_bvec=False)
        compose_blocks(d["dnC"], order_rev=False)
        apply_blocks()
        for t32 in range(NHU):
            P.op("vector", lambda e, t32=t32: e.tensor_copy(
                rowb[:, 0:D], wt[:, t32 * D:(t32 + 1) * D]),
                wait=["dv", ("qp", P.n("qp"))], inc=["dv"])
            P.dma("qp", lambda e, t32=t32: e.dma_start(
                d["wdown_d"][t32 * 128:(t32 + 1) * 128, :], rowb[:, 0:D]), war=["dv"])
        for i, n in enumerate(("bv_bc", "bo_bc", "bcd_bc")):
            P.dma("qp", lambda e, i=i, n=n: e.dma_start(d["bias_bc_d"][i], g[n][:]), war=["dv"])
        P.op("gpsimd", lambda e: e.nop(), wait=[("qp", P.n("qp"))])

        with self.nc.Block() as blk:
            P.emit_into_block(blk, None, self.sems)
        stk.close()

    def phase_AB(self):
        for b in range(self.n_seq):
            self._ab_seq(b)

    AB_PARTS = ("A", "B1", "B2")

    def _ab_seq(self, b):
        nc, d, g = self.nc, self.d, self.g
        S, NCH = self.S, self.NCH
        NG = S // 512
        sstk = ExitStack()
        qT = self.sbuf(sstk, "ab_qT", [128, NF * S], BF16)
        kT = self.sbuf(sstk, "ab_kT", [128, NF * S], BF16)
        vx = self.sbuf(sstk, "ab_vx", [128, NCH * H * 65], BF16)
        gates = self.sbuf(sstk, "ab_gates", [128, NCH * EXP], F32)

        astk = ExitStack()
        wq = self.sbuf(astk, "ab_wq", [128, NF * D], BF16)
        wk = self.sbuf(astk, "ab_wk", [128, NF * D], BF16)
        wv = self.sbuf(astk, "ab_wv", [128, NF * D], BF16)
        wr = self.sbuf(astk, "ab_wr", [128, NF * EXP], F32)
        stg = self.sbuf(astk, "ab_stg", [128, 2 * D], F32)
        xt = self.sbuf(astk, "a_xt", [128, 2 * D], F32)
        xnT = self.sbuf(astk, "a_xnT", [128, NF * 512], BF16)
        xnT32 = self.sbuf(astk, "a_xnT32", [128, NF * 128], F32)
        xnb = self.sbuf(astk, "a_xnb2", [128, 2 * D], F32)
        scr2 = self.sbuf(astk, "a_scr2", [128, 512], F32)
        scr3 = self.sbuf(astk, "a_scr3", [128, 512], F32)
        st8 = self.sbuf(astk, "a_st8", [128, 8], F32)
        pr8 = self.sbuf(astk, "a_pr8", [128, 8], F32)
        ptA = self.psum(astk, "a_ptA", [128, 128], F32)
        ptB = self.psum(astk, "a_ptB", [128, 128], F32)
        pq = self.psum(astk, "a_pq", [128, 512], F32)
        pq2 = self.psum(astk, "a_pq2", [128, 512], F32)
        pr = self.psum(astk, "a_pr", [128, 8], F32)

        P = Prog(self.semc)
        for wsb, srcw in ((wq, d["wqT"]), (wk, d["wkT"]), (wv, d["wvT"])):
            for fi in range(NF):
                pp = fi % 2
                P.dma("qs", lambda e, fi=fi, pp=pp, srcw=srcw: e.dma_start(
                    stg[:, pp * D:(pp + 1) * D], srcw[fi * 128:(fi + 1) * 128, :]),
                    war=["dv"])
                P.op("vector", lambda e, fi=fi, pp=pp, wsb=wsb: e.tensor_copy(
                    wsb[:, fi * D:(fi + 1) * D], stg[:, pp * D:(pp + 1) * D]),
                    wait=["qs"], inc=["dv"])
        P.dma("qs", lambda e: e.dma_start(wr[:].rearrange(
            "p (f x) -> p f x", f=NF), d["wrT"].rearrange("(f p) x -> p f x", p=128)),
            war=["dv"])
        wr_loaded = P.n("qs")
        pq_free = {}
        SUM, SQ, MEAN, VAR, RSTD, NMR = 0, 1, 2, 3, 4, 5
        RSUM, RM1, RM2 = 12, 13, 14
        st8x = self.sbuf(astk, "a_st8x", [128, 16], F32)

        def stats(ch):
            par = (ch % 2) * 16
            pp = ch % 2
            xt_c = xt[:, pp * D:(pp + 1) * D]
            xnb_c = xnb[:, pp * D:(pp + 1) * D]
            C = lambda i: st8x[:, (ch % 2) * 6 + i:(ch % 2) * 6 + i + 1] if False else None
            base = (ch % 2) * 6
            sl = lambda i: st8x[:, base + i: base + i + 1]
            P.dma("qs", lambda e, b=b, ch=ch, xt_c=xt_c: e.dma_start(
                xt_c, d["x"][b, ch * 128:(ch + 1) * 128, :]),
                war=[("ac", ln_free.get(ch - 2, (0, 0))[0]),
                     ("dv", ln_free.get(ch - 2, (0, 0))[1])])
            qs_x = P.n("qs")
            P.op("scalar", lambda e, xt_c=xt_c, xnb_c=xnb_c: e.activation(
                xnb_c, xt_c, AF.Square, accum_out=sl(SQ)),
                wait=[("qs", qs_x), ("pe", trans_free.get(ch - 2, 0))], inc=["ac"])
            P.op("vector", lambda e, xt_c=xt_c: e.reduce_sum(
                sl(SUM), xt_c, axis=AX.X), wait=[("qs", qs_x)], inc=["dv"])
            P.op("vector", lambda e: e.tensor_scalar(
                sl(MEAN), sl(SUM), 1.0 / D, None, ALU.mult), inc=["dv"])
            P.op("vector", lambda e: e.tensor_scalar(
                sl(NMR), sl(MEAN), -1.0 / D, None, ALU.mult), inc=["dv"])
            P.op("vector", lambda e: e.tensor_scalar(
                sl(VAR), sl(SQ), 1.0 / D, None, ALU.mult), wait=["ac"], inc=["dv"])
            P.op("vector", lambda e: e.scalar_tensor_tensor(
                sl(VAR), sl(SUM), sl(NMR), sl(VAR), ALU.mult, ALU.add), inc=["dv"])
            dv_v = P.n("dv")
            P.op("scalar", lambda e: e.activation(
                sl(RSTD), sl(VAR), AF.Ln, bias=g["eps_ln"][:]),
                wait=[("dv", dv_v)], inc=["ac"])
            P.op("scalar", lambda e: e.activation(
                sl(RSTD), sl(RSTD), AF.Exp, scale=-0.5), inc=["ac"])
            P.op("vector", lambda e: e.tensor_scalar(
                sl(NMR), sl(MEAN), -1.0, None, ALU.mult), inc=["dv"])
            return (P.n("ac"), P.n("dv"))

        ln_free = {}
        trans_free = {}
        stat_done = {}

        def transform(ch, gg, ct):
            pp = ch % 2
            base = (ch % 2) * 6
            sl = lambda i: st8x[:, base + i: base + i + 1]
            xt_c = xt[:, pp * D:(pp + 1) * D]
            xnb_c = xnb[:, pp * D:(pp + 1) * D]
            ac_v, dv_v = stat_done[ch]
            P.op("vector", lambda e, xt_c=xt_c, xnb_c=xnb_c: e.tensor_scalar(
                xnb_c, xt_c, sl(NMR), sl(RSTD), ALU.add, ALU.mult),
                wait=[("ac", ac_v), ("dv", dv_v), "pe"], inc=["dv"])
            xn_dv = P.n("dv")
            cp_dv = {0: xn_dv, 1: xn_dv}
            for f in range(NF):
                pt = ptA if f % 2 == 0 else ptB
                P.op("tensor", lambda e, f=f, pt=pt, xnb_c=xnb_c: e.transpose(
                    pt[:], xnb_c[:, f * 128:(f + 1) * 128], g["ident32"][:]),
                    wait=[("dv", cp_dv[f % 2])], inc=["pe"])
                P.op("vector", lambda e, f=f, ct=ct, pt=pt: e.tensor_copy(
                    xnT[:, f * 512 + ct * 128: f * 512 + (ct + 1) * 128], pt[:]),
                    wait=["pe", "qp"], inc=["dv"])
                P.op("vector", lambda e, f=f, pt=pt: e.tensor_copy(
                    xnT32[:, f * 128:(f + 1) * 128], pt[:]), inc=["dv"])
                cp_dv[f % 2] = P.n("dv")
            trans_free[ch] = P.n("pe")
            xnT32_dv = P.n("dv")
            for fi in range(NF):
                P.op("tensor", lambda e, fi=fi: e.matmul(
                    pr[:], xnT32[:, fi * 128:(fi + 1) * 128],
                    wr[:, fi * EXP:(fi + 1) * EXP],
                    start=(fi == 0), stop=(fi == NF - 1)),
                    wait=([("dv", max(xnT32_dv, rt_free[0])), ("qs", wr_loaded),
                           ("ac", rt_ac[0])] if fi == 0 else []),
                    inc=(["pe"] if fi == NF - 1 else []))
            pr_pe = P.n("pe")
            P.op("vector", lambda e: e.reduce_max(
                st8x[:, RM1:RM1 + 1], pr[:], axis=AX.X, negate=True),
                wait=[("pe", pr_pe)], inc=["dv"])
            P.op("scalar", lambda e: e.activation(
                pr8[:], pr[:], AF.Exp, bias=st8x[:, RM1:RM1 + 1],
                accum_out=st8x[:, RSUM:RSUM + 1]), wait=["dv"], inc=["ac"])
            rt_ac[0] = P.n("ac")
            P.op("vector", lambda e: e.reciprocal(
                st8x[:, RM2:RM2 + 1], st8x[:, RSUM:RSUM + 1]), wait=["ac"], inc=["dv"])
            P.op("vector", lambda e: e.tensor_scalar(
                pr8[:], pr8[:], st8x[:, RM2:RM2 + 1], None, ALU.mult), inc=["dv"])
            P.op("vector", lambda e: e.reduce_max(
                st8x[:, RM1:RM1 + 1], pr8[:], axis=AX.X), inc=["dv"])
            P.op("vector", lambda e: e.tensor_scalar(
                scr2[:, 0:8], pr8[:], st8x[:, RM1:RM1 + 1], None, ALU.is_ge), inc=["dv"])
            P.op("vector", lambda e: e.scalar_tensor_tensor(
                scr2[:, 0:8], scr2[:, 0:8], -1e30, pr8[:], ALU.mult, ALU.add), inc=["dv"])
            P.op("vector", lambda e: e.reduce_max(
                st8x[:, RM2:RM2 + 1], scr2[:, 0:8], axis=AX.X), inc=["dv"])
            P.op("vector", lambda e, ch=ch: e.tensor_scalar(
                gates[:, ch * EXP:(ch + 1) * EXP], pr8[:], st8x[:, RM2:RM2 + 1], None,
                ALU.is_ge), inc=["dv"])
            P.op("vector", lambda e, ch=ch: e.tensor_mul(
                gates[:, ch * EXP:(ch + 1) * EXP], gates[:, ch * EXP:(ch + 1) * EXP],
                pr8[:]), inc=["dv"])
            rt_free[0] = P.n("dv")
            ln_free[ch] = (P.n("ac"), P.n("dv"))

        rt_free = {0: 0}
        rt_ac = {0: 0}
        stat_done[0] = stats(0)
        for gg in range(NG):
            for ct in range(4):
                ch = gg * 4 + ct
                if ch + 1 < NCH:
                    stat_done[ch + 1] = stats(ch + 1)
                transform(ch, gg, ct)
            t0 = gg * 512
            P.dma("qp", lambda e, b=b, t0=t0: e.dma_start(
                d["xnT_d"][b].rearrange("(f p) t -> p f t", p=128)[:, :, t0:t0 + 512],
                xnT[:].rearrange("p (f t) -> p f t", f=NF)), war=["dv"])
            for wsb, dstT, bias, isq in ((wq, qT, g["bq_f"], True), (wk, kT, g["bk_f"], True)):
                for fo in range(NF):
                    pqx = pq if fo % 2 == 0 else pq2
                    fa = pq_free.get(id(pqx), ["dv", "ac"])
                    for fi in range(NF):
                        P.op("tensor", lambda e, fo=fo, fi=fi, wsb=wsb, pqx=pqx: e.matmul(
                            pqx[:], wsb[:, fi * D + fo * 128: fi * D + (fo + 1) * 128],
                            xnT[:, fi * 512:(fi + 1) * 512],
                            start=(fi == 0), stop=(fi == NF - 1)),
                            wait=(fa if fi == 0 else []),
                            inc=(["pe"] if fi == NF - 1 else []))
                    P.op("vector", lambda e, fo=fo, bias=bias, pqx=pqx: e.tensor_scalar(
                        scr2[:], pqx[:], bias[:, fo:fo + 1], 0.0, ALU.add, ALU.min),
                        wait=["pe"], inc=["dv"])
                    P.op("vector", lambda e, fo=fo, bias=bias, pqx=pqx: e.tensor_scalar(
                        scr3[:], pqx[:], bias[:, fo:fo + 1], 0.0, ALU.add, ALU.max),
                        inc=["dv"])
                    pq_free[id(pqx)] = [("dv", P.n("dv"))]
                    P.op("scalar", lambda e: e.activation(
                        scr2[:], scr2[:], AF.Exp), wait=["dv"], inc=["ac"])
                    P.op("vector", lambda e, fo=fo, t0=t0, dstT=dstT: e.tensor_add(
                        dstT[:, fo * S + t0: fo * S + t0 + 512], scr2[:], scr3[:]),
                        wait=["ac"], inc=["dv"])
            for ct in range(4):
                ch = gg * 4 + ct
                for oc in range(2):
                    pvx = pq if oc == 0 else pq2
                    fa = pq_free.get(id(pvx), ["dv", "ac"])
                    for fi in range(NF):
                        P.op("tensor", lambda e, ct=ct, oc=oc, fi=fi, pvx=pvx: e.matmul(
                            pvx[:], xnT[:, fi * 512 + ct * 128: fi * 512 + (ct + 1) * 128],
                            wv[:, fi * D + oc * 512: fi * D + (oc + 1) * 512],
                            start=(fi == 0), stop=(fi == NF - 1)),
                            wait=(fa if fi == 0 else []),
                            inc=(["pe"] if fi == NF - 1 else []))
                    vdst = vx[:, ch * H * 65 + oc * 8 * 65: ch * H * 65 + (oc + 1) * 8 * 65]
                    vdst = vdst.rearrange("p (a j) -> p a j", j=65)[:, :, 0:64]
                    P.op("vector", lambda e, oc=oc, vdst=vdst, pvx=pvx: e.scalar_tensor_tensor(
                        vdst, pvx[:].rearrange("p (a j) -> p a j", j=64), 1.0,
                        g["bv_bc"][:, oc * 512:(oc + 1) * 512].rearrange("p (a j) -> p a j", j=64),
                        ALU.mult, ALU.add), wait=["pe"], inc=["dv"])
                    pq_free[id(pvx)] = [("dv", P.n("dv"))]
        P.op("vector", lambda e: e.memset(
            vx[:].rearrange("p (c j) -> p c j", j=65)[:, :, 64:65], 1.0), inc=["dv"])
        P.dma("qp", lambda e, b=b: e.dma_start(d["gates_d"][b], gates[:]), war=["dv"])
        with nc.Block() as blk:
            P.emit_into_block(blk, None, self.sems)
        astk.close()
        o_s = self.sbuf(sstk, "ab_o", [128, NCH * D], BF16)
        if "B1" not in self.AB_PARTS:
            sstk.close()
            return

        bstk = ExitStack()
        atm = self.sbuf(bstk, "b_atm", [128, H * 128], BF16)
        ktok = self.sbuf(bstk, "b_ktok", [128, D], BF16)
        stS = self.sbuf(bstk, "b_stS", [128, 8 * 128], BF16)
        stZ = self.sbuf(bstk, "b_stZ", [128, 16], BF16)
        dsc = self.sbuf(bstk, "b_dsc", [128, 2 * H], F32)
        p_at = self.psum(bstk, "b_pat", [128, 2048], F32)
        p_stS = self.psum(bstk, "b_pstS", [128, 512], F32)
        p_stZ = self.psum(bstk, "b_pstZ", [128, 8], F32)
        p_tk = self.psum(bstk, "b_ptk", [128, 128], BF16)
        p_tk2 = self.psum(bstk, "b_ptk2", [128, 128], BF16)

        LV = getattr(self, "B1_LEVEL", 5)
        SLOT = [(h % 2) * 8 + (h // 2) for h in range(H)]
        P = Prog(self.semc)
        P.op("vector", lambda e: e.memset(p_stS[:], 0.0), inc=["dv"])
        P.op("vector", lambda e: e.memset(p_stZ[:], 0.0), inc=["dv"])
        P.op("vector", lambda e: e.memset(stS[:], 0.0), inc=["dv"])
        P.op("vector", lambda e: e.memset(stZ[:], 0.0), inc=["dv"])
        for ch in range(NCH):
            ts = ch * 128
            tk_free = {0: ("dv", P.n("dv")), 1: ("dv", P.n("dv"))}
            for f in range(NF):
                ptk = p_tk if f % 2 == 0 else p_tk2
                P.op("tensor", lambda e, f=f, ts=ts, ptk=ptk: e.transpose(
                    ptk[:], kT[:, f * S + ts: f * S + ts + 128], g["ident"][:]),
                    wait=([("ac", P.n("ac")), tk_free[f % 2]] if f <= 1 else [tk_free[f % 2]]),
                    inc=["pe"])
                P.op("vector", lambda e, f=f, ptk=ptk: e.tensor_copy(
                    ktok[:, f * 128:(f + 1) * 128], ptk[:]), wait=["pe"], inc=["dv"])
                tk_free[f % 2] = ("dv", P.n("dv"))
            if LV < 2:
                continue
            for sl in range(H):
                h = (sl // 8) + 2 * (sl % 8)
                f, po = h // 2, 64 * (h % 2)
                P.op("tensor", lambda e, sl=sl, f=f, po=po, ts=ts: e.matmul(
                    p_at[:, sl * 128:(sl + 1) * 128],
                    kT[po:po + 64, f * S + ts: f * S + ts + 128],
                    qT[po:po + 64, f * S + ts: f * S + ts + 128],
                    start=(sl % 4 == 0), stop=(sl % 4 == 3)),
                    wait=(["dv", "ac"] if sl == 0 else []), inc=(["pe"] if sl == H - 1 else []))
            for bk in range(4):
                P.op("vector", lambda e, bk=bk: e.tensor_tensor(
                    atm[:, bk * 512:(bk + 1) * 512], p_at[:, bk * 512:(bk + 1) * 512],
                    g["tri16"][:, bk * 512:(bk + 1) * 512], ALU.mult),
                    wait=(["pe"] if bk == 0 else []), inc=["dv"])
            if LV < 3:
                continue
            if LV >= 5 and ch > 0:
                sv = stS[:].rearrange("p (j c) -> p j c", c=128)
                P.op("vector", lambda e, sv=sv: e.tensor_copy(
                    sv[0:64, :, 0:64], p_stS[0:64, :].rearrange("p (j c) -> p j c", c=64)),
                    wait=["pe"], inc=["dv"])
                P.op("scalar", lambda e, sv=sv: e.activation(
                    sv[64:128, :, 64:128], p_stS[64:128, :].rearrange("p (j c) -> p j c", c=64),
                    AF.Copy), wait=["pe"], inc=["ac"])
                zv = stZ[:].rearrange("p (j c) -> p j c", c=2)
                P.op("vector", lambda e, zv=zv: e.tensor_copy(
                    zv[0:64, :, 0:1], p_stZ[0:64, :].rearrange("p (j c) -> p j c", c=1)),
                    inc=["dv"])
                P.op("scalar", lambda e, zv=zv: e.activation(
                    zv[64:128, :, 1:2], p_stZ[64:128, :].rearrange("p (j c) -> p j c", c=1),
                    AF.Copy), inc=["ac"])
            intra_stop = (ch == 0 or LV < 5)
            for h in range(H):
                asl = SLOT[h] * 128
                vsl = vx[:, ch * H * 65 + h * 65: ch * H * 65 + h * 65 + 64]
                P.op("tensor", lambda e, h=h, vsl=vsl, asl=asl, st_=intra_stop: e.matmul(
                    p_at[:, h * 64:(h + 1) * 64], atm[:, asl: asl + 128], vsl,
                    start=(h % 8 == 0), stop=(st_ and h % 8 == 7)),
                    wait=(["dv"] if h == 0 else []))
                P.op("tensor", lambda e, h=h, asl=asl, st_=intra_stop: e.matmul(
                    p_at[:, 2032 + h: 2033 + h], atm[:, asl: asl + 128],
                    g["ones_c"][:], start=(h == 0), stop=(st_ and h == H - 1)),
                    inc=(["pe"] if (intra_stop and h == H - 1) else []))
            if LV >= 5 and ch > 0:
                for j in range(8):
                    P.op("tensor", lambda e, j=j, ts=ts: e.matmul(
                        p_at[:, j * 128:(j + 1) * 128],
                        qT[:, j * S + ts: j * S + ts + 128],
                        stS[:, j * 128:(j + 1) * 128],
                        start=False, stop=(j % 4 == 3)),
                        wait=(["ac"] if j == 0 else []))
                    P.op("tensor", lambda e, j=j, ts=ts: e.matmul(
                        p_at[:, 2032 + 2 * j: 2034 + 2 * j],
                        qT[:, j * S + ts: j * S + ts + 128],
                        stZ[:, 2 * j: 2 * j + 2], start=False, stop=(j == 7)),
                        inc=(["pe"] if j == 7 else []))
            if LV < 4:
                pass
            for h in range(H) if LV >= 4 else []:
                vsl = vx[:, ch * H * 65 + h * 65: ch * H * 65 + h * 65 + 64]
                po, sc = 64 * (h % 2), 64 * (h // 2)
                P.op("tensor", lambda e, h=h, vsl=vsl, po=po, sc=sc: e.matmul(
                    p_stS[po:po + 64, sc:sc + 64], ktok[:, h * 64:(h + 1) * 64], vsl,
                    start=False, stop=False, skip_group_check=True),
                    wait=(["ac"] if h == 0 else []))
                P.op("tensor", lambda e, h=h, po=po: e.matmul(
                    p_stZ[po:po + 64, h // 2: h // 2 + 1], ktok[:, h * 64:(h + 1) * 64],
                    g["ones_c"][:], start=False, stop=False,
                    skip_group_check=True), inc=(["pe"] if h == H - 1 else []))
            dens = p_at[:, 2032:2048]
            P.op("vector", lambda e, dens=dens: e.tensor_scalar(
                dsc[:, 0:H], dens, ATT_EPS, None, ALU.add), wait=["pe"], inc=["dv"])
            P.op("vector", lambda e: e.reciprocal(dsc[:, H:2 * H], dsc[:, 0:H]), inc=["dv"])
            for h in range(H):
                P.op("scalar", lambda e, h=h, ch=ch: e.activation(
                    o_s[:, ch * D + h * 64: ch * D + (h + 1) * 64],
                    p_at[:, h * 64:(h + 1) * 64], AF.Copy,
                    scale=dsc[:, H + h:H + h + 1]), wait=["dv"], inc=["ac"])
        with nc.Block() as blk:
            P.emit_into_block(blk, None, self.sems)
        bstk.close()
        if "B2" not in self.AB_PARTS:
            sstk.close()
            return

        cstk = ExitStack()
        wo = self.sbuf(cstk, "b2_wo", [128, NF * D], BF16)
        stg2 = self.sbuf(cstk, "b2_stg", [128, 2 * D], F32)
        oT = self.sbuf(cstk, "b2_oT", [128, D], BF16)
        xt2 = self.sbuf(cstk, "b2_xt", [128, 2 * D], F32)
        pout = self.sbuf(cstk, "b2_pout", [128, 2 * D], F32)
        p_tr1 = self.psum(cstk, "b2_ptr1", [128, 128], BF16)
        p_tr2 = self.psum(cstk, "b2_ptr2", [128, 128], BF16)
        p_o = self.psum(cstk, "b2_po", [128, D], F32)

        P = Prog(self.semc)
        for fi in range(NF):
            pp = fi % 2
            P.dma("qs", lambda e, fi=fi, pp=pp: e.dma_start(
                stg2[:, pp * D:(pp + 1) * D], d["woT"][fi * 128:(fi + 1) * 128, :]),
                war=["dv"])
            P.op("vector", lambda e, fi=fi, pp=pp: e.tensor_copy(
                wo[:, fi * D:(fi + 1) * D], stg2[:, pp * D:(pp + 1) * D]),
                wait=["qs"], inc=["dv"])
        for ch in range(NCH):
            pp = ch % 2
            P.dma("qs", lambda e, b=b, ch=ch, pp=pp: e.dma_start(
                xt2[:, pp * D:(pp + 1) * D], d["x"][b, ch * 128:(ch + 1) * 128, :]),
                war=["dv"])
            qs_x = P.n("qs")
            for f in range(NF):
                pt = p_tr1 if f % 2 == 0 else p_tr2
                P.op("tensor", lambda e, f=f, ch=ch, pt=pt: e.transpose(
                    pt[:], o_s[:, ch * D + f * 128: ch * D + (f + 1) * 128], g["ident"][:]),
                    wait=["dv"], inc=["pe"])
                P.op("vector", lambda e, f=f, pt=pt: e.tensor_copy(
                    oT[:, f * 128:(f + 1) * 128], pt[:]), wait=["pe"], inc=["dv"])
            for dc in range(2):
                for fe in range(NF):
                    P.op("tensor", lambda e, dc=dc, fe=fe: e.matmul(
                        p_o[:, dc * 512:(dc + 1) * 512], oT[:, fe * 128:(fe + 1) * 128],
                        wo[:, fe * D + dc * 512: fe * D + (dc + 1) * 512],
                        start=(fe == 0), stop=(fe == NF - 1)),
                        wait=(["dv"] if fe == 0 else []),
                        inc=(["pe"] if fe == NF - 1 else []))
            P.op("vector", lambda e, pp=pp: e.scalar_tensor_tensor(
                pout[:, pp * D:(pp + 1) * D], p_o[:], 1.0, xt2[:, pp * D:(pp + 1) * D],
                ALU.mult, ALU.add), wait=["pe", ("qs", qs_x), "qp"], inc=["dv"])
            P.op("vector", lambda e, pp=pp: e.tensor_tensor(
                pout[:, pp * D:(pp + 1) * D], pout[:, pp * D:(pp + 1) * D],
                g["bo_bc"][:], ALU.add), inc=["dv"])
            P.op("tensor", lambda e: e.nop(), wait=["dv"])
            P.dma("qp", lambda e, b=b, ch=ch, pp=pp: e.dma_start(
                d["partial_d"][b, ch * 128:(ch + 1) * 128, :], pout[:, pp * D:(pp + 1) * D]),
                war=["dv"])
        P.op("gpsimd", lambda e: e.nop(), wait=["qp"])
        with nc.Block() as blk:
            P.emit_into_block(blk, None, self.sems)
        cstk.close()
        sstk.close()

    def phase_C(self):
        for b in range(self.n_seq):
            self._c_seq(b)

    def _c_seq(self, b):
        nc, d, g = self.nc, self.d, self.g
        S, NCH = self.S, self.NCH
        NTG = S // 256
        stk = ExitStack()
        gts = self.sbuf(stk, "c_gts", [128, NCH * EXP], F32)
        xnT = self.sbuf(stk, "c_xnT", [128, 2 * NF * 256], BF16)
        accs = self.sbuf(stk, "c_accs", [128, 2 * 2 * D], F32)
        w1sb = [self.sbuf(stk, "c_w1", [128, NF * MH], BF16) for _ in range(2)]
        w3sb = [self.sbuf(stk, "c_w3", [128, NF * MH], BF16) for _ in range(2)]
        w2sb = [self.sbuf(stk, "c_w2", [128, NHT * D], BF16) for _ in range(2)]
        s1 = self.sbuf(stk, "c_s1", [128, 2 * 256], F32)
        gsb = self.sbuf(stk, "c_gsb", [128, 2 * 256], BF16)
        xpt = self.sbuf(stk, "c_xpt", [128, 2 * D], F32)
        x1t = self.sbuf(stk, "c_x1t", [128, 2 * D], F32)
        p_h1 = [self.psum(stk, "c_ph1", [128, 256], F32) for _ in range(2)]
        p_h3 = [self.psum(stk, "c_ph3", [128, 256], F32) for _ in range(2)]
        p_y0 = self.psum(stk, "c_py0", [128, D], F32)
        p_y1 = self.psum(stk, "c_py1", [128, D], F32)

        P = Prog(self.semc)
        P.dma("qs", lambda e: e.dma_start(gts[:], d["gates_d"][b]), war=["dv", "pe"])
        qs_base = P.n("qs")
        xnT_v = xnT[:].rearrange("p (u f t) -> p u f t", u=2, f=NF)
        accs_v = accs[:].rearrange("p (u c x) -> p u c x", u=2, c=2)

        def load_weights(e_, war):
            up = e_ % 2
            P.dma("qs", lambda e, e_=e_, up=up: e.dma_start(
                w1sb[up][:].rearrange("p (f h) -> p f h", f=NF),
                d["w1b_d"][e_].rearrange("(f p) h -> p f h", p=128)), war=war)
            P.dma("qs", lambda e, e_=e_, up=up: e.dma_start(
                w3sb[up][:].rearrange("p (f h) -> p f h", f=NF),
                d["w3b_d"][e_].rearrange("(f p) h -> p f h", p=128)))
            P.dma("qs", lambda e, e_=e_, up=up: e.dma_start(
                w2sb[up][:].rearrange("p (i x) -> p i x", i=NHT),
                d["w2b_d"][e_].rearrange("(i p) x -> p i x", p=128)))
            return P.n("qs")

        def emit_up(e_, tg, ht, ready_qs, free_at, pp):
            up = e_ % 2
            for pi, (ph, wsb) in enumerate(((p_h1[pp], w1sb[up]), (p_h3[pp], w3sb[up]))):
                for fi in range(NF):
                    w0 = []
                    if pi == 0 and fi == 0:
                        w0 = list(free_at) + [("qs", ready_qs)]
                    P.op("tensor", lambda e, tg=tg, ht=ht, fi=fi, ph=ph, wsb=wsb: e.matmul(
                        ph[:], wsb[:, fi * MH + ht * 128: fi * MH + (ht + 1) * 128],
                        xnT_v[:, tg % 2, fi, :],
                        start=(fi == 0), stop=(fi == NF - 1)),
                        wait=w0, inc=(["pe"] if fi == NF - 1 else []))
            pe_now = P.n("pe")
            P.op("scalar", lambda e, pp=pp: e.activation(
                s1[:, pp * 256:(pp + 1) * 256], p_h1[pp][:], AF.Silu),
                wait=[("pe", pe_now)], inc=["ac"])
            ac_now = P.n("ac")
            P.op("vector", lambda e, pp=pp: e.tensor_tensor(
                gsb[:, pp * 256:(pp + 1) * 256], s1[:, pp * 256:(pp + 1) * 256],
                p_h3[pp][:], ALU.mult), wait=[("ac", ac_now)], inc=["dv"])
            return [("ac", ac_now), ("dv", P.n("dv"))]

        def emit_y(e_, tg, ht, dv_need, pp, stt_guard=0):
            up = e_ % 2
            for sub, py in ((0, p_y0), (1, p_y1)):
                for dc in range(2):
                    P.op("tensor", lambda e, sub=sub, dc=dc, ht=ht, py=py, pp=pp, up=up: e.matmul(
                        py[:, dc * 512:(dc + 1) * 512],
                        gsb[:, pp * 256 + sub * 128: pp * 256 + sub * 128 + 128],
                        w2sb[up][:, ht * D + dc * 512: ht * D + (dc + 1) * 512],
                        start=(ht == 0), stop=(ht == NHT - 1)),
                        wait=([("dv", max(dv_need, stt_guard))] if (dc == 0 and sub == 0) else []),
                        inc=(["pe"] if (ht == NHT - 1 and sub == 1 and dc == 1) else []))

        wq_ready = {0: load_weights(0, ["pe", "dv"])}
        xnT_ready = {}
        xnT_free = {}
        pt_free = {0: 0}
        x1_stored = {0: 0}
        acc_stored = {}
        pe_after_expert = {}
        for e_ in range(EXP):
            done_at = {}
            stt_dv = {}
            NU = NTG * NHT

            def tg_loads(tg):
                war = [("pe", xnT_free.get((e_, tg - 2), xnT_free.get((e_ - 1, max(NTG - 2 + (tg % 2), 0)), 0)))]
                P.dma("qs", lambda e, b=b, tg=tg: e.dma_start(
                    xnT_v[:, tg % 2], d["xnT_d"][b].rearrange(
                        "(f p) t -> p f t", p=128)[:, :, tg * 256:(tg + 1) * 256]),
                    war=war)
                if e_ > 0:
                    qp_need = max(acc_stored.get((e_ - 1, tg), 0),
                                  acc_stored.get((e_, tg - 2), 0))
                    P.dma("qs", lambda e, b=b, tg=tg: e.dma_start(
                        accs_v[:, tg % 2], d["acc_d"][b, tg * 256:(tg + 1) * 256, :]
                        .rearrange("(c p) x -> p c x", p=128)),
                        war=[("qp", qp_need)])
                xnT_ready[tg] = P.n("qs")

            def tg_epilogue(tg):
                xnT_free[(e_, tg)] = P.n("pe")
                qs_need = xnT_ready[tg]
                if e_ == EXP - 1:
                    P.dma("qs", lambda e, b=b, tg=tg: e.dma_start(
                        xpt[:].rearrange("p (c x) -> p c x", c=2),
                        d["partial_d"][b, tg * 256:(tg + 1) * 256, :]
                        .rearrange("(c p) x -> p c x", p=128)), war=[("dv", pt_free[0])])
                    qs_need = P.n("qs")
                for sub, py in ((0, p_y0), (1, p_y1)):
                    ch = tg * 2 + sub
                    gcol = gts[:, ch * EXP + e_: ch * EXP + e_ + 1]
                    asl = accs_v[:, tg % 2, sub, :]
                    if e_ == 0:
                        P.op("vector", lambda e, asl=asl, py=py, gcol=gcol: e.tensor_scalar(
                            asl, py[:], gcol, None, ALU.mult),
                            wait=["pe", ("qs", qs_base),
                                  ("qp", acc_stored.get((e_, tg - 2), 0))], inc=["dv"])
                    else:
                        P.op("vector", lambda e, asl=asl, py=py, gcol=gcol: e.scalar_tensor_tensor(
                            asl, py[:], gcol, asl, ALU.mult, ALU.add),
                            wait=["pe", ("qs", qs_need),
                                  ("qp", acc_stored.get((e_, tg - 2), 0))], inc=["dv"])
                    if e_ == EXP - 1:
                        P.op("vector", lambda e, asl=asl, sub=sub: e.tensor_tensor(
                            x1t[:, sub * D:(sub + 1) * D], asl,
                            xpt[:, sub * D:(sub + 1) * D], ALU.add),
                            wait=[("qp", x1_stored[0])], inc=["dv"])
                stt_dv[tg] = P.n("dv")
                if e_ == EXP - 1:
                    pt_free[0] = P.n("dv")
                    P.dma("qp", lambda e, b=b, tg=tg: e.dma_start(
                        d["x1_d"][b, tg * 256:(tg + 1) * 256, :].rearrange(
                            "(c p) x -> p c x", p=128),
                        x1t[:].rearrange("p (c x) -> p c x", c=2)), war=["dv"])
                    x1_stored[0] = P.n("qp")
                else:
                    P.dma("qp", lambda e, b=b, tg=tg: e.dma_start(
                        d["acc_d"][b, tg * 256:(tg + 1) * 256, :].rearrange(
                            "(c p) x -> p c x", p=128), accs_v[:, tg % 2]), war=["dv"])
                acc_stored[(e_, tg)] = P.n("qp")

            for u in range(NU + 1):
                if u == NHT and e_ + 1 < EXP:
                    wq_ready[e_ + 1] = load_weights(
                        e_ + 1, [("pe", pe_after_expert.get(e_ - 1, 0))])
                if u < NU:
                    tg, ht = divmod(u, NHT)
                    if ht == 0:
                        tg_loads(tg)
                    if u - 2 >= 0:
                        free_at = done_at[u - 2]
                    else:
                        free_at = [("dv", P.n("dv")), ("ac", P.n("ac"))]
                    done_at[u] = emit_up(e_, tg, ht,
                                         max(wq_ready[e_], xnT_ready[tg]), free_at,
                                         u % 2)
                if u > 0:
                    ptg, pht = divmod(u - 1, NHT)
                    emit_y(e_, ptg, pht, dict(done_at[u - 1])["dv"], (u - 1) % 2,
                           stt_dv.get(ptg - 1, 0) if pht == 0 else 0)
                    if pht == NHT - 1:
                        tg_epilogue(ptg)
            pe_after_expert[e_] = P.n("pe")
        P.op("gpsimd", lambda e: e.nop(), wait=["qp"])
        with nc.Block() as blk:
            P.emit_into_block(blk, None, self.sems)
        stk.close()

    def phase_D(self):
        for b in range(self.n_seq):
            self._d_seq(b)

    def _d_seq(self, b):
        nc, d, g = self.nc, self.d, self.g
        S, NCH = self.S, self.NCH
        NG = S // 512
        sstk = ExitStack()
        qT = self.sbuf(sstk, "d_qT", [128, NF * S], BF16)
        kT = self.sbuf(sstk, "d_kT", [128, NF * S], BF16)
        vx = self.sbuf(sstk, "d_vx", [128, NCH * H * 65], BF16)

        wstk = ExitStack()
        wQ = self.sbuf(wstk, "d_wQ", [128, NF * D], BF16)
        wK = self.sbuf(wstk, "d_wK", [128, NF * D], BF16)
        wV = self.sbuf(wstk, "d_wV", [128, NF * D], BF16)
        stg = self.sbuf(wstk, "d_stg", [128, 2 * D], F32)
        xt = self.sbuf(wstk, "d_xt", [128, 2 * D], F32)
        xnb = self.sbuf(wstk, "d_xnb", [128, D], BF16)
        xnT = self.sbuf(wstk, "d_xnT", [128, NF * 512], BF16)
        scr = self.sbuf(wstk, "d_scr", [128, D], F32)
        st8 = self.sbuf(wstk, "d_st8", [128, 8], F32)
        ptA = self.psum(wstk, "d_ptA", [128, 128], BF16)
        ptB = self.psum(wstk, "d_ptB", [128, 128], BF16)
        pq = self.psum(wstk, "d_pq", [128, 512], F32)
        pq2 = self.psum(wstk, "d_pq2", [128, 512], F32)

        P = Prog(self.semc)
        pq_free = {}
        for wsb, src in ((wQ, d["wQ"]), (wK, d["wK"]), (wV, d["wV"])):
            for fi in range(NF):
                pp = fi % 2
                P.dma("qs", lambda e, fi=fi, pp=pp, src=src: e.dma_start(
                    stg[:, pp * D:(pp + 1) * D], src[fi * 128:(fi + 1) * 128, :]),
                    war=["dv"])
                P.op("vector", lambda e, fi=fi, pp=pp, wsb=wsb: e.tensor_copy(
                    wsb[:, fi * D:(fi + 1) * D], stg[:, pp * D:(pp + 1) * D]),
                    wait=["qs"], inc=["dv"])
        SUM, SQ, MEAN, VAR, RSTD, NMR = range(6)
        st8x = self.sbuf(wstk, "d_st8x", [128, 12], F32)
        xnb2 = self.sbuf(wstk, "d_xnb2", [128, 2 * D], F32)
        ln_free = {}
        trans_free = {}
        stat_done = {}

        def stats(ch):
            pp = ch % 2
            base = pp * 6
            sl = lambda i: st8x[:, base + i: base + i + 1]
            xt_c = xt[:, pp * D:(pp + 1) * D]
            xnb_c = xnb2[:, pp * D:(pp + 1) * D]
            P.dma("qs", lambda e, b=b, ch=ch, xt_c=xt_c: e.dma_start(
                xt_c, d["x1_d"][b, ch * 128:(ch + 1) * 128, :]),
                war=[("ac", ln_free.get(ch - 2, (0, 0))[0]),
                     ("dv", ln_free.get(ch - 2, (0, 0))[1])])
            qs_x = P.n("qs")
            P.op("scalar", lambda e, xt_c=xt_c, xnb_c=xnb_c: e.activation(
                xnb_c, xt_c, AF.Square, accum_out=sl(SQ)),
                wait=[("qs", qs_x), ("pe", trans_free.get(ch - 2, 0))], inc=["ac"])
            P.op("vector", lambda e, xt_c=xt_c: e.reduce_sum(
                sl(SUM), xt_c, axis=AX.X), wait=[("qs", qs_x)], inc=["dv"])
            P.op("vector", lambda e: e.tensor_scalar(
                sl(MEAN), sl(SUM), 1.0 / D, None, ALU.mult), inc=["dv"])
            P.op("vector", lambda e: e.tensor_scalar(
                sl(NMR), sl(MEAN), -1.0 / D, None, ALU.mult), inc=["dv"])
            P.op("vector", lambda e: e.tensor_scalar(
                sl(VAR), sl(SQ), 1.0 / D, None, ALU.mult), wait=["ac"], inc=["dv"])
            P.op("vector", lambda e: e.scalar_tensor_tensor(
                sl(VAR), sl(SUM), sl(NMR), sl(VAR), ALU.mult, ALU.add), inc=["dv"])
            dv_v = P.n("dv")
            P.op("scalar", lambda e: e.activation(
                sl(RSTD), sl(VAR), AF.Ln, bias=g["eps_ln"][:]),
                wait=[("dv", dv_v)], inc=["ac"])
            P.op("scalar", lambda e: e.activation(
                sl(RSTD), sl(RSTD), AF.Exp, scale=-0.5), inc=["ac"])
            P.op("vector", lambda e: e.tensor_scalar(
                sl(NMR), sl(MEAN), -1.0, None, ALU.mult), inc=["dv"])
            return (P.n("ac"), P.n("dv"))

        def transform(ch, ct):
            pp = ch % 2
            base = pp * 6
            sl = lambda i: st8x[:, base + i: base + i + 1]
            xt_c = xt[:, pp * D:(pp + 1) * D]
            xnb_c = xnb2[:, pp * D:(pp + 1) * D]
            ac_v, dv_v = stat_done[ch]
            P.op("vector", lambda e, xt_c=xt_c: e.tensor_scalar(
                xnb[:], xt_c, sl(NMR), sl(RSTD), ALU.add, ALU.mult),
                wait=[("ac", ac_v), ("dv", dv_v), "pe"], inc=["dv"])
            xn_dv = P.n("dv")
            cp_dv = {0: xn_dv, 1: xn_dv}
            for f in range(NF):
                pt = ptA if f % 2 == 0 else ptB
                P.op("tensor", lambda e, f=f, pt=pt: e.transpose(
                    pt[:], xnb[:, f * 128:(f + 1) * 128], g["ident"][:]),
                    wait=[("dv", cp_dv[f % 2])], inc=["pe"])
                P.op("vector", lambda e, f=f, ct=ct, pt=pt: e.tensor_copy(
                    xnT[:, f * 512 + ct * 128: f * 512 + (ct + 1) * 128], pt[:]),
                    wait=["pe", "qp"], inc=["dv"])
                cp_dv[f % 2] = P.n("dv")
            trans_free[ch] = P.n("pe")
            ln_free[ch] = (P.n("ac"), P.n("dv"))

        stat_done[0] = stats(0)
        for gg in range(NG):
            for ct in range(4):
                ch = gg * 4 + ct
                if ch + 1 < NCH:
                    stat_done[ch + 1] = stats(ch + 1)
                transform(ch, ct)
            t0 = gg * 512
            P.dma("qp", lambda e, b=b, t0=t0: e.dma_start(
                d["xn1T_d"][b].rearrange("(f p) t -> p f t", p=128)[:, :, t0:t0 + 512],
                xnT[:].rearrange("p (f t) -> p f t", f=NF)), war=["dv"])
            for wsb, dstT in ((wQ, qT), (wK, kT)):
                for fo in range(NF):
                    pqx = pq if fo % 2 == 0 else pq2
                    fa = pq_free.get(id(pqx), ["dv"])
                    for fi in range(NF):
                        P.op("tensor", lambda e, fo=fo, fi=fi, wsb=wsb, pqx=pqx: e.matmul(
                            pqx[:], wsb[:, fi * D + fo * 128: fi * D + (fo + 1) * 128],
                            xnT[:, fi * 512:(fi + 1) * 512],
                            start=(fi == 0), stop=(fi == NF - 1)),
                            wait=(fa if fi == 0 else []),
                            inc=(["pe"] if fi == NF - 1 else []))
                    P.op("vector", lambda e, fo=fo, t0=t0, dstT=dstT, pqx=pqx: e.tensor_copy(
                        dstT[:, fo * S + t0: fo * S + t0 + 512], pqx[:]),
                        wait=["pe"], inc=["dv"])
                    pq_free[id(pqx)] = [("dv", P.n("dv"))]
            for ct in range(4):
                ch = gg * 4 + ct
                for oc in range(2):
                    pvx = pq if oc == 0 else pq2
                    fa = pq_free.get(id(pvx), ["dv"])
                    for fi in range(NF):
                        P.op("tensor", lambda e, ct=ct, oc=oc, fi=fi, pvx=pvx: e.matmul(
                            pvx[:], xnT[:, fi * 512 + ct * 128: fi * 512 + (ct + 1) * 128],
                            wV[:, fi * D + oc * 512: fi * D + (oc + 1) * 512],
                            start=(fi == 0), stop=(fi == NF - 1)),
                            wait=(fa if fi == 0 else []),
                            inc=(["pe"] if fi == NF - 1 else []))
                    vdst = vx[:, ch * H * 65 + oc * 8 * 65: ch * H * 65 + (oc + 1) * 8 * 65]
                    vdst = vdst.rearrange("p (a j) -> p a j", j=65)[:, :, 0:64]
                    P.op("vector", lambda e, vdst=vdst, pvx=pvx: e.tensor_copy(
                        vdst, pvx[:].rearrange("p (a j) -> p a j", j=64)),
                        wait=["pe"], inc=["dv"])
                    pq_free[id(pvx)] = [("dv", P.n("dv"))]
        P.op("vector", lambda e: e.memset(
            vx[:].rearrange("p (c j) -> p c j", j=65)[:, :, 64:65], 1.0), inc=["dv"])
        with nc.Block() as blk:
            P.emit_into_block(blk, None, self.sems)
        wstk.close()

        fstk = ExitStack()
        o_s = self.sbuf(fstk, "d_o", [128, NCH * D], BF16)
        psb = self.sbuf(fstk, "d_psb", [128, 2 * 2048], BF16)
        dsc = self.sbuf(fstk, "d_dsc", [128, 2 * H], F32)
        wO = self.sbuf(fstk, "d_wO", [128, NF * D], BF16)
        stg2 = self.sbuf(fstk, "d_stg2", [128, 2 * D], F32)
        oT = self.sbuf(fstk, "d_oT", [128, D], BF16)
        xt2 = self.sbuf(fstk, "d_xt2", [128, 2 * D], F32)
        pout = self.sbuf(fstk, "d_pout", [128, 2 * D], F32)
        p_st = self.psum(fstk, "d_pst", [128, 2048], F32)
        p_num = self.psum(fstk, "d_pnum", [128, D], F32)
        p_den = self.psum(fstk, "d_pden", [128, H], F32)
        p_tr = self.psum(fstk, "d_ptr", [128, 128], BF16)

        P = Prog(self.semc)
        for fi in range(NF):
            pp = fi % 2
            P.dma("qs", lambda e, fi=fi, pp=pp: e.dma_start(
                stg2[:, pp * D:(pp + 1) * D], d["wO"][fi * 128:(fi + 1) * 128, :]),
                war=["dv"])
            P.op("vector", lambda e, fi=fi, pp=pp: e.tensor_copy(
                wO[:, fi * D:(fi + 1) * D], stg2[:, pp * D:(pp + 1) * D]),
                wait=["qs"], inc=["dv"])
        for qt in range(NCH):
            qs_ = qt * 128
            SLOT = [(h % 2) * 8 + (h // 2) for h in range(H)]
            exp_done = {}

            def emit_st(kt):
                ks = kt * 128
                pb = (kt % 2) * 2048
                for half in range(2):
                    for sl in range(half * 8, half * 8 + 8):
                        h = (sl // 8) + 2 * (sl % 8)
                        f, po = h // 2, 64 * (h % 2)
                        w0 = ([exp_done[(kt - 1, half)]] if (kt > 0 and sl % 8 == 0)
                              else (["ac", "dv"] if (kt == 0 and sl == 0) else []))
                        P.op("tensor", lambda e, sl=sl, f=f, po=po, ks=ks, qs_=qs_: e.matmul(
                            p_st[:, sl * 128:(sl + 1) * 128],
                            kT[po:po + 64, f * S + ks: f * S + ks + 128],
                            qT[po:po + 64, f * S + qs_: f * S + qs_ + 128],
                            start=(sl % 4 == 0), stop=(sl % 4 == 3)),
                            wait=w0, inc=(["pe"] if sl % 8 == 7 else []))
                    pe_now = P.n("pe")
                    P.op("scalar", lambda e, half=half, pb=pb: e.activation(
                        psb[:, pb + half * 1024: pb + (half + 1) * 1024],
                        p_st[:, half * 1024:(half + 1) * 1024], AF.Exp, scale=0.125),
                        wait=[("pe", pe_now)], inc=["ac"])
                    if kt == qt:
                        P.op("vector", lambda e, half=half, pb=pb: e.tensor_tensor(
                            psb[:, pb + half * 1024: pb + (half + 1) * 1024],
                            psb[:, pb + half * 1024: pb + (half + 1) * 1024],
                            g["tri16"][:, half * 1024:(half + 1) * 1024], ALU.mult),
                            wait=["ac"], inc=["dv"])
                        exp_done[(kt, half)] = ("dv", P.n("dv"))
                    else:
                        exp_done[(kt, half)] = ("ac", P.n("ac"))

            def emit_pv(kt):
                pb = (kt % 2) * 2048
                for half in range(2):
                    heads = [(sl // 8) + 2 * (sl % 8) for sl in range(half * 8, half * 8 + 8)]
                    for i, h in enumerate(heads):
                        asl = pb + SLOT[h] * 128
                        vsl = vx[:, kt * H * 65 + h * 65: kt * H * 65 + h * 65 + 64]
                        w1_ = [exp_done[(kt, half)]] if i == 0 else []
                        if kt == 0 and i == 0:
                            w1_ = w1_ + [("dv", dv_qt0), ("ac", ac_qt0)]
                        P.op("tensor", lambda e, h=h, vsl=vsl, asl=asl, kt=kt, qt=qt: e.matmul(
                            p_num[:, h * 64:(h + 1) * 64], psb[:, asl: asl + 128], vsl,
                            start=(kt == 0 and h % 8 == 0), stop=(kt == qt and h % 8 == 7)),
                            wait=w1_)
                        P.op("tensor", lambda e, h=h, asl=asl, kt=kt, qt=qt: e.matmul(
                            p_den[:, h:h + 1], psb[:, asl: asl + 128], g["ones_c"][:],
                            start=(kt == 0 and h == 0), stop=(kt == qt and h == H - 1)),
                            inc=(["pe"] if (half == 1 and i == 7) else []))

            dv_qt0 = P.n("dv")
            ac_qt0 = P.n("ac")
            for kt in range(qt + 2):
                if kt <= qt:
                    emit_st(kt)
                if kt > 0:
                    emit_pv(kt - 1)
            P.op("vector", lambda e: e.reciprocal(dsc[:, H:2 * H], p_den[:]),
                 wait=["pe"], inc=["dv"])
            for h in range(H):
                P.op("vector", lambda e, h=h, qt=qt: e.tensor_scalar(
                    o_s[:, qt * D + h * 64: qt * D + (h + 1) * 64],
                    p_num[:, h * 64:(h + 1) * 64], dsc[:, H + h:H + h + 1], None,
                    ALU.mult), inc=["dv"])
        for ch in range(NCH):
            pp = ch % 2
            P.dma("qs", lambda e, b=b, ch=ch, pp=pp: e.dma_start(
                xt2[:, pp * D:(pp + 1) * D], d["x1_d"][b, ch * 128:(ch + 1) * 128, :]),
                war=["dv"])
            qs_x = P.n("qs")
            for f in range(NF):
                P.op("tensor", lambda e, f=f, ch=ch: e.transpose(
                    p_tr[:], o_s[:, ch * D + f * 128: ch * D + (f + 1) * 128], g["ident"][:]),
                    wait=["dv", "ac"], inc=["pe"])
                P.op("vector", lambda e, f=f: e.tensor_copy(
                    oT[:, f * 128:(f + 1) * 128], p_tr[:]), wait=["pe"], inc=["dv"])
            for dc in range(2):
                for fe in range(NF):
                    P.op("tensor", lambda e, dc=dc, fe=fe: e.matmul(
                        p_num[:, dc * 512:(dc + 1) * 512], oT[:, fe * 128:(fe + 1) * 128],
                        wO[:, fe * D + dc * 512: fe * D + (dc + 1) * 512],
                        start=(fe == 0), stop=(fe == NF - 1)),
                        wait=(["dv", "ac"] if fe == 0 else []),
                        inc=(["pe"] if fe == NF - 1 else []))
            P.op("vector", lambda e, pp=pp: e.scalar_tensor_tensor(
                pout[:, pp * D:(pp + 1) * D], p_num[:], 1.0, xt2[:, pp * D:(pp + 1) * D],
                ALU.mult, ALU.add), wait=["pe", ("qs", qs_x), "qp"], inc=["dv"])
            P.op("tensor", lambda e: e.nop(), wait=["dv"])
            P.dma("qp", lambda e, b=b, ch=ch, pp=pp: e.dma_start(
                d["partial1_d"][b, ch * 128:(ch + 1) * 128, :], pout[:, pp * D:(pp + 1) * D]),
                war=["dv"])
        P.op("gpsimd", lambda e: e.nop(), wait=["qp"])
        with nc.Block() as blk:
            P.emit_into_block(blk, None, self.sems)
        fstk.close()
        sstk.close()

    def phase_E(self):
        nc, d, g = self.nc, self.d, self.g
        S, NCH = self.S, self.NCH
        stk = ExitStack()
        wup = self.sbuf(stk, "e_wup", [128, NF * DH], BF16)
        wdn = self.sbuf(stk, "e_wdn", [128, NHU * D], BF16)
        bup = self.sbuf(stk, "e_bup", [128, NHU], F32)
        P = Prog(self.semc)
        P.dma("qs", lambda e: e.dma_start(
            wup[:].rearrange("p (f h) -> p f h", f=NF),
            d["wup_d"].rearrange("(f p) h -> p f h", p=128)), war=["pe", "dv"])
        P.dma("qs", lambda e: e.dma_start(
            wdn[:].rearrange("p (u x) -> p u x", u=NHU),
            d["wdown_d"].rearrange("(u p) x -> p u x", p=128)), war=["dv"])
        P.dma("qs", lambda e: e.dma_start(bup[:], d["bup_d"][:]), war=["dv"])
        with nc.Block() as blk:
            P.emit_into_block(blk, None, self.sems)
        for b in range(self.n_seq):
            self._e_seq(b, wup, wdn, bup)
        stk.close()

    def _e_seq(self, b, wup, wdn, bup):
        nc, d, g = self.nc, self.d, self.g
        S, NCH = self.S, self.NCH
        NTG = S // 256
        stk = ExitStack()
        xnT = self.sbuf(stk, "e_xnT", [128, 2 * NF * 256], BF16)
        gsb = self.sbuf(stk, "e_gsb", [128, 2 * NHU * 256], BF16)
        xpt = self.sbuf(stk, "e_xpt", [128, 2 * D], F32)
        yt = self.sbuf(stk, "e_yt", [128, 2 * D], F32)
        p_h = [self.psum(stk, "e_ph", [128, 256], F32) for _ in range(2)]
        p_y0 = self.psum(stk, "e_py0", [128, D], F32)
        p_y1 = self.psum(stk, "e_py1", [128, D], F32)

        P = Prog(self.semc)
        xnT_v = xnT[:].rearrange("p (u f t) -> p u f t", u=2, f=NF)
        gsb_v = gsb[:].rearrange("p (u h t) -> p u h t", u=2, h=NHU)
        xnT_ready = {}
        up_done = {}
        epi_dv = {0: 0}
        y_stored = {0: 0}

        def loads(tg):
            P.dma("qs", lambda e, b=b, tg=tg: e.dma_start(
                xnT_v[:, tg % 2], d["xn1T_d"][b].rearrange(
                    "(f p) t -> p f t", p=128)[:, :, tg * 256:(tg + 1) * 256]),
                war=[("pe", up_done.get(tg - 2, 0))])
            xnT_ready[tg] = P.n("qs")

        def up_gelu(tg):
            for hu in range(NHU):
                ph = p_h[hu % 2]
                for fd in range(NF):
                    P.op("tensor", lambda e, hu=hu, fd=fd, tg=tg, ph=ph: e.matmul(
                        ph[:], wup[:, fd * DH + hu * 128: fd * DH + (hu + 1) * 128],
                        xnT_v[:, tg % 2, fd, :],
                        start=(fd == 0), stop=(fd == NF - 1)),
                        wait=((["ac", "dv", ("qs", xnT_ready[tg])]) if (fd == 0 and hu <= 1)
                              else (["ac"] if fd == 0 else [])),
                        inc=(["pe"] if fd == NF - 1 else []))
                pe_now = P.n("pe")
                P.op("scalar", lambda e, hu=hu, tg=tg, ph=ph: e.activation(
                    gsb_v[:, tg % 2, hu, :], ph[:], AF.Gelu,
                    bias=bup[:, hu:hu + 1]), wait=[("pe", pe_now)], inc=["ac"])
            up_done[tg] = P.n("pe")

        def down_epi(tg):
            for sub, py in ((0, p_y0), (1, p_y1)):
                for dc in range(2):
                    for hu in range(NHU):
                        P.op("tensor", lambda e, sub=sub, dc=dc, hu=hu, py=py, tg=tg: e.matmul(
                            py[:, dc * 512:(dc + 1) * 512],
                            gsb_v[:, tg % 2, hu, sub * 128: sub * 128 + 128],
                            wdn[:, hu * D + dc * 512: hu * D + (dc + 1) * 512],
                            start=(hu == 0), stop=(hu == NHU - 1)),
                            wait=((["ac"] + [("dv", epi_dv[0])]) if hu == 0 else []),
                            inc=(["pe"] if hu == NHU - 1 else []))
            for sub, py in ((0, p_y0), (1, p_y1)):
                ch = tg * 2 + sub
                pp = ch % 2
                P.dma("qs", lambda e, b=b, ch=ch, pp=pp: e.dma_start(
                    xpt[:, pp * D:(pp + 1) * D], d["partial1_d"][b, ch * 128:(ch + 1) * 128, :]),
                    war=[("dv", epi_dv[0])])
                qs_x = P.n("qs")
                P.op("vector", lambda e, py=py, pp=pp: e.scalar_tensor_tensor(
                    yt[:, pp * D:(pp + 1) * D], py[:], 1.0, xpt[:, pp * D:(pp + 1) * D],
                    ALU.mult, ALU.add),
                    wait=["pe", ("qs", qs_x), ("qp", y_stored[0])], inc=["dv"])
                P.op("vector", lambda e, pp=pp: e.tensor_tensor(
                    yt[:, pp * D:(pp + 1) * D], yt[:, pp * D:(pp + 1) * D],
                    g["bcd_bc"][:], ALU.add), inc=["dv"])
                P.dma("qp", lambda e, b=b, ch=ch, pp=pp: e.dma_start(
                    d["y"][b, ch * 128:(ch + 1) * 128, :], yt[:, pp * D:(pp + 1) * D]),
                    war=["dv"])
                y_stored[0] = P.n("qp")
            epi_dv[0] = P.n("dv")

        for tg in range(NTG):
            loads(tg)
            up_gelu(tg)
            if tg > 0:
                down_epi(tg - 1)
        down_epi(NTG - 1)
        P.op("gpsimd", lambda e: e.nop(), wait=["qp"])
        with nc.Block() as blk:
            P.emit_into_block(blk, None, self.sems)
        stk.close()


from concourse.bass_utils import run_bass_kernel_spmd


def kernel(**inputs):
    in_maps = stage_host(inputs, n_cores=8)
    b = Builder(S=2048, n_seq=2)
    b.phase_W()
    b.phase_AB()
    b.phase_C()
    b.phase_D()
    b.phase_E()
    res = run_bass_kernel_spmd(b.nc, in_maps, core_ids=list(range(8)))
    out = np.concatenate(
        [np.asarray(r["y"], dtype=np.float32).reshape(2, 2048, D) for r in res.results],
        axis=0)
    return out


# version 09 (final)
# speedup vs baseline: 1.0358x; 1.0052x over previous
import numpy as np
import ml_dtypes
from contextlib import ExitStack
from collections import defaultdict

import concourse.bass as bass
import concourse.mybir as mybir

F32 = mybir.dt.float32
BF16 = mybir.dt.bfloat16
AF = mybir.ActivationFunctionType
ALU = mybir.AluOpType
AX = mybir.AxisListType
BF = ml_dtypes.bfloat16

D = 1024
H = 16
HD = 64
NF = 8
EXP = 8
MH = 1408
NHT = 11
DH = 4096
NHU = 32
LN_EPS = 1e-5
ATT_EPS = 1e-6


class Prog:

    ENGS = ("sync", "tensor", "vector", "scalar", "gpsimd")

    def __init__(self, semc=None):
        self.ops = []
        self.semc = semc if semc is not None else defaultdict(int)

    def n(self, sem):
        return self.semc[sem]

    DMA_ENG = {"qs": "sync", "qp": "gpsimd", "qa": "scalar", "qv": "vector"}

    def dma(self, stream, fn, war=()):
        self.op(self.DMA_ENG[stream], fn, wait=[stream] + list(war), inc=[stream],
                is_dma=True)
        return self.semc[stream]

    AUTO = {"vector": "dv", "scalar": "ac"}

    def op(self, eng, fn, wait=(), inc=(), is_dma=False):
        ws = []
        for w in wait:
            if isinstance(w, str):
                ws.append((w, self.semc[w]))
            else:
                ws.append((w[0], w[1]))
        incs = []
        for i in inc:
            if isinstance(i, str):
                amt = 16 if i.startswith("q") else 1
                incs.append((i, amt))
            else:
                incs.append((i[0], i[1]))
        own = self.AUTO.get(eng) if not is_dma else None
        if own is not None:
            if not any(s == own for s, _ in ws):
                ws.append((own, self.semc[own]))
            if not any(s == own for s, _ in incs):
                incs.append((own, 1))
        for s, amt in incs:
            self.semc[s] += amt
        self.ops.append((eng, fn, ws, incs))

    def emit_into_block(self, blk, engines, sems):
        for eng_name in self.ENGS:
            my_ops = [o for o in self.ops if o[0] == eng_name]
            if not my_ops:
                continue
            def body(e, my_ops=my_ops, eng_name=eng_name):
                known = defaultdict(int)
                for _, fn, ws, incs in my_ops:
                    for s, v in ws:
                        if known[s] < v:
                            e.wait_ge(sems[s], v)
                            known[s] = v
                    inst = fn(e)
                    for s, amt in incs:
                        assert inst is not None, f"inc {s} on None inst"
                        inst.then_inc(sems[s], amt)

            getattr(blk, eng_name)(body)


def _dense_stage_mats(w, st, transpose):
    s = 1 << st
    out = np.zeros((32, 128, 128), dtype=np.float32)
    ws = w[st].reshape(DH // (2 * s), s, 2, 2)
    for blk in range(32):
        base = blk * 128
        for iloc in range(128):
            i = base + iloc
            if i & s:
                continue
            n = i >> (st + 1)
            j = i & (s - 1)
            w2 = ws[n, j]
            if transpose:
                w2 = w2.T
            out[blk, iloc, iloc] = w2[0, 0]
            out[blk, iloc, iloc + s] = w2[0, 1]
            out[blk, iloc + s, iloc] = w2[1, 0]
            out[blk, iloc + s, iloc + s] = w2[1, 1]
    return out


def stage_up_blocks(up_w):
    arr = np.empty((32, 7, 128, 128), dtype=np.float32)
    for k in range(6):
        arr[:, k] = _dense_stage_mats(up_w, k, transpose=False)
    arr[:, 6] = _dense_stage_mats(up_w, 6, transpose=True)
    return arr


def stage_dn_blocks(dn_w):
    arr = np.empty((32, 7, 128, 128), dtype=np.float32)
    arr[:, 0] = _dense_stage_mats(dn_w, 0, transpose=False)
    for k in range(1, 7):
        arr[:, k] = _dense_stage_mats(dn_w, k, transpose=True)
    return arr


def stage_tile_coefs(w, transpose):
    out = np.empty((128, 5, 16, 4), dtype=np.float32)
    for q in range(5):
        st = 7 + q
        s = 1 << st
        ws = w[st].reshape(DH // (2 * s), s, 2, 2)
        bit = 1 << (st - 7)
        atiles = [t for t in range(32) if not (t & bit)]
        assert len(atiles) == 16
        for pi, T in enumerate(atiles):
            for p in range(128):
                i = T * 128 + p
                n = i >> (st + 1)
                j = i & (s - 1)
                w2 = ws[n, j]
                if transpose:
                    w2 = w2.T
                out[p, q, pi, 0] = w2[0, 0]
                out[p, q, pi, 1] = w2[0, 1]
                out[p, q, pi, 2] = w2[1, 0]
                out[p, q, pi, 3] = w2[1, 1]
    return out.reshape(128, 5 * 16 * 4)


def stage_dn_tile_coefs(dn_w):
    base = stage_tile_coefs(dn_w, transpose=True).reshape(128, 5, 16, 4)
    rev = base[:, ::-1]
    return np.ascontiguousarray(rev.reshape(128, 5 * 16 * 4))


def stage_host(inputs, n_cores=8):
    f32 = np.float32
    x = np.ascontiguousarray(inputs["L0_attn_x"], dtype=f32)
    shared = {
        "wqT": np.ascontiguousarray(inputs["L0_attn_Wq"].T, f32),
        "wkT": np.ascontiguousarray(inputs["L0_attn_Wk"].T, f32),
        "wvT": np.ascontiguousarray(inputs["L0_attn_Wv"].T, f32),
        "woT": np.ascontiguousarray(inputs["L0_attn_Wo"].T, f32),
        "bq_f": np.ascontiguousarray(np.asarray(inputs["L0_attn_bq"], f32).reshape(NF, 128).T),
        "bk_f": np.ascontiguousarray(np.asarray(inputs["L0_attn_bk"], f32).reshape(NF, 128).T),
        "bv_r": np.asarray(inputs["L0_attn_bv"], f32).reshape(1, D),
        "bo_r": np.asarray(inputs["L0_attn_bo"], f32).reshape(1, D),
        "wrT": np.ascontiguousarray(np.asarray(inputs["L0_ffn_w_router"], f32).T),
        "w1": np.ascontiguousarray(inputs["L0_ffn_w1"], f32),
        "w3": np.ascontiguousarray(inputs["L0_ffn_w3"], f32),
        "w2": np.ascontiguousarray(inputs["L0_ffn_w2"], f32),
        "wQ": np.ascontiguousarray(inputs["L1_attn_W_Q"], f32),
        "wK": np.ascontiguousarray(inputs["L1_attn_W_K"], f32),
        "wV": np.ascontiguousarray(inputs["L1_attn_W_V"], f32),
        "wO": np.ascontiguousarray(inputs["L1_attn_W_o"], f32),
        "wexpT": np.ascontiguousarray(np.asarray(inputs["L1_ffn_W_exp"], f32).T),
        "bexp_f": np.ascontiguousarray(np.asarray(inputs["L1_ffn_b_exp"], f32).reshape(NHU, 128).T),
        "upbias_f": np.ascontiguousarray(np.asarray(inputs["L1_ffn_up_bias"], f32).reshape(NHU, 128).T),
        "upA": stage_up_blocks(np.asarray(inputs["L1_ffn_up_weights"], f32)),
        "upC": stage_tile_coefs(np.asarray(inputs["L1_ffn_up_weights"], f32), transpose=False),
        "dnC": stage_dn_blocks(np.asarray(inputs["L1_ffn_down_weights"], f32)),
        "dnCoef": stage_dn_tile_coefs(np.asarray(inputs["L1_ffn_down_weights"], f32)),
        "wcon": np.ascontiguousarray(inputs["L1_ffn_W_con"], f32),
        "bcon_r": np.asarray(inputs["L1_ffn_b_con"], f32).reshape(1, D),
        "dbias_r": np.asarray(inputs["L1_ffn_down_bias"], f32).reshape(1, D),
        "ident": np.eye(128, dtype=BF),
        "ident32": np.eye(128, dtype=f32),
        "tri16": np.ascontiguousarray(np.tile(np.tril(np.ones((128, 128), f32)).T, (1, 16)).astype(BF)),
        "ones_c": np.ones((128, 1), dtype=BF),
        "ones_r": np.ones((1, 128), dtype=f32),
    }
    in_maps = []
    bpc = 16 // n_cores
    for c in range(n_cores):
        m = dict(shared)
        m["x"] = np.ascontiguousarray(x[c * bpc:(c + 1) * bpc])
        in_maps.append(m)
    return in_maps


class Builder:
    SEMS = ("qs", "qp", "qa", "pe", "dv", "ac")

    def __init__(self, S=2048, n_seq=2, phases=("W", "AB", "C", "D", "E"), expose=False):
        self.S = S
        self.NCH = S // 128
        self.n_seq = n_seq
        self.phases = phases
        self.expose = expose
        nc = self.nc = bass.Bass("TRN2", target_bir_lowering=False)
        self.stack = ExitStack()
        self._declare_dram()
        self.sems = {n: self.stack.enter_context(nc.semaphore(n)) for n in self.SEMS}
        self.semc = defaultdict(int)
        self.engines = {}

    def _declare_dram(self):
        nc, S = self.nc, self.S
        t = lambda n, sh, dt, kind: nc.dram_tensor(n, sh, dt, kind=kind).ap()
        I, O, N = "ExternalInput", "ExternalOutput", "Internal"
        if self.expose:
            N = O
        d = self.d = {}
        d["x"] = t("x", [self.n_seq, S, D], F32, I)
        for n in ("wqT", "wkT", "wvT", "woT", "wQ", "wK", "wV", "wO"):
            d[n] = t(n, [D, D], F32, I)
        d["bq_f"] = t("bq_f", [128, NF], F32, I)
        d["bk_f"] = t("bk_f", [128, NF], F32, I)
        for n in ("bv_r", "bo_r", "bcon_r", "dbias_r"):
            d[n] = t(n, [1, D], F32, I)
        d["wrT"] = t("wrT", [D, EXP], F32, I)
        d["w1"] = t("w1", [EXP, D, MH], F32, I)
        d["w3"] = t("w3", [EXP, D, MH], F32, I)
        d["w2"] = t("w2", [EXP, MH, D], F32, I)
        d["wexpT"] = t("wexpT", [DH, D], F32, I)
        d["wcon"] = t("wcon", [DH, D], F32, I)
        d["bexp_f"] = t("bexp_f", [128, NHU], F32, I)
        d["upbias_f"] = t("upbias_f", [128, NHU], F32, I)
        d["upA"] = t("upA", [32, 7, 128, 128], F32, I)
        d["dnC"] = t("dnC", [32, 7, 128, 128], F32, I)
        d["upC"] = t("upC", [128, 320], F32, I)
        d["dnCoef"] = t("dnCoef", [128, 320], F32, I)
        d["ident"] = t("ident", [128, 128], BF16, I)
        d["ident32"] = t("ident32", [128, 128], F32, I)
        d["tri16"] = t("tri16", [128, 128 * 16], BF16, I)
        d["ones_c"] = t("ones_c", [128, 1], BF16, I)
        d["ones_r"] = t("ones_r", [1, 128], F32, I)
        d["acc_d"] = t("acc_d", [self.n_seq, S, D], F32, N)
        d["w1b_d"] = t("w1b_d", [EXP, D, MH], BF16, N)
        d["w3b_d"] = t("w3b_d", [EXP, D, MH], BF16, N)
        d["w2b_d"] = t("w2b_d", [EXP, MH, D], BF16, N)
        d["wup_d"] = t("wup_d", [D, DH], BF16, N)
        d["wdown_d"] = t("wdown_d", [DH, D], BF16, N)
        d["bup_d"] = t("bup_d", [128, NHU], F32, N)
        d["bias_bc_d"] = t("bias_bc_d", [3, 128, D], F32, N)
        d["xnT_d"] = t("xnT_d", [self.n_seq, D, S], BF16, N)
        d["gates_d"] = t("gates_d", [self.n_seq, 128, self.NCH * EXP], F32, N)
        d["partial_d"] = t("partial_d", [self.n_seq, S, D], F32, N)
        d["x1_d"] = t("x1_d", [self.n_seq, S, D], F32, N)
        d["xn1T_d"] = t("xn1T_d", [self.n_seq, D, S], BF16, N)
        d["partial1_d"] = t("partial1_d", [self.n_seq, S, D], F32, N)
        d["y"] = t("y", [self.n_seq, S, D], F32, "ExternalOutput")

    _uid = 0

    def sbuf(self, stk, name, shape, dtype):
        Builder._uid += 1
        return stk.enter_context(self.nc.sbuf_tensor(f"{name}_{Builder._uid}", shape, dtype))

    def psum(self, stk, name, shape, dtype=F32):
        Builder._uid += 1
        return stk.enter_context(self.nc.psum_tensor(f"{name}_{Builder._uid}", shape, dtype))

    def phase_W(self):
        nc, d = self.nc, self.d
        P = Prog(self.semc)
        stk = ExitStack()
        g = self.g = {}
        g["ident"] = self.sbuf(self.stack, "c_ident", [128, 128], BF16)
        g["ident32"] = self.sbuf(self.stack, "c_ident32", [128, 128], F32)
        g["tri16"] = self.sbuf(self.stack, "c_tri16", [128, 2048], BF16)
        g["ones_c"] = self.sbuf(self.stack, "c_ones_c", [128, 1], BF16)
        g["ones_r"] = self.sbuf(self.stack, "c_ones_r", [1, 128], F32)
        g["bq_f"] = self.sbuf(self.stack, "c_bq_f", [128, NF], F32)
        g["bk_f"] = self.sbuf(self.stack, "c_bk_f", [128, NF], F32)
        g["bv_bc"] = self.sbuf(self.stack, "c_bv_bc", [128, D], F32)
        g["bo_bc"] = self.sbuf(self.stack, "c_bo_bc", [128, D], F32)
        g["bcd_bc"] = self.sbuf(self.stack, "c_bcd_bc", [128, D], F32)
        g["eps_ln"] = self.sbuf(self.stack, "c_eps_ln", [128, 1], F32)

        P.op("vector", lambda e: e.memset(g["eps_ln"][:], LN_EPS), inc=["dv"])
        for n in ("ident32", "ident", "tri16", "ones_c", "ones_r", "bq_f", "bk_f"):
            src = d[n][:, 0:2048] if n == "tri16" else d[n][:]
            P.dma("qs", lambda e, a=g[n], b=src: e.dma_start(a[:], b))
        cvstk = ExitStack()
        rows = self.sbuf(cvstk, "w_rows", [1, 4 * D], F32)
        for i, n in enumerate(("bv_r", "bo_r", "bcon_r", "dbias_r")):
            P.dma("qs", lambda e, i=i, n=n: e.dma_start(rows[0:1, i * D:(i + 1) * D], d[n][:]))
        pbc = self.psum(stk, "w_pbc", [128, 512], F32)
        qs_done = P.n("qs")
        for dst, ri in ((g["bv_bc"], 0), (g["bo_bc"], 1)):
            for h in range(2):
                P.op("tensor", lambda e, ri=ri, h=h: e.matmul(
                    pbc[:], g["ones_r"][:], rows[0:1, ri * D + h * 512: ri * D + (h + 1) * 512],
                    start=True, stop=True), wait=[("qs", qs_done)], inc=["pe"])
                P.op("vector", lambda e, dst=dst, h=h: e.tensor_copy(dst[:, h * 512:(h + 1) * 512], pbc[:]),
                     wait=["pe"], inc=["dv"])
                P.op("tensor", lambda e: e.nop(), wait=["dv"])
        for h in range(2):
            P.op("tensor", lambda e, h=h: e.matmul(
                pbc[:], g["ones_r"][:], rows[0:1, 2 * D + h * 512: 2 * D + (h + 1) * 512],
                start=True, stop=True), wait=[("qs", qs_done)], inc=["pe"])
            P.op("vector", lambda e, h=h: e.tensor_copy(g["bcd_bc"][:, h * 512:(h + 1) * 512], pbc[:]),
                 wait=["pe"], inc=["dv"])
            P.op("tensor", lambda e: e.nop(), wait=["dv"])
            P.op("tensor", lambda e, h=h: e.matmul(
                pbc[:], g["ones_r"][:], rows[0:1, 3 * D + h * 512: 3 * D + (h + 1) * 512],
                start=True, stop=True), wait=[("qs", qs_done)], inc=["pe"])
            P.op("vector", lambda e, h=h: e.tensor_tensor(
                g["bcd_bc"][:, h * 512:(h + 1) * 512], g["bcd_bc"][:, h * 512:(h + 1) * 512],
                pbc[:], ALU.add), wait=["pe", "dv"], inc=["dv"])
            P.op("tensor", lambda e: e.nop(), wait=["dv"])

        NW = NF * MH
        cstg = self.sbuf(cvstk, "wc_stg", [128, 2 * NW], F32)
        cbuf = self.sbuf(cvstk, "wc_buf", [128, 2 * NW], BF16)
        conv_jobs = []
        for e_ in range(EXP):
            conv_jobs += [(d["w1"], d["w1b_d"], e_, NF, MH), (d["w3"], d["w3b_d"], e_, NF, MH),
                          (d["w2"], d["w2b_d"], e_, NHT, D)]
        for ji, (srcw, dstw, e_, nch, w) in enumerate(conv_jobs):
            pp = ji % 2
            n = nch * w
            strm = "qs" if ji % 2 == 0 else "qa"
            P.dma(strm, lambda e, srcw=srcw, e_=e_, nch=nch, n=n, pp=pp: e.dma_start(
                cstg[:, pp * NW: pp * NW + n].rearrange("p (c x) -> p c x", c=nch),
                srcw[e_].rearrange("(c p) x -> p c x", p=128)),
                war=(["dv"] if ji % 2 == 0 else ["ac"]))
            if ji % 2 == 0:
                P.op("vector", lambda e, n=n, pp=pp: e.tensor_copy(
                    cbuf[:, pp * NW: pp * NW + n], cstg[:, pp * NW: pp * NW + n]),
                    wait=["qs", ("qp", P.n("qp"))], inc=["dv"])
            else:
                P.op("scalar", lambda e, n=n, pp=pp: e.activation(
                    cbuf[:, pp * NW: pp * NW + n], cstg[:, pp * NW: pp * NW + n],
                    AF.Copy), wait=["qa", ("qp", P.n("qp"))], inc=["ac"])
            P.dma("qp", lambda e, dstw=dstw, e_=e_, nch=nch, n=n, pp=pp: e.dma_start(
                dstw[e_].rearrange("(c p) x -> p c x", p=128),
                cbuf[:, pp * NW: pp * NW + n].rearrange("p (c x) -> p c x", c=nch)),
                war=["dv", "ac"])
        with self.nc.Block() as blk:
            P.emit_into_block(blk, None, self.sems)
        cvstk.close()
        P = Prog(self.semc)
        wt = self.sbuf(stk, "w_wt", [128, NHU * D], F32)
        coef = self.sbuf(stk, "w_coef", [128, 320], F32)
        blkbuf = self.sbuf(stk, "w_blk", [128, 2 * 7 * 128], F32)
        comp = self.sbuf(stk, "w_comp", [128, 32 * 128], F32)
        scr0 = self.sbuf(stk, "w_scr0", [128, D], F32)
        scr1 = self.sbuf(stk, "w_scr1", [128, D], F32)
        scr0b = self.sbuf(stk, "w_scr0b", [128, D], F32)
        scr1b = self.sbuf(stk, "w_scr1b", [128, D], F32)
        bvec = self.sbuf(stk, "w_bvec", [128, NHU], F32)
        bscr = self.sbuf(stk, "w_bscr", [128, 2], F32)
        rowb = self.sbuf(stk, "w_rowb", [128, DH], BF16)
        tb16 = self.sbuf(stk, "w_tb16", [128, 128], BF16)
        pcmp = self.psum(stk, "w_pcmp", [128, 128], F32)
        pcmp2 = self.psum(stk, "w_pcmp2", [128, 128], F32)
        papp = self.psum(stk, "w_papp", [128, 512], F32)
        papp2 = self.psum(stk, "w_papp2", [128, 512], F32)
        pbv = self.psum(stk, "w_pbv", [128, 1], F32)
        ptr = self.psum(stk, "w_ptr", [128, 128], BF16)
        ptr2 = self.psum(stk, "w_ptr2", [128, 128], BF16)

        def compose_blocks(src_ap, order_rev):
            seed_k = 6 if order_rev else 0
            ks = list(range(5, -1, -1) if order_rev else range(1, 7))
            dmad = {}
            for pair in range(16):
                for half in range(2):
                    blk = pair * 2 + half
                    pp = blk % 2
                    bb = blkbuf[:, pp * 7 * 128:(pp + 1) * 7 * 128]
                    P.dma("qs", lambda e, blk=blk, bb=bb: e.dma_start(
                        bb[:].rearrange("p (k n) -> p k n", k=7), src_ap[blk]),
                        war=["pe", "dv"])
                    dmad[blk] = P.n("qs")
                st = {}
                for half in range(2):
                    blk = pair * 2 + half
                    bb = blkbuf[:, (blk % 2) * 7 * 128:((blk % 2) + 1) * 7 * 128]
                    P.op("vector", lambda e, bb=bb, blk=blk: e.tensor_copy(
                        comp[:, blk * 128:(blk + 1) * 128],
                        bb[:, seed_k * 128:(seed_k + 1) * 128]),
                        wait=[("qs", dmad[blk])], inc=["dv"])
                    st[half] = {"dv": P.n("dv"), "pe": 0}
                for k in ks:
                    for half in range(2):
                        blk = pair * 2 + half
                        bb = blkbuf[:, (blk % 2) * 7 * 128:((blk % 2) + 1) * 7 * 128]
                        pc = pcmp if half == 0 else pcmp2
                        P.op("tensor", lambda e, bb=bb, blk=blk, k=k, pc=pc: e.matmul(
                            pc[:], bb[:, k * 128:(k + 1) * 128],
                            comp[:, blk * 128:(blk + 1) * 128],
                            start=True, stop=True),
                            wait=[("dv", st[half]["dv"])], inc=["pe"])
                        st[half]["pe"] = P.n("pe")
                    for half in range(2):
                        blk = pair * 2 + half
                        pc = pcmp if half == 0 else pcmp2
                        P.op("vector", lambda e, blk=blk, pc=pc: e.tensor_copy(
                            comp[:, blk * 128:(blk + 1) * 128], pc[:]),
                            wait=[("pe", st[half]["pe"])], inc=["dv"])
                        st[half]["dv"] = P.n("dv")

        def apply_blocks():
            for blk in range(32):
                base = blk * D
                for h, pp in ((0, papp), (1, papp2)):
                    P.op("tensor", lambda e, blk=blk, base=base, h=h, pp=pp: e.matmul(
                        pp[:], comp[:, blk * 128:(blk + 1) * 128],
                        wt[:, base + h * 512: base + (h + 1) * 512], start=True, stop=True),
                        wait=["dv"], inc=["pe"])
                for h, pp in ((0, papp), (1, papp2)):
                    P.op("vector", lambda e, base=base, h=h, pp=pp: e.tensor_copy(
                        wt[:, base + h * 512: base + (h + 1) * 512], pp[:]),
                        wait=["pe"], inc=["dv"])
                P.op("tensor", lambda e: e.nop(), wait=["dv"])

        def apply_blocks_bvec():
            for blk in range(32):
                P.op("tensor", lambda e, blk=blk: e.matmul(
                    pbv[:], comp[:, blk * 128:(blk + 1) * 128], bvec[:, blk:blk + 1],
                    start=True, stop=True), wait=["dv"], inc=["pe"])
                P.op("vector", lambda e, blk=blk: e.tensor_copy(bvec[:, blk:blk + 1], pbv[:]),
                     wait=["pe"], inc=["dv"])
                P.op("tensor", lambda e: e.nop(), wait=["dv"])

        def vector_stages(with_bvec):
            scrs = [(scr0, scr1), (scr0b, scr1b)]
            dv_after = {}
            gi = 0
            for q in range(5):
                bit = 1 << q
                atiles = [t for t in range(32) if not (t & bit)]
                for pi, Ta in enumerate(atiles):
                    Tb = Ta + bit
                    cbase = (q * 16 + pi) * 4
                    ca0 = coef[:, cbase + 0:cbase + 1]
                    cb0 = coef[:, cbase + 1:cbase + 2]
                    ca1 = coef[:, cbase + 2:cbase + 3]
                    cb1 = coef[:, cbase + 3:cbase + 4]
                    a = wt[:, Ta * D:(Ta + 1) * D]
                    b = wt[:, Tb * D:(Tb + 1) * D]
                    s0, s1_ = scrs[gi % 2]
                    if pi == 0:
                        wv = ["dv"]
                    else:
                        wv = [("dv", dv_after[gi - 2])] if gi >= 2 else []
                    P.op("scalar", lambda e, b=b, cb0=cb0, s0=s0: e.activation(
                        s0[:], b, AF.Copy, scale=cb0), wait=wv, inc=["ac"])
                    P.op("scalar", lambda e, a=a, ca1=ca1, s1_=s1_: e.activation(
                        s1_[:], a, AF.Copy, scale=ca1), inc=["ac"])
                    ac_now = P.n("ac")
                    P.op("vector", lambda e, a=a, ca0=ca0, s0=s0: e.scalar_tensor_tensor(
                        a, a, ca0, s0[:], ALU.mult, ALU.add), wait=[("ac", ac_now)], inc=["dv"])
                    P.op("vector", lambda e, b=b, cb1=cb1, s1_=s1_: e.scalar_tensor_tensor(
                        b, b, cb1, s1_[:], ALU.mult, ALU.add), inc=["dv"])
                    dv_after[gi] = P.n("dv")
                    gi += 1
            if with_bvec:
                for q in range(5):
                    bit = 1 << q
                    atiles = [t for t in range(32) if not (t & bit)]
                    for pi, Ta in enumerate(atiles):
                        Tb = Ta + bit
                        cbase = (q * 16 + pi) * 4
                        ca0 = coef[:, cbase + 0:cbase + 1]
                        cb0 = coef[:, cbase + 1:cbase + 2]
                        ca1 = coef[:, cbase + 2:cbase + 3]
                        cb1 = coef[:, cbase + 3:cbase + 4]
                        va = bvec[:, Ta:Ta + 1]
                        vb = bvec[:, Tb:Tb + 1]
                        P.op("scalar", lambda e, vb=vb, cb0=cb0: e.activation(
                            bscr[:, 0:1], vb, AF.Copy, scale=cb0), wait=["dv"], inc=["ac"])
                        P.op("scalar", lambda e, va=va, ca1=ca1: e.activation(
                            bscr[:, 1:2], va, AF.Copy, scale=ca1), inc=["ac"])
                        P.op("vector", lambda e, va=va, ca0=ca0: e.scalar_tensor_tensor(
                            va, va, ca0, bscr[:, 0:1], ALU.mult, ALU.add), wait=["ac"], inc=["dv"])
                        P.op("vector", lambda e, vb=vb, cb1=cb1: e.scalar_tensor_tensor(
                            vb, vb, cb1, bscr[:, 1:2], ALU.mult, ALU.add), inc=["dv"])
                        P.op("scalar", lambda e: e.nop(), wait=["dv"])

        P.dma("qs", lambda e: e.dma_start(
            wt[:].rearrange("p (t d) -> p t d", t=NHU), d["wexpT"].rearrange("(t p) d -> p t d", p=128)),
            war=["dv", "qp"])
        P.dma("qs", lambda e: e.dma_start(coef[:], d["upC"][:]))
        P.dma("qs", lambda e: e.dma_start(bvec[:], d["bexp_f"][:]))
        wt_loaded = P.n("qs")
        P.op("vector", lambda e: e.tensor_copy(scr0[:, 0:1], wt[:, 0:1]),
             wait=[("qs", wt_loaded)], inc=["dv"])
        compose_blocks(d["upA"], order_rev=True)
        apply_blocks()
        apply_blocks_bvec()
        vector_stages(with_bvec=True)
        ub = self.sbuf(stk, "w_ub", [128, NHU], F32)
        P.dma("qs", lambda e: e.dma_start(ub[:], d["upbias_f"][:]))
        P.op("vector", lambda e: e.tensor_tensor(bvec[:], bvec[:], ub[:], ALU.add),
             wait=["qs", "dv"], inc=["dv"])
        P.dma("qp", lambda e: e.dma_start(d["bup_d"][:], bvec[:]), war=["dv"])
        for dc in range(NF):
            for t32 in range(NHU):
                pp = t32 % 2
                ptr_ = ptr if pp == 0 else ptr2
                P.op("vector", lambda e, dc=dc, t32=t32: e.tensor_copy(
                    tb16[:], wt[:, t32 * D + dc * 128: t32 * D + (dc + 1) * 128]),
                    wait=["dv", "pe"], inc=["dv"])
                P.op("tensor", lambda e, ptr_=ptr_: e.transpose(ptr_[:], tb16[:], g["ident"][:]),
                     wait=["dv"], inc=["pe"])
                P.op("vector", lambda e, t32=t32, ptr_=ptr_: e.tensor_copy(
                    rowb[:, t32 * 128:(t32 + 1) * 128], ptr_[:]), wait=["pe"], inc=["dv"])
            P.dma("qp", lambda e, dc=dc: e.dma_start(
                d["wup_d"][dc * 128:(dc + 1) * 128, :], rowb[:]), war=["dv"])
            P.op("vector", lambda e: e.tensor_copy(scr0[:, 0:1], scr0[:, 0:1]),
                 wait=[("qp", P.n("qp"))], inc=["dv"])

        P.dma("qs", lambda e: e.dma_start(
            wt[:].rearrange("p (t d) -> p t d", t=NHU), d["wcon"].rearrange("(t p) d -> p t d", p=128)),
            war=["dv"])
        P.dma("qs", lambda e: e.dma_start(coef[:], d["dnCoef"][:]))
        wt2_loaded = P.n("qs")
        P.op("vector", lambda e: e.tensor_copy(scr0[:, 0:1], wt[:, 0:1]),
             wait=[("qs", wt2_loaded)], inc=["dv"])
        vector_stages(with_bvec=False)
        compose_blocks(d["dnC"], order_rev=False)
        apply_blocks()
        for t32 in range(NHU):
            P.op("vector", lambda e, t32=t32: e.tensor_copy(
                rowb[:, 0:D], wt[:, t32 * D:(t32 + 1) * D]),
                wait=["dv", ("qp", P.n("qp"))], inc=["dv"])
            P.dma("qp", lambda e, t32=t32: e.dma_start(
                d["wdown_d"][t32 * 128:(t32 + 1) * 128, :], rowb[:, 0:D]), war=["dv"])
        for i, n in enumerate(("bv_bc", "bo_bc", "bcd_bc")):
            P.dma("qp", lambda e, i=i, n=n: e.dma_start(d["bias_bc_d"][i], g[n][:]), war=["dv"])
        P.op("gpsimd", lambda e: e.nop(), wait=[("qp", P.n("qp"))])

        with self.nc.Block() as blk:
            P.emit_into_block(blk, None, self.sems)
        stk.close()

    def phase_AB(self):
        for b in range(self.n_seq):
            self._ab_seq(b)

    AB_PARTS = ("A", "B1", "B2")

    def _ab_seq(self, b):
        nc, d, g = self.nc, self.d, self.g
        S, NCH = self.S, self.NCH
        NG = S // 512
        sstk = ExitStack()
        qT = self.sbuf(sstk, "ab_qT", [128, NF * S], BF16)
        kT = self.sbuf(sstk, "ab_kT", [128, NF * S], BF16)
        vx = self.sbuf(sstk, "ab_vx", [128, NCH * H * 65], BF16)
        gates = self.sbuf(sstk, "ab_gates", [128, NCH * EXP], F32)

        astk = ExitStack()
        wq = self.sbuf(astk, "ab_wq", [128, NF * D], BF16)
        wk = self.sbuf(astk, "ab_wk", [128, NF * D], BF16)
        wv = self.sbuf(astk, "ab_wv", [128, NF * D], BF16)
        wr = self.sbuf(astk, "ab_wr", [128, NF * EXP], F32)
        stg = self.sbuf(astk, "ab_stg", [128, 2 * D], F32)
        xt = self.sbuf(astk, "a_xt", [128, 2 * D], F32)
        xnT = self.sbuf(astk, "a_xnT", [128, NF * 512], BF16)
        xnT32 = self.sbuf(astk, "a_xnT32", [128, NF * 128], F32)
        xnb = self.sbuf(astk, "a_xnb2", [128, 2 * D], F32)
        scr2 = self.sbuf(astk, "a_scr2", [128, 512], F32)
        scr3 = self.sbuf(astk, "a_scr3", [128, 512], F32)
        st8 = self.sbuf(astk, "a_st8", [128, 8], F32)
        pr8 = self.sbuf(astk, "a_pr8", [128, 8], F32)
        ptA = self.psum(astk, "a_ptA", [128, 128], F32)
        ptB = self.psum(astk, "a_ptB", [128, 128], F32)
        pq = self.psum(astk, "a_pq", [128, 512], F32)
        pq2 = self.psum(astk, "a_pq2", [128, 512], F32)
        pr = self.psum(astk, "a_pr", [128, 8], F32)

        P = Prog(self.semc)
        for wsb, srcw in ((wq, d["wqT"]), (wk, d["wkT"]), (wv, d["wvT"])):
            for fi in range(NF):
                pp = fi % 2
                P.dma("qs", lambda e, fi=fi, pp=pp, srcw=srcw: e.dma_start(
                    stg[:, pp * D:(pp + 1) * D], srcw[fi * 128:(fi + 1) * 128, :]),
                    war=["dv"])
                P.op("vector", lambda e, fi=fi, pp=pp, wsb=wsb: e.tensor_copy(
                    wsb[:, fi * D:(fi + 1) * D], stg[:, pp * D:(pp + 1) * D]),
                    wait=["qs"], inc=["dv"])
        P.dma("qs", lambda e: e.dma_start(wr[:].rearrange(
            "p (f x) -> p f x", f=NF), d["wrT"].rearrange("(f p) x -> p f x", p=128)),
            war=["dv"])
        wr_loaded = P.n("qs")
        pq_free = {}
        SUM, SQ, MEAN, VAR, RSTD, NMR = 0, 1, 2, 3, 4, 5
        RSUM, RM1, RM2 = 12, 13, 14
        st8x = self.sbuf(astk, "a_st8x", [128, 16], F32)

        def stats(ch):
            par = (ch % 2) * 16
            pp = ch % 2
            xt_c = xt[:, pp * D:(pp + 1) * D]
            xnb_c = xnb[:, pp * D:(pp + 1) * D]
            C = lambda i: st8x[:, (ch % 2) * 6 + i:(ch % 2) * 6 + i + 1] if False else None
            base = (ch % 2) * 6
            sl = lambda i: st8x[:, base + i: base + i + 1]
            P.dma("qs", lambda e, b=b, ch=ch, xt_c=xt_c: e.dma_start(
                xt_c, d["x"][b, ch * 128:(ch + 1) * 128, :]),
                war=[("ac", ln_free.get(ch - 2, (0, 0))[0]),
                     ("dv", ln_free.get(ch - 2, (0, 0))[1])])
            qs_x = P.n("qs")
            P.op("scalar", lambda e, xt_c=xt_c, xnb_c=xnb_c: e.activation(
                xnb_c, xt_c, AF.Square, accum_out=sl(SQ)),
                wait=[("qs", qs_x), ("pe", trans_free.get(ch - 2, 0))], inc=["ac"])
            P.op("vector", lambda e, xt_c=xt_c: e.reduce_sum(
                sl(SUM), xt_c, axis=AX.X), wait=[("qs", qs_x)], inc=["dv"])
            P.op("vector", lambda e: e.tensor_scalar(
                sl(MEAN), sl(SUM), 1.0 / D, None, ALU.mult), inc=["dv"])
            P.op("vector", lambda e: e.tensor_scalar(
                sl(NMR), sl(MEAN), -1.0 / D, None, ALU.mult), inc=["dv"])
            P.op("vector", lambda e: e.tensor_scalar(
                sl(VAR), sl(SQ), 1.0 / D, None, ALU.mult), wait=["ac"], inc=["dv"])
            P.op("vector", lambda e: e.scalar_tensor_tensor(
                sl(VAR), sl(SUM), sl(NMR), sl(VAR), ALU.mult, ALU.add), inc=["dv"])
            dv_v = P.n("dv")
            P.op("scalar", lambda e: e.activation(
                sl(RSTD), sl(VAR), AF.Ln, bias=g["eps_ln"][:]),
                wait=[("dv", dv_v)], inc=["ac"])
            P.op("scalar", lambda e: e.activation(
                sl(RSTD), sl(RSTD), AF.Exp, scale=-0.5), inc=["ac"])
            P.op("vector", lambda e: e.tensor_scalar(
                sl(NMR), sl(MEAN), -1.0, None, ALU.mult), inc=["dv"])
            return (P.n("ac"), P.n("dv"))

        ln_free = {}
        trans_free = {}
        stat_done = {}

        def transform(ch, gg, ct):
            pp = ch % 2
            base = (ch % 2) * 6
            sl = lambda i: st8x[:, base + i: base + i + 1]
            xt_c = xt[:, pp * D:(pp + 1) * D]
            xnb_c = xnb[:, pp * D:(pp + 1) * D]
            ac_v, dv_v = stat_done[ch]
            P.op("vector", lambda e, xt_c=xt_c, xnb_c=xnb_c: e.tensor_scalar(
                xnb_c, xt_c, sl(NMR), sl(RSTD), ALU.add, ALU.mult),
                wait=[("ac", ac_v), ("dv", dv_v), "pe"], inc=["dv"])
            xn_dv = P.n("dv")
            cp_dv = {0: xn_dv, 1: xn_dv}
            for f in range(NF):
                pt = ptA if f % 2 == 0 else ptB
                P.op("tensor", lambda e, f=f, pt=pt, xnb_c=xnb_c: e.transpose(
                    pt[:], xnb_c[:, f * 128:(f + 1) * 128], g["ident32"][:]),
                    wait=[("dv", cp_dv[f % 2])], inc=["pe"])
                P.op("vector", lambda e, f=f, ct=ct, pt=pt: e.tensor_copy(
                    xnT[:, f * 512 + ct * 128: f * 512 + (ct + 1) * 128], pt[:]),
                    wait=["pe", "qp"], inc=["dv"])
                P.op("vector", lambda e, f=f, pt=pt: e.tensor_copy(
                    xnT32[:, f * 128:(f + 1) * 128], pt[:]), inc=["dv"])
                cp_dv[f % 2] = P.n("dv")
            trans_free[ch] = P.n("pe")
            xnT32_dv = P.n("dv")
            for fi in range(NF):
                P.op("tensor", lambda e, fi=fi: e.matmul(
                    pr[:], xnT32[:, fi * 128:(fi + 1) * 128],
                    wr[:, fi * EXP:(fi + 1) * EXP],
                    start=(fi == 0), stop=(fi == NF - 1)),
                    wait=([("dv", max(xnT32_dv, rt_free[0])), ("qs", wr_loaded),
                           ("ac", rt_ac[0])] if fi == 0 else []),
                    inc=(["pe"] if fi == NF - 1 else []))
            pr_pe = P.n("pe")
            P.op("vector", lambda e: e.reduce_max(
                st8x[:, RM1:RM1 + 1], pr[:], axis=AX.X, negate=True),
                wait=[("pe", pr_pe)], inc=["dv"])
            P.op("scalar", lambda e: e.activation(
                pr8[:], pr[:], AF.Exp, bias=st8x[:, RM1:RM1 + 1],
                accum_out=st8x[:, RSUM:RSUM + 1]), wait=["dv"], inc=["ac"])
            rt_ac[0] = P.n("ac")
            P.op("vector", lambda e: e.reciprocal(
                st8x[:, RM2:RM2 + 1], st8x[:, RSUM:RSUM + 1]), wait=["ac"], inc=["dv"])
            P.op("vector", lambda e: e.tensor_scalar(
                pr8[:], pr8[:], st8x[:, RM2:RM2 + 1], None, ALU.mult), inc=["dv"])
            P.op("vector", lambda e: e.reduce_max(
                st8x[:, RM1:RM1 + 1], pr8[:], axis=AX.X), inc=["dv"])
            P.op("vector", lambda e: e.tensor_scalar(
                scr2[:, 0:8], pr8[:], st8x[:, RM1:RM1 + 1], None, ALU.is_ge), inc=["dv"])
            P.op("vector", lambda e: e.scalar_tensor_tensor(
                scr2[:, 0:8], scr2[:, 0:8], -1e30, pr8[:], ALU.mult, ALU.add), inc=["dv"])
            P.op("vector", lambda e: e.reduce_max(
                st8x[:, RM2:RM2 + 1], scr2[:, 0:8], axis=AX.X), inc=["dv"])
            P.op("vector", lambda e, ch=ch: e.tensor_scalar(
                gates[:, ch * EXP:(ch + 1) * EXP], pr8[:], st8x[:, RM2:RM2 + 1], None,
                ALU.is_ge), inc=["dv"])
            P.op("vector", lambda e, ch=ch: e.tensor_mul(
                gates[:, ch * EXP:(ch + 1) * EXP], gates[:, ch * EXP:(ch + 1) * EXP],
                pr8[:]), inc=["dv"])
            rt_free[0] = P.n("dv")
            ln_free[ch] = (P.n("ac"), P.n("dv"))

        rt_free = {0: 0}
        rt_ac = {0: 0}
        stat_done[0] = stats(0)
        for gg in range(NG):
            for ct in range(4):
                ch = gg * 4 + ct
                if ch + 1 < NCH:
                    stat_done[ch + 1] = stats(ch + 1)
                transform(ch, gg, ct)
            t0 = gg * 512
            P.dma("qp", lambda e, b=b, t0=t0: e.dma_start(
                d["xnT_d"][b].rearrange("(f p) t -> p f t", p=128)[:, :, t0:t0 + 512],
                xnT[:].rearrange("p (f t) -> p f t", f=NF)), war=["dv"])
            for wsb, dstT, bias, isq in ((wq, qT, g["bq_f"], True), (wk, kT, g["bk_f"], True)):
                for fo in range(NF):
                    pqx = pq if fo % 2 == 0 else pq2
                    fa = pq_free.get(id(pqx), ["dv", "ac"])
                    for fi in range(NF):
                        P.op("tensor", lambda e, fo=fo, fi=fi, wsb=wsb, pqx=pqx: e.matmul(
                            pqx[:], wsb[:, fi * D + fo * 128: fi * D + (fo + 1) * 128],
                            xnT[:, fi * 512:(fi + 1) * 512],
                            start=(fi == 0), stop=(fi == NF - 1)),
                            wait=(fa if fi == 0 else []),
                            inc=(["pe"] if fi == NF - 1 else []))
                    P.op("vector", lambda e, fo=fo, bias=bias, pqx=pqx: e.tensor_scalar(
                        scr2[:], pqx[:], bias[:, fo:fo + 1], 0.0, ALU.add, ALU.min),
                        wait=["pe"], inc=["dv"])
                    P.op("vector", lambda e, fo=fo, bias=bias, pqx=pqx: e.tensor_scalar(
                        scr3[:], pqx[:], bias[:, fo:fo + 1], 0.0, ALU.add, ALU.max),
                        inc=["dv"])
                    pq_free[id(pqx)] = [("dv", P.n("dv"))]
                    P.op("scalar", lambda e: e.activation(
                        scr2[:], scr2[:], AF.Exp), wait=["dv"], inc=["ac"])
                    P.op("vector", lambda e, fo=fo, t0=t0, dstT=dstT: e.tensor_add(
                        dstT[:, fo * S + t0: fo * S + t0 + 512], scr2[:], scr3[:]),
                        wait=["ac"], inc=["dv"])
            for ct in range(4):
                ch = gg * 4 + ct
                for oc in range(2):
                    pvx = pq if oc == 0 else pq2
                    fa = pq_free.get(id(pvx), ["dv", "ac"])
                    for fi in range(NF):
                        P.op("tensor", lambda e, ct=ct, oc=oc, fi=fi, pvx=pvx: e.matmul(
                            pvx[:], xnT[:, fi * 512 + ct * 128: fi * 512 + (ct + 1) * 128],
                            wv[:, fi * D + oc * 512: fi * D + (oc + 1) * 512],
                            start=(fi == 0), stop=(fi == NF - 1)),
                            wait=(fa if fi == 0 else []),
                            inc=(["pe"] if fi == NF - 1 else []))
                    vdst = vx[:, ch * H * 65 + oc * 8 * 65: ch * H * 65 + (oc + 1) * 8 * 65]
                    vdst = vdst.rearrange("p (a j) -> p a j", j=65)[:, :, 0:64]
                    P.op("vector", lambda e, oc=oc, vdst=vdst, pvx=pvx: e.scalar_tensor_tensor(
                        vdst, pvx[:].rearrange("p (a j) -> p a j", j=64), 1.0,
                        g["bv_bc"][:, oc * 512:(oc + 1) * 512].rearrange("p (a j) -> p a j", j=64),
                        ALU.mult, ALU.add), wait=["pe"], inc=["dv"])
                    pq_free[id(pvx)] = [("dv", P.n("dv"))]
        P.op("vector", lambda e: e.memset(
            vx[:].rearrange("p (c j) -> p c j", j=65)[:, :, 64:65], 1.0), inc=["dv"])
        P.dma("qp", lambda e, b=b: e.dma_start(d["gates_d"][b], gates[:]), war=["dv"])
        with nc.Block() as blk:
            P.emit_into_block(blk, None, self.sems)
        astk.close()
        o_s = self.sbuf(sstk, "ab_o", [128, NCH * D], BF16)
        if "B1" not in self.AB_PARTS:
            sstk.close()
            return

        bstk = ExitStack()
        atm = self.sbuf(bstk, "b_atm", [128, H * 128], BF16)
        ktok = self.sbuf(bstk, "b_ktok", [128, D], BF16)
        stS = self.sbuf(bstk, "b_stS", [128, 8 * 128], BF16)
        stZ = self.sbuf(bstk, "b_stZ", [128, 16], BF16)
        dsc = self.sbuf(bstk, "b_dsc", [128, 2 * H], F32)
        p_at = self.psum(bstk, "b_pat", [128, 2048], F32)
        p_stS = self.psum(bstk, "b_pstS", [128, 512], F32)
        p_stZ = self.psum(bstk, "b_pstZ", [128, 8], F32)
        p_tk = self.psum(bstk, "b_ptk", [128, 128], BF16)
        p_tk2 = self.psum(bstk, "b_ptk2", [128, 128], BF16)

        LV = getattr(self, "B1_LEVEL", 5)
        SLOT = [(h % 2) * 8 + (h // 2) for h in range(H)]
        P = Prog(self.semc)
        P.op("vector", lambda e: e.memset(p_stS[:], 0.0), inc=["dv"])
        P.op("vector", lambda e: e.memset(p_stZ[:], 0.0), inc=["dv"])
        P.op("vector", lambda e: e.memset(stS[:], 0.0), inc=["dv"])
        P.op("vector", lambda e: e.memset(stZ[:], 0.0), inc=["dv"])
        for ch in range(NCH):
            ts = ch * 128
            tk_free = {0: ("dv", P.n("dv")), 1: ("dv", P.n("dv"))}
            for f in range(NF):
                ptk = p_tk if f % 2 == 0 else p_tk2
                P.op("tensor", lambda e, f=f, ts=ts, ptk=ptk: e.transpose(
                    ptk[:], kT[:, f * S + ts: f * S + ts + 128], g["ident"][:]),
                    wait=([("ac", P.n("ac")), tk_free[f % 2]] if f <= 1 else [tk_free[f % 2]]),
                    inc=["pe"])
                P.op("vector", lambda e, f=f, ptk=ptk: e.tensor_copy(
                    ktok[:, f * 128:(f + 1) * 128], ptk[:]), wait=["pe"], inc=["dv"])
                tk_free[f % 2] = ("dv", P.n("dv"))
            if LV < 2:
                continue
            for sl in range(H):
                h = (sl // 8) + 2 * (sl % 8)
                f, po = h // 2, 64 * (h % 2)
                P.op("tensor", lambda e, sl=sl, f=f, po=po, ts=ts: e.matmul(
                    p_at[:, sl * 128:(sl + 1) * 128],
                    kT[po:po + 64, f * S + ts: f * S + ts + 128],
                    qT[po:po + 64, f * S + ts: f * S + ts + 128],
                    start=(sl % 4 == 0), stop=(sl % 4 == 3)),
                    wait=(["dv", "ac"] if sl == 0 else []), inc=(["pe"] if sl == H - 1 else []))
            for bk in range(4):
                P.op("vector", lambda e, bk=bk: e.tensor_tensor(
                    atm[:, bk * 512:(bk + 1) * 512], p_at[:, bk * 512:(bk + 1) * 512],
                    g["tri16"][:, bk * 512:(bk + 1) * 512], ALU.mult),
                    wait=(["pe"] if bk == 0 else []), inc=["dv"])
            if LV < 3:
                continue
            if LV >= 5 and ch > 0:
                sv = stS[:].rearrange("p (j c) -> p j c", c=128)
                P.op("vector", lambda e, sv=sv: e.tensor_copy(
                    sv[0:64, :, 0:64], p_stS[0:64, :].rearrange("p (j c) -> p j c", c=64)),
                    wait=["pe"], inc=["dv"])
                P.op("scalar", lambda e, sv=sv: e.activation(
                    sv[64:128, :, 64:128], p_stS[64:128, :].rearrange("p (j c) -> p j c", c=64),
                    AF.Copy), wait=["pe"], inc=["ac"])
                zv = stZ[:].rearrange("p (j c) -> p j c", c=2)
                P.op("vector", lambda e, zv=zv: e.tensor_copy(
                    zv[0:64, :, 0:1], p_stZ[0:64, :].rearrange("p (j c) -> p j c", c=1)),
                    inc=["dv"])
                P.op("scalar", lambda e, zv=zv: e.activation(
                    zv[64:128, :, 1:2], p_stZ[64:128, :].rearrange("p (j c) -> p j c", c=1),
                    AF.Copy), inc=["ac"])
            intra_stop = (ch == 0 or LV < 5)
            for h in range(H):
                asl = SLOT[h] * 128
                vsl = vx[:, ch * H * 65 + h * 65: ch * H * 65 + h * 65 + 64]
                P.op("tensor", lambda e, h=h, vsl=vsl, asl=asl, st_=intra_stop: e.matmul(
                    p_at[:, h * 64:(h + 1) * 64], atm[:, asl: asl + 128], vsl,
                    start=(h % 8 == 0), stop=(st_ and h % 8 == 7)),
                    wait=(["dv"] if h == 0 else []))
                P.op("tensor", lambda e, h=h, asl=asl, st_=intra_stop: e.matmul(
                    p_at[:, 2032 + h: 2033 + h], atm[:, asl: asl + 128],
                    g["ones_c"][:], start=(h == 0), stop=(st_ and h == H - 1)),
                    inc=(["pe"] if (intra_stop and h == H - 1) else []))
            if LV >= 5 and ch > 0:
                for j in range(8):
                    P.op("tensor", lambda e, j=j, ts=ts: e.matmul(
                        p_at[:, j * 128:(j + 1) * 128],
                        qT[:, j * S + ts: j * S + ts + 128],
                        stS[:, j * 128:(j + 1) * 128],
                        start=False, stop=(j % 4 == 3)),
                        wait=(["ac"] if j == 0 else []))
                    P.op("tensor", lambda e, j=j, ts=ts: e.matmul(
                        p_at[:, 2032 + 2 * j: 2034 + 2 * j],
                        qT[:, j * S + ts: j * S + ts + 128],
                        stZ[:, 2 * j: 2 * j + 2], start=False, stop=(j == 7)),
                        inc=(["pe"] if j == 7 else []))
            if LV < 4:
                pass
            for h in range(H) if LV >= 4 else []:
                vsl = vx[:, ch * H * 65 + h * 65: ch * H * 65 + h * 65 + 64]
                po, sc = 64 * (h % 2), 64 * (h // 2)
                P.op("tensor", lambda e, h=h, vsl=vsl, po=po, sc=sc: e.matmul(
                    p_stS[po:po + 64, sc:sc + 64], ktok[:, h * 64:(h + 1) * 64], vsl,
                    start=False, stop=False, skip_group_check=True),
                    wait=(["ac"] if h == 0 else []))
                P.op("tensor", lambda e, h=h, po=po: e.matmul(
                    p_stZ[po:po + 64, h // 2: h // 2 + 1], ktok[:, h * 64:(h + 1) * 64],
                    g["ones_c"][:], start=False, stop=False,
                    skip_group_check=True), inc=(["pe"] if h == H - 1 else []))
            dens = p_at[:, 2032:2048]
            P.op("vector", lambda e, dens=dens: e.tensor_scalar(
                dsc[:, 0:H], dens, ATT_EPS, None, ALU.add), wait=["pe"], inc=["dv"])
            P.op("vector", lambda e: e.reciprocal(dsc[:, H:2 * H], dsc[:, 0:H]), inc=["dv"])
            for h in range(H):
                P.op("scalar", lambda e, h=h, ch=ch: e.activation(
                    o_s[:, ch * D + h * 64: ch * D + (h + 1) * 64],
                    p_at[:, h * 64:(h + 1) * 64], AF.Copy,
                    scale=dsc[:, H + h:H + h + 1]), wait=["dv"], inc=["ac"])
        with nc.Block() as blk:
            P.emit_into_block(blk, None, self.sems)
        bstk.close()
        if "B2" not in self.AB_PARTS:
            sstk.close()
            return

        cstk = ExitStack()
        wo = self.sbuf(cstk, "b2_wo", [128, NF * D], BF16)
        stg2 = self.sbuf(cstk, "b2_stg", [128, 2 * D], F32)
        oT = self.sbuf(cstk, "b2_oT", [128, D], BF16)
        xt2 = self.sbuf(cstk, "b2_xt", [128, 2 * D], F32)
        pout = self.sbuf(cstk, "b2_pout", [128, 2 * D], F32)
        p_tr1 = self.psum(cstk, "b2_ptr1", [128, 128], BF16)
        p_tr2 = self.psum(cstk, "b2_ptr2", [128, 128], BF16)
        p_o = self.psum(cstk, "b2_po", [128, D], F32)

        P = Prog(self.semc)
        for fi in range(NF):
            pp = fi % 2
            P.dma("qs", lambda e, fi=fi, pp=pp: e.dma_start(
                stg2[:, pp * D:(pp + 1) * D], d["woT"][fi * 128:(fi + 1) * 128, :]),
                war=["dv"])
            P.op("vector", lambda e, fi=fi, pp=pp: e.tensor_copy(
                wo[:, fi * D:(fi + 1) * D], stg2[:, pp * D:(pp + 1) * D]),
                wait=["qs"], inc=["dv"])
        for ch in range(NCH):
            pp = ch % 2
            P.dma("qs", lambda e, b=b, ch=ch, pp=pp: e.dma_start(
                xt2[:, pp * D:(pp + 1) * D], d["x"][b, ch * 128:(ch + 1) * 128, :]),
                war=["dv"])
            qs_x = P.n("qs")
            for f in range(NF):
                pt = p_tr1 if f % 2 == 0 else p_tr2
                P.op("tensor", lambda e, f=f, ch=ch, pt=pt: e.transpose(
                    pt[:], o_s[:, ch * D + f * 128: ch * D + (f + 1) * 128], g["ident"][:]),
                    wait=["dv"], inc=["pe"])
                P.op("vector", lambda e, f=f, pt=pt: e.tensor_copy(
                    oT[:, f * 128:(f + 1) * 128], pt[:]), wait=["pe"], inc=["dv"])
            for dc in range(2):
                for fe in range(NF):
                    P.op("tensor", lambda e, dc=dc, fe=fe: e.matmul(
                        p_o[:, dc * 512:(dc + 1) * 512], oT[:, fe * 128:(fe + 1) * 128],
                        wo[:, fe * D + dc * 512: fe * D + (dc + 1) * 512],
                        start=(fe == 0), stop=(fe == NF - 1)),
                        wait=(["dv"] if fe == 0 else []),
                        inc=(["pe"] if fe == NF - 1 else []))
            P.op("vector", lambda e, pp=pp: e.scalar_tensor_tensor(
                pout[:, pp * D:(pp + 1) * D], p_o[:], 1.0, xt2[:, pp * D:(pp + 1) * D],
                ALU.mult, ALU.add), wait=["pe", ("qs", qs_x), "qp"], inc=["dv"])
            P.op("vector", lambda e, pp=pp: e.tensor_tensor(
                pout[:, pp * D:(pp + 1) * D], pout[:, pp * D:(pp + 1) * D],
                g["bo_bc"][:], ALU.add), inc=["dv"])
            P.op("tensor", lambda e: e.nop(), wait=["dv"])
            P.dma("qp", lambda e, b=b, ch=ch, pp=pp: e.dma_start(
                d["partial_d"][b, ch * 128:(ch + 1) * 128, :], pout[:, pp * D:(pp + 1) * D]),
                war=["dv"])
        P.op("gpsimd", lambda e: e.nop(), wait=["qp"])
        with nc.Block() as blk:
            P.emit_into_block(blk, None, self.sems)
        cstk.close()
        sstk.close()

    def phase_C(self):
        for b in range(self.n_seq):
            self._c_seq(b)

    def _c_seq(self, b):
        nc, d, g = self.nc, self.d, self.g
        S, NCH = self.S, self.NCH
        NTG = S // 256
        stk = ExitStack()
        gts = self.sbuf(stk, "c_gts", [128, NCH * EXP], F32)
        xnT = self.sbuf(stk, "c_xnT", [128, 2 * NF * 256], BF16)
        accs = self.sbuf(stk, "c_accs", [128, 2 * 2 * D], F32)
        w1sb = [self.sbuf(stk, "c_w1", [128, NF * MH], BF16) for _ in range(2)]
        w3sb = [self.sbuf(stk, "c_w3", [128, NF * MH], BF16) for _ in range(2)]
        w2sb = [self.sbuf(stk, "c_w2", [128, NHT * D], BF16) for _ in range(2)]
        s1 = self.sbuf(stk, "c_s1", [128, 2 * 256], F32)
        gsb = self.sbuf(stk, "c_gsb", [128, 2 * 256], BF16)
        xpt = self.sbuf(stk, "c_xpt", [128, 2 * D], F32)
        x1t = self.sbuf(stk, "c_x1t", [128, 2 * D], F32)
        p_h1 = [self.psum(stk, "c_ph1", [128, 256], F32) for _ in range(2)]
        p_h3 = [self.psum(stk, "c_ph3", [128, 256], F32) for _ in range(2)]
        p_y0 = self.psum(stk, "c_py0", [128, D], F32)
        p_y1 = self.psum(stk, "c_py1", [128, D], F32)

        P = Prog(self.semc)
        P.dma("qs", lambda e: e.dma_start(gts[:], d["gates_d"][b]), war=["dv", "pe"])
        qs_base = P.n("qs")
        xnT_v = xnT[:].rearrange("p (u f t) -> p u f t", u=2, f=NF)
        accs_v = accs[:].rearrange("p (u c x) -> p u c x", u=2, c=2)

        def load_weights(e_, war):
            up = e_ % 2
            P.dma("qs", lambda e, e_=e_, up=up: e.dma_start(
                w1sb[up][:].rearrange("p (f h) -> p f h", f=NF),
                d["w1b_d"][e_].rearrange("(f p) h -> p f h", p=128)), war=war)
            P.dma("qs", lambda e, e_=e_, up=up: e.dma_start(
                w3sb[up][:].rearrange("p (f h) -> p f h", f=NF),
                d["w3b_d"][e_].rearrange("(f p) h -> p f h", p=128)))
            P.dma("qs", lambda e, e_=e_, up=up: e.dma_start(
                w2sb[up][:].rearrange("p (i x) -> p i x", i=NHT),
                d["w2b_d"][e_].rearrange("(i p) x -> p i x", p=128)))
            return P.n("qs")

        def emit_up(e_, tg, ht, ready_qs, free_at, pp):
            up = e_ % 2
            for pi, (ph, wsb) in enumerate(((p_h1[pp], w1sb[up]), (p_h3[pp], w3sb[up]))):
                for fi in range(NF):
                    w0 = []
                    if pi == 0 and fi == 0:
                        w0 = list(free_at) + [("qs", ready_qs)]
                    P.op("tensor", lambda e, tg=tg, ht=ht, fi=fi, ph=ph, wsb=wsb: e.matmul(
                        ph[:], wsb[:, fi * MH + ht * 128: fi * MH + (ht + 1) * 128],
                        xnT_v[:, tg % 2, fi, :],
                        start=(fi == 0), stop=(fi == NF - 1)),
                        wait=w0, inc=(["pe"] if fi == NF - 1 else []))
            pe_now = P.n("pe")
            P.op("scalar", lambda e, pp=pp: e.activation(
                s1[:, pp * 256:(pp + 1) * 256], p_h1[pp][:], AF.Silu),
                wait=[("pe", pe_now)], inc=["ac"])
            ac_now = P.n("ac")
            P.op("vector", lambda e, pp=pp: e.tensor_tensor(
                gsb[:, pp * 256:(pp + 1) * 256], s1[:, pp * 256:(pp + 1) * 256],
                p_h3[pp][:], ALU.mult), wait=[("ac", ac_now)], inc=["dv"])
            return [("ac", ac_now), ("dv", P.n("dv"))]

        def emit_y(e_, tg, ht, dv_need, pp, stt_guard=0):
            up = e_ % 2
            for sub, py in ((0, p_y0), (1, p_y1)):
                for dc in range(2):
                    P.op("tensor", lambda e, sub=sub, dc=dc, ht=ht, py=py, pp=pp, up=up: e.matmul(
                        py[:, dc * 512:(dc + 1) * 512],
                        gsb[:, pp * 256 + sub * 128: pp * 256 + sub * 128 + 128],
                        w2sb[up][:, ht * D + dc * 512: ht * D + (dc + 1) * 512],
                        start=(ht == 0), stop=(ht == NHT - 1)),
                        wait=([("dv", max(dv_need, stt_guard))] if (dc == 0 and sub == 0) else []),
                        inc=(["pe"] if (ht == NHT - 1 and sub == 1 and dc == 1) else []))

        wq_ready = {0: load_weights(0, ["pe", "dv"])}
        xnT_ready = {}
        xnT_free = {}
        pt_free = {0: 0}
        x1_stored = {0: 0}
        acc_stored = {}
        pe_after_expert = {}
        for e_ in range(EXP):
            done_at = {}
            stt_dv = {}
            NU = NTG * NHT

            def tg_loads(tg):
                war = [("pe", xnT_free.get((e_, tg - 2), xnT_free.get((e_ - 1, max(NTG - 2 + (tg % 2), 0)), 0)))]
                P.dma("qs", lambda e, b=b, tg=tg: e.dma_start(
                    xnT_v[:, tg % 2], d["xnT_d"][b].rearrange(
                        "(f p) t -> p f t", p=128)[:, :, tg * 256:(tg + 1) * 256]),
                    war=war)
                if e_ > 0:
                    qp_need = max(acc_stored.get((e_ - 1, tg), 0),
                                  acc_stored.get((e_, tg - 2), 0))
                    P.dma("qs", lambda e, b=b, tg=tg: e.dma_start(
                        accs_v[:, tg % 2], d["acc_d"][b, tg * 256:(tg + 1) * 256, :]
                        .rearrange("(c p) x -> p c x", p=128)),
                        war=[("qp", qp_need)])
                xnT_ready[tg] = P.n("qs")

            def tg_epilogue(tg):
                xnT_free[(e_, tg)] = P.n("pe")
                qs_need = xnT_ready[tg]
                if e_ == EXP - 1:
                    P.dma("qs", lambda e, b=b, tg=tg: e.dma_start(
                        xpt[:].rearrange("p (c x) -> p c x", c=2),
                        d["partial_d"][b, tg * 256:(tg + 1) * 256, :]
                        .rearrange("(c p) x -> p c x", p=128)), war=[("dv", pt_free[0])])
                    qs_need = P.n("qs")
                for sub, py in ((0, p_y0), (1, p_y1)):
                    ch = tg * 2 + sub
                    gcol = gts[:, ch * EXP + e_: ch * EXP + e_ + 1]
                    asl = accs_v[:, tg % 2, sub, :]
                    if e_ == 0:
                        P.op("vector", lambda e, asl=asl, py=py, gcol=gcol: e.tensor_scalar(
                            asl, py[:], gcol, None, ALU.mult),
                            wait=["pe", ("qs", qs_base),
                                  ("qp", acc_stored.get((e_, tg - 2), 0))], inc=["dv"])
                    else:
                        P.op("vector", lambda e, asl=asl, py=py, gcol=gcol: e.scalar_tensor_tensor(
                            asl, py[:], gcol, asl, ALU.mult, ALU.add),
                            wait=["pe", ("qs", qs_need),
                                  ("qp", acc_stored.get((e_, tg - 2), 0))], inc=["dv"])
                    if e_ == EXP - 1:
                        P.op("vector", lambda e, asl=asl, sub=sub: e.tensor_tensor(
                            x1t[:, sub * D:(sub + 1) * D], asl,
                            xpt[:, sub * D:(sub + 1) * D], ALU.add),
                            wait=[("qp", x1_stored[0])], inc=["dv"])
                stt_dv[tg] = P.n("dv")
                if e_ == EXP - 1:
                    pt_free[0] = P.n("dv")
                    P.dma("qp", lambda e, b=b, tg=tg: e.dma_start(
                        d["x1_d"][b, tg * 256:(tg + 1) * 256, :].rearrange(
                            "(c p) x -> p c x", p=128),
                        x1t[:].rearrange("p (c x) -> p c x", c=2)), war=["dv"])
                    x1_stored[0] = P.n("qp")
                else:
                    P.dma("qp", lambda e, b=b, tg=tg: e.dma_start(
                        d["acc_d"][b, tg * 256:(tg + 1) * 256, :].rearrange(
                            "(c p) x -> p c x", p=128), accs_v[:, tg % 2]), war=["dv"])
                acc_stored[(e_, tg)] = P.n("qp")

            for u in range(NU + 1):
                if u == NHT and e_ + 1 < EXP:
                    wq_ready[e_ + 1] = load_weights(
                        e_ + 1, [("pe", pe_after_expert.get(e_ - 1, 0))])
                if u < NU:
                    tg, ht = divmod(u, NHT)
                    if ht == 0:
                        tg_loads(tg)
                    if u - 2 >= 0:
                        free_at = done_at[u - 2]
                    else:
                        free_at = [("dv", P.n("dv")), ("ac", P.n("ac"))]
                    done_at[u] = emit_up(e_, tg, ht,
                                         max(wq_ready[e_], xnT_ready[tg]), free_at,
                                         u % 2)
                if u > 0:
                    ptg, pht = divmod(u - 1, NHT)
                    emit_y(e_, ptg, pht, dict(done_at[u - 1])["dv"], (u - 1) % 2,
                           stt_dv.get(ptg - 1, 0) if pht == 0 else 0)
                    if pht == NHT - 1:
                        tg_epilogue(ptg)
            pe_after_expert[e_] = P.n("pe")
        P.op("gpsimd", lambda e: e.nop(), wait=["qp"])
        with nc.Block() as blk:
            P.emit_into_block(blk, None, self.sems)
        stk.close()

    def phase_D(self):
        for b in range(self.n_seq):
            self._d_seq(b)

    def _d_seq(self, b):
        nc, d, g = self.nc, self.d, self.g
        S, NCH = self.S, self.NCH
        NG = S // 512
        sstk = ExitStack()
        qT = self.sbuf(sstk, "d_qT", [128, NF * S], BF16)
        kT = self.sbuf(sstk, "d_kT", [128, NF * S], BF16)
        vx = self.sbuf(sstk, "d_vx", [128, NCH * H * 65], BF16)

        wstk = ExitStack()
        wQ = self.sbuf(wstk, "d_wQ", [128, NF * D], BF16)
        wK = self.sbuf(wstk, "d_wK", [128, NF * D], BF16)
        wV = self.sbuf(wstk, "d_wV", [128, NF * D], BF16)
        stg = self.sbuf(wstk, "d_stg", [128, 2 * D], F32)
        xt = self.sbuf(wstk, "d_xt", [128, 2 * D], F32)
        xnb = self.sbuf(wstk, "d_xnb", [128, D], BF16)
        xnT = self.sbuf(wstk, "d_xnT", [128, NF * 512], BF16)
        scr = self.sbuf(wstk, "d_scr", [128, D], F32)
        st8 = self.sbuf(wstk, "d_st8", [128, 8], F32)
        ptA = self.psum(wstk, "d_ptA", [128, 128], BF16)
        ptB = self.psum(wstk, "d_ptB", [128, 128], BF16)
        pq = self.psum(wstk, "d_pq", [128, 512], F32)
        pq2 = self.psum(wstk, "d_pq2", [128, 512], F32)

        P = Prog(self.semc)
        pq_free = {}
        for wsb, src in ((wQ, d["wQ"]), (wK, d["wK"]), (wV, d["wV"])):
            for fi in range(NF):
                pp = fi % 2
                P.dma("qs", lambda e, fi=fi, pp=pp, src=src: e.dma_start(
                    stg[:, pp * D:(pp + 1) * D], src[fi * 128:(fi + 1) * 128, :]),
                    war=["dv"])
                P.op("vector", lambda e, fi=fi, pp=pp, wsb=wsb: e.tensor_copy(
                    wsb[:, fi * D:(fi + 1) * D], stg[:, pp * D:(pp + 1) * D]),
                    wait=["qs"], inc=["dv"])
        SUM, SQ, MEAN, VAR, RSTD, NMR = range(6)
        st8x = self.sbuf(wstk, "d_st8x", [128, 12], F32)
        xnb2 = self.sbuf(wstk, "d_xnb2", [128, 2 * D], F32)
        ln_free = {}
        trans_free = {}
        stat_done = {}

        def stats(ch):
            pp = ch % 2
            base = pp * 6
            sl = lambda i: st8x[:, base + i: base + i + 1]
            xt_c = xt[:, pp * D:(pp + 1) * D]
            xnb_c = xnb2[:, pp * D:(pp + 1) * D]
            P.dma("qs", lambda e, b=b, ch=ch, xt_c=xt_c: e.dma_start(
                xt_c, d["x1_d"][b, ch * 128:(ch + 1) * 128, :]),
                war=[("ac", ln_free.get(ch - 2, (0, 0))[0]),
                     ("dv", ln_free.get(ch - 2, (0, 0))[1])])
            qs_x = P.n("qs")
            P.op("scalar", lambda e, xt_c=xt_c, xnb_c=xnb_c: e.activation(
                xnb_c, xt_c, AF.Square, accum_out=sl(SQ)),
                wait=[("qs", qs_x), ("pe", trans_free.get(ch - 2, 0))], inc=["ac"])
            P.op("vector", lambda e, xt_c=xt_c: e.reduce_sum(
                sl(SUM), xt_c, axis=AX.X), wait=[("qs", qs_x)], inc=["dv"])
            P.op("vector", lambda e: e.tensor_scalar(
                sl(MEAN), sl(SUM), 1.0 / D, None, ALU.mult), inc=["dv"])
            P.op("vector", lambda e: e.tensor_scalar(
                sl(NMR), sl(MEAN), -1.0 / D, None, ALU.mult), inc=["dv"])
            P.op("vector", lambda e: e.tensor_scalar(
                sl(VAR), sl(SQ), 1.0 / D, None, ALU.mult), wait=["ac"], inc=["dv"])
            P.op("vector", lambda e: e.scalar_tensor_tensor(
                sl(VAR), sl(SUM), sl(NMR), sl(VAR), ALU.mult, ALU.add), inc=["dv"])
            dv_v = P.n("dv")
            P.op("scalar", lambda e: e.activation(
                sl(RSTD), sl(VAR), AF.Ln, bias=g["eps_ln"][:]),
                wait=[("dv", dv_v)], inc=["ac"])
            P.op("scalar", lambda e: e.activation(
                sl(RSTD), sl(RSTD), AF.Exp, scale=-0.5), inc=["ac"])
            P.op("vector", lambda e: e.tensor_scalar(
                sl(NMR), sl(MEAN), -1.0, None, ALU.mult), inc=["dv"])
            return (P.n("ac"), P.n("dv"))

        def transform(ch, ct):
            pp = ch % 2
            base = pp * 6
            sl = lambda i: st8x[:, base + i: base + i + 1]
            xt_c = xt[:, pp * D:(pp + 1) * D]
            xnb_c = xnb2[:, pp * D:(pp + 1) * D]
            ac_v, dv_v = stat_done[ch]
            P.op("vector", lambda e, xt_c=xt_c: e.tensor_scalar(
                xnb[:], xt_c, sl(NMR), sl(RSTD), ALU.add, ALU.mult),
                wait=[("ac", ac_v), ("dv", dv_v), "pe"], inc=["dv"])
            xn_dv = P.n("dv")
            cp_dv = {0: xn_dv, 1: xn_dv}
            for f in range(NF):
                pt = ptA if f % 2 == 0 else ptB
                P.op("tensor", lambda e, f=f, pt=pt: e.transpose(
                    pt[:], xnb[:, f * 128:(f + 1) * 128], g["ident"][:]),
                    wait=[("dv", cp_dv[f % 2])], inc=["pe"])
                P.op("vector", lambda e, f=f, ct=ct, pt=pt: e.tensor_copy(
                    xnT[:, f * 512 + ct * 128: f * 512 + (ct + 1) * 128], pt[:]),
                    wait=["pe", "qp"], inc=["dv"])
                cp_dv[f % 2] = P.n("dv")
            trans_free[ch] = P.n("pe")
            ln_free[ch] = (P.n("ac"), P.n("dv"))

        stat_done[0] = stats(0)
        for gg in range(NG):
            for ct in range(4):
                ch = gg * 4 + ct
                if ch + 1 < NCH:
                    stat_done[ch + 1] = stats(ch + 1)
                transform(ch, ct)
            t0 = gg * 512
            P.dma("qp", lambda e, b=b, t0=t0: e.dma_start(
                d["xn1T_d"][b].rearrange("(f p) t -> p f t", p=128)[:, :, t0:t0 + 512],
                xnT[:].rearrange("p (f t) -> p f t", f=NF)), war=["dv"])
            for wsb, dstT in ((wQ, qT), (wK, kT)):
                for fo in range(NF):
                    pqx = pq if fo % 2 == 0 else pq2
                    fa = pq_free.get(id(pqx), ["dv"])
                    for fi in range(NF):
                        P.op("tensor", lambda e, fo=fo, fi=fi, wsb=wsb, pqx=pqx: e.matmul(
                            pqx[:], wsb[:, fi * D + fo * 128: fi * D + (fo + 1) * 128],
                            xnT[:, fi * 512:(fi + 1) * 512],
                            start=(fi == 0), stop=(fi == NF - 1)),
                            wait=(fa if fi == 0 else []),
                            inc=(["pe"] if fi == NF - 1 else []))
                    P.op("vector", lambda e, fo=fo, t0=t0, dstT=dstT, pqx=pqx: e.tensor_copy(
                        dstT[:, fo * S + t0: fo * S + t0 + 512], pqx[:]),
                        wait=["pe"], inc=["dv"])
                    pq_free[id(pqx)] = [("dv", P.n("dv"))]
            for ct in range(4):
                ch = gg * 4 + ct
                for oc in range(2):
                    pvx = pq if oc == 0 else pq2
                    fa = pq_free.get(id(pvx), ["dv"])
                    for fi in range(NF):
                        P.op("tensor", lambda e, ct=ct, oc=oc, fi=fi, pvx=pvx: e.matmul(
                            pvx[:], xnT[:, fi * 512 + ct * 128: fi * 512 + (ct + 1) * 128],
                            wV[:, fi * D + oc * 512: fi * D + (oc + 1) * 512],
                            start=(fi == 0), stop=(fi == NF - 1)),
                            wait=(fa if fi == 0 else []),
                            inc=(["pe"] if fi == NF - 1 else []))
                    vdst = vx[:, ch * H * 65 + oc * 8 * 65: ch * H * 65 + (oc + 1) * 8 * 65]
                    vdst = vdst.rearrange("p (a j) -> p a j", j=65)[:, :, 0:64]
                    P.op("vector", lambda e, vdst=vdst, pvx=pvx: e.tensor_copy(
                        vdst, pvx[:].rearrange("p (a j) -> p a j", j=64)),
                        wait=["pe"], inc=["dv"])
                    pq_free[id(pvx)] = [("dv", P.n("dv"))]
        P.op("vector", lambda e: e.memset(
            vx[:].rearrange("p (c j) -> p c j", j=65)[:, :, 64:65], 1.0), inc=["dv"])
        with nc.Block() as blk:
            P.emit_into_block(blk, None, self.sems)
        wstk.close()

        fstk = ExitStack()
        o_s = self.sbuf(fstk, "d_o", [128, NCH * D], BF16)
        psb = self.sbuf(fstk, "d_psb", [128, 2 * 2048], BF16)
        dsc = self.sbuf(fstk, "d_dsc", [128, 2 * H], F32)
        wO = self.sbuf(fstk, "d_wO", [128, NF * D], BF16)
        stg2 = self.sbuf(fstk, "d_stg2", [128, 2 * D], F32)
        oT = self.sbuf(fstk, "d_oT", [128, D], BF16)
        xt2 = self.sbuf(fstk, "d_xt2", [128, 2 * D], F32)
        pout = self.sbuf(fstk, "d_pout", [128, 2 * D], F32)
        p_st = self.psum(fstk, "d_pst", [128, 2048], F32)
        p_num = self.psum(fstk, "d_pnum", [128, D], F32)
        p_den = self.psum(fstk, "d_pden", [128, H], F32)
        p_tr = self.psum(fstk, "d_ptr", [128, 128], BF16)

        P = Prog(self.semc)
        for fi in range(NF):
            pp = fi % 2
            P.dma("qs", lambda e, fi=fi, pp=pp: e.dma_start(
                stg2[:, pp * D:(pp + 1) * D], d["wO"][fi * 128:(fi + 1) * 128, :]),
                war=["dv"])
            P.op("vector", lambda e, fi=fi, pp=pp: e.tensor_copy(
                wO[:, fi * D:(fi + 1) * D], stg2[:, pp * D:(pp + 1) * D]),
                wait=["qs"], inc=["dv"])
        for qt in range(NCH):
            qs_ = qt * 128
            SLOT = [(h % 2) * 8 + (h // 2) for h in range(H)]
            exp_done = {}

            def emit_st(kt):
                ks = kt * 128
                pb = (kt % 2) * 2048
                for half in range(2):
                    for sl in range(half * 8, half * 8 + 8):
                        h = (sl // 8) + 2 * (sl % 8)
                        f, po = h // 2, 64 * (h % 2)
                        w0 = ([exp_done[(kt - 1, half)]] if (kt > 0 and sl % 8 == 0)
                              else (["ac", "dv"] if (kt == 0 and sl == 0) else []))
                        P.op("tensor", lambda e, sl=sl, f=f, po=po, ks=ks, qs_=qs_: e.matmul(
                            p_st[:, sl * 128:(sl + 1) * 128],
                            kT[po:po + 64, f * S + ks: f * S + ks + 128],
                            qT[po:po + 64, f * S + qs_: f * S + qs_ + 128],
                            start=(sl % 4 == 0), stop=(sl % 4 == 3)),
                            wait=w0, inc=(["pe"] if sl % 8 == 7 else []))
                    pe_now = P.n("pe")
                    P.op("scalar", lambda e, half=half, pb=pb: e.activation(
                        psb[:, pb + half * 1024: pb + (half + 1) * 1024],
                        p_st[:, half * 1024:(half + 1) * 1024], AF.Exp, scale=0.125),
                        wait=[("pe", pe_now)], inc=["ac"])
                    if kt == qt:
                        P.op("vector", lambda e, half=half, pb=pb: e.tensor_tensor(
                            psb[:, pb + half * 1024: pb + (half + 1) * 1024],
                            psb[:, pb + half * 1024: pb + (half + 1) * 1024],
                            g["tri16"][:, half * 1024:(half + 1) * 1024], ALU.mult),
                            wait=["ac"], inc=["dv"])
                        exp_done[(kt, half)] = ("dv", P.n("dv"))
                    else:
                        exp_done[(kt, half)] = ("ac", P.n("ac"))

            def emit_pv(kt):
                pb = (kt % 2) * 2048
                for half in range(2):
                    heads = [(sl // 8) + 2 * (sl % 8) for sl in range(half * 8, half * 8 + 8)]
                    for i, h in enumerate(heads):
                        asl = pb + SLOT[h] * 128
                        vsl = vx[:, kt * H * 65 + h * 65: kt * H * 65 + h * 65 + 64]
                        w1_ = [exp_done[(kt, half)]] if i == 0 else []
                        if kt == 0 and i == 0:
                            w1_ = w1_ + [("dv", dv_qt0), ("ac", ac_qt0)]
                        P.op("tensor", lambda e, h=h, vsl=vsl, asl=asl, kt=kt, qt=qt: e.matmul(
                            p_num[:, h * 64:(h + 1) * 64], psb[:, asl: asl + 128], vsl,
                            start=(kt == 0 and h % 8 == 0), stop=(kt == qt and h % 8 == 7)),
                            wait=w1_)
                        P.op("tensor", lambda e, h=h, asl=asl, kt=kt, qt=qt: e.matmul(
                            p_den[:, h:h + 1], psb[:, asl: asl + 128], g["ones_c"][:],
                            start=(kt == 0 and h == 0), stop=(kt == qt and h == H - 1)),
                            inc=(["pe"] if (half == 1 and i == 7) else []))

            dv_qt0 = P.n("dv")
            ac_qt0 = P.n("ac")
            for kt in range(qt + 2):
                if kt <= qt:
                    emit_st(kt)
                if kt > 0:
                    emit_pv(kt - 1)
            P.op("vector", lambda e: e.reciprocal(dsc[:, H:2 * H], p_den[:]),
                 wait=["pe"], inc=["dv"])
            for h in range(H):
                P.op("vector", lambda e, h=h, qt=qt: e.tensor_scalar(
                    o_s[:, qt * D + h * 64: qt * D + (h + 1) * 64],
                    p_num[:, h * 64:(h + 1) * 64], dsc[:, H + h:H + h + 1], None,
                    ALU.mult), inc=["dv"])
        for ch in range(NCH):
            pp = ch % 2
            P.dma("qs", lambda e, b=b, ch=ch, pp=pp: e.dma_start(
                xt2[:, pp * D:(pp + 1) * D], d["x1_d"][b, ch * 128:(ch + 1) * 128, :]),
                war=["dv"])
            qs_x = P.n("qs")
            for f in range(NF):
                P.op("tensor", lambda e, f=f, ch=ch: e.transpose(
                    p_tr[:], o_s[:, ch * D + f * 128: ch * D + (f + 1) * 128], g["ident"][:]),
                    wait=["dv", "ac"], inc=["pe"])
                P.op("vector", lambda e, f=f: e.tensor_copy(
                    oT[:, f * 128:(f + 1) * 128], p_tr[:]), wait=["pe"], inc=["dv"])
            for dc in range(2):
                for fe in range(NF):
                    P.op("tensor", lambda e, dc=dc, fe=fe: e.matmul(
                        p_num[:, dc * 512:(dc + 1) * 512], oT[:, fe * 128:(fe + 1) * 128],
                        wO[:, fe * D + dc * 512: fe * D + (dc + 1) * 512],
                        start=(fe == 0), stop=(fe == NF - 1)),
                        wait=(["dv", "ac"] if fe == 0 else []),
                        inc=(["pe"] if fe == NF - 1 else []))
            P.op("vector", lambda e, pp=pp: e.scalar_tensor_tensor(
                pout[:, pp * D:(pp + 1) * D], p_num[:], 1.0, xt2[:, pp * D:(pp + 1) * D],
                ALU.mult, ALU.add), wait=["pe", ("qs", qs_x), "qp"], inc=["dv"])
            P.op("tensor", lambda e: e.nop(), wait=["dv"])
            P.dma("qp", lambda e, b=b, ch=ch, pp=pp: e.dma_start(
                d["partial1_d"][b, ch * 128:(ch + 1) * 128, :], pout[:, pp * D:(pp + 1) * D]),
                war=["dv"])
        P.op("gpsimd", lambda e: e.nop(), wait=["qp"])
        with nc.Block() as blk:
            P.emit_into_block(blk, None, self.sems)
        fstk.close()
        sstk.close()

    def phase_E(self):
        nc, d, g = self.nc, self.d, self.g
        S, NCH = self.S, self.NCH
        stk = ExitStack()
        wup = self.sbuf(stk, "e_wup", [128, NF * DH], BF16)
        wdn = self.sbuf(stk, "e_wdn", [128, NHU * D], BF16)
        bup = self.sbuf(stk, "e_bup", [128, NHU], F32)
        P = Prog(self.semc)
        P.dma("qs", lambda e: e.dma_start(
            wup[:].rearrange("p (f h) -> p f h", f=NF),
            d["wup_d"].rearrange("(f p) h -> p f h", p=128)), war=["pe", "dv"])
        P.dma("qs", lambda e: e.dma_start(
            wdn[:].rearrange("p (u x) -> p u x", u=NHU),
            d["wdown_d"].rearrange("(u p) x -> p u x", p=128)), war=["dv"])
        P.dma("qs", lambda e: e.dma_start(bup[:], d["bup_d"][:]), war=["dv"])
        with nc.Block() as blk:
            P.emit_into_block(blk, None, self.sems)
        for b in range(self.n_seq):
            self._e_seq(b, wup, wdn, bup)
        stk.close()

    def _e_seq(self, b, wup, wdn, bup):
        nc, d, g = self.nc, self.d, self.g
        S, NCH = self.S, self.NCH
        NTG = S // 256
        stk = ExitStack()
        xnT = self.sbuf(stk, "e_xnT", [128, 2 * NF * 256], BF16)
        gsb = self.sbuf(stk, "e_gsb", [128, 2 * NHU * 256], BF16)
        xpt = self.sbuf(stk, "e_xpt", [128, 2 * D], F32)
        yt = self.sbuf(stk, "e_yt", [128, 2 * D], F32)
        p_h = [self.psum(stk, "e_ph", [128, 256], F32) for _ in range(2)]
        p_y0 = self.psum(stk, "e_py0", [128, D], F32)
        p_y1 = self.psum(stk, "e_py1", [128, D], F32)

        P = Prog(self.semc)
        xnT_v = xnT[:].rearrange("p (u f t) -> p u f t", u=2, f=NF)
        gsb_v = gsb[:].rearrange("p (u h t) -> p u h t", u=2, h=NHU)
        xnT_ready = {}
        up_done = {}
        gelu_ac = {}
        epi_dv = {0: 0}
        y_stored = {0: 0}

        def loads(tg):
            P.dma("qs", lambda e, b=b, tg=tg: e.dma_start(
                xnT_v[:, tg % 2], d["xn1T_d"][b].rearrange(
                    "(f p) t -> p f t", p=128)[:, :, tg * 256:(tg + 1) * 256]),
                war=[("pe", up_done.get(tg - 2, 0))])
            xnT_ready[tg] = P.n("qs")

        def up_gelu(tg):
            for hu in range(NHU):
                ph = p_h[hu % 2]
                for fd in range(NF):
                    P.op("tensor", lambda e, hu=hu, fd=fd, tg=tg, ph=ph: e.matmul(
                        ph[:], wup[:, fd * DH + hu * 128: fd * DH + (hu + 1) * 128],
                        xnT_v[:, tg % 2, fd, :],
                        start=(fd == 0), stop=(fd == NF - 1)),
                        wait=((["ac", "dv", ("qs", xnT_ready[tg])]) if (fd == 0 and hu <= 1)
                              else (["ac"] if fd == 0 else [])),
                        inc=(["pe"] if fd == NF - 1 else []))
                pe_now = P.n("pe")
                P.op("scalar", lambda e, hu=hu, tg=tg, ph=ph: e.activation(
                    gsb_v[:, tg % 2, hu, :], ph[:], AF.Gelu,
                    bias=bup[:, hu:hu + 1]), wait=[("pe", pe_now)], inc=["ac"])
            up_done[tg] = P.n("pe")
            gelu_ac[tg] = P.n("ac")

        def down_epi(tg):
            for sub, py in ((0, p_y0), (1, p_y1)):
                for dc in range(2):
                    for hu in range(NHU):
                        P.op("tensor", lambda e, sub=sub, dc=dc, hu=hu, py=py, tg=tg: e.matmul(
                            py[:, dc * 512:(dc + 1) * 512],
                            gsb_v[:, tg % 2, hu, sub * 128: sub * 128 + 128],
                            wdn[:, hu * D + dc * 512: hu * D + (dc + 1) * 512],
                            start=(hu == 0), stop=(hu == NHU - 1)),
                            wait=(([("ac", gelu_ac[tg]), ("dv", epi_dv[0])]) if hu == 0 else []),
                            inc=(["pe"] if hu == NHU - 1 else []))
            for sub, py in ((0, p_y0), (1, p_y1)):
                ch = tg * 2 + sub
                pp = ch % 2
                P.dma("qs", lambda e, b=b, ch=ch, pp=pp: e.dma_start(
                    xpt[:, pp * D:(pp + 1) * D], d["partial1_d"][b, ch * 128:(ch + 1) * 128, :]),
                    war=[("dv", epi_dv[0])])
                qs_x = P.n("qs")
                P.op("vector", lambda e, py=py, pp=pp: e.scalar_tensor_tensor(
                    yt[:, pp * D:(pp + 1) * D], py[:], 1.0, xpt[:, pp * D:(pp + 1) * D],
                    ALU.mult, ALU.add),
                    wait=["pe", ("qs", qs_x), ("qp", y_stored[0])], inc=["dv"])
                P.op("vector", lambda e, pp=pp: e.tensor_tensor(
                    yt[:, pp * D:(pp + 1) * D], yt[:, pp * D:(pp + 1) * D],
                    g["bcd_bc"][:], ALU.add), inc=["dv"])
                P.dma("qp", lambda e, b=b, ch=ch, pp=pp: e.dma_start(
                    d["y"][b, ch * 128:(ch + 1) * 128, :], yt[:, pp * D:(pp + 1) * D]),
                    war=["dv"])
                y_stored[0] = P.n("qp")
            epi_dv[0] = P.n("dv")

        for tg in range(NTG):
            loads(tg)
            up_gelu(tg)
            if tg > 0:
                down_epi(tg - 1)
        down_epi(NTG - 1)
        P.op("gpsimd", lambda e: e.nop(), wait=["qp"])
        with nc.Block() as blk:
            P.emit_into_block(blk, None, self.sems)
        stk.close()


from concourse.bass_utils import run_bass_kernel_spmd


def kernel(**inputs):
    in_maps = stage_host(inputs, n_cores=8)
    b = Builder(S=2048, n_seq=2)
    b.phase_W()
    b.phase_AB()
    b.phase_C()
    b.phase_D()
    b.phase_E()
    res = run_bass_kernel_spmd(b.nc, in_maps, core_ids=list(range(8)))
    out = np.concatenate(
        [np.asarray(r["y"], dtype=np.float32).reshape(2, 2048, D) for r in res.results],
        axis=0)
    return out
